# version 19
# baseline (speedup 1.0000x reference)
import numpy as np
import ml_dtypes
from contextlib import ExitStack
import concourse.bass as bass
import concourse.mybir as mybir
from concourse.bass_utils import run_bass_kernel_spmd
from concourse.bass import IndirectOffsetOnAxis

F32 = mybir.dt.float32
BF16 = mybir.dt.bfloat16
U32 = mybir.dt.uint32
U8 = mybir.dt.uint8
AF = mybir.ActivationFunctionType
ALU = mybir.AluOpType
AX = mybir.AxisListType
DS = bass.ds

T = 2048
D = 1024
DI = 2048
NH = 32
HD = 64
NST = 128
CH = 256
NCH = T // CH
CONVD = 2304
DIP = 4384
CAP = 1024
_T1DBG = 0



class Emit:

    ENGS = ("sync", "scalar", "vector", "tensor", "gpsimd")

    def __init__(self, nc):
        self.nc = nc
        self.q = {k: [] for k in self.ENGS}
        self.ctx = ExitStack()
        self._arenas = []
        self.ch = None
        self.n = 0
        self.gch = None
        self.gn = 0
        self._chain_id = 0
        self._names = {}

    def name(self, base):
        i = self._names.get(base, 0)
        self._names[base] = i + 1
        return f"{base}{i}"

    def _alloc_ctx(self):
        return self._arenas[-1] if self._arenas else self.ctx

    def sb(self, base, shape, dt=F32):
        return self._alloc_ctx().enter_context(self.nc.sbuf_tensor(self.name("sb_" + base), shape, dt))

    def ps(self, base, shape, dt=F32):
        return self._alloc_ctx().enter_context(self.nc.psum_tensor(self.name("ps_" + base), shape, dt))

    def sem(self, base):
        return self.ctx.enter_context(self.nc.semaphore(self.name("sem_" + base)))

    def arena(self):
        from contextlib import contextmanager

        @contextmanager
        def _a():
            st = ExitStack()
            self._arenas.append(st)
            try:
                with st:
                    yield
            finally:
                self._arenas.pop()
        return _a()

    def init_chain(self):
        self.ch = self.sem("ch")
        self.gch = self.sem("gch")
        self._consts = {}
        self._ctile = self.ctx.enter_context(self.nc.sbuf_tensor(self.name("consts"), [128, 32], F32))

    def new_chain(self):
        oc, on, og, ogn = self.ch, self.n, self.gch, self.gn
        for eng in self.ENGS:
            self.q[eng].append(lambda e, oc=oc, on=on, og=og, ogn=ogn: (
                e.wait_ge(oc, on), e.wait_ge(og, ogn)))
        self.ch = self.sem("ch")
        self.gch = self.sem("gch")
        self.n = 0
        self.gn = 0

    def cf(self, v):
        v = float(v)
        if v not in self._consts:
            i = len(self._consts)
            assert i < 32
            self._consts[v] = i
            self.step("vector", lambda e, i=i, v=v: e.memset(self._ctile[:, i:i + 1], v))
        return self._ctile[:, self._consts[v]:self._consts[v] + 1]

    def cf1(self, v):
        ap = self.cf(v)
        return self._ctile[0:1, self._consts[float(v)]:self._consts[float(v)] + 1]

    def step(self, eng, fn, dma=False, n_dma=1):
        need = self.n
        gneed = self.gn
        use_g = dma and eng == "gpsimd"
        if use_g:
            self.gn += 16 * n_dma
        elif dma:
            self.n += 16 * n_dma
        else:
            self.n += 1
        tch = self.gch if use_g else self.ch

        def wrapped(e, fn=fn, need=need, gneed=gneed, dma=dma, tch=tch):
            e.wait_ge(self.ch if tch is not self.ch else tch, need) if False else None
            e.wait_ge(self.ch, need)
            if self.gn or gneed:
                e.wait_ge(self.gch, gneed)
            r = fn(e)
            if dma:
                rs = r if isinstance(r, (list, tuple)) else [r]
                for x in rs:
                    x.then_inc(tch, 16)
            else:
                r.then_inc(tch, 1)
        ch_now, gch_now = self.ch, self.gch

        def wrapped2(e, fn=fn, need=need, gneed=gneed, dma=dma, use_g=use_g,
                     ch=ch_now, gch=gch_now):
            e.wait_ge(ch, need)
            e.wait_ge(gch, gneed)
            r = fn(e)
            tgt = gch if use_g else ch
            if dma:
                rs = r if isinstance(r, (list, tuple)) else [r]
                for x in rs:
                    x.then_inc(tgt, 16)
            else:
                r.then_inc(tgt, 1)
        self.q[eng].append(wrapped2)

    def mksem(self, base):
        class _S:
            __slots__ = ("h", "n")
        s = _S()
        s.h = self.sem(base)
        s.n = 0
        return s

    def stepd(self, eng, fn, deps=(), inc=None, dma=False, n_dma=1):
        deps = [(d[0].h if hasattr(d[0], "h") else d[0], d[1]) for d in deps]
        ih = inc.h if inc is not None else None

        def wrapped(e, fn=fn, deps=deps, ih=ih, dma=dma):
            for (h, v) in deps:
                if v > 0:
                    e.wait_ge(h, v)
            r = fn(e)
            if ih is not None:
                if dma:
                    rs = r if isinstance(r, (list, tuple)) else [r]
                    for x in rs:
                        x.then_inc(ih, 16)
                else:
                    r.then_inc(ih, 1)
        self.q[eng].append(wrapped)
        if inc is not None:
            inc.n += 16 * n_dma if dma else 1

    def join(self, sems):
        vals = [(s.h, s.n) for s in sems]
        for eng in self.ENGS:
            def w(e, vals=vals):
                for (h, v) in vals:
                    if v > 0:
                        e.wait_ge(h, v)
            self.q[eng].append(w)

    def step_manual(self, eng, fn, n_inc=1, g_inc=0):
        need, gneed = self.n, self.gn
        ch_now, gch_now = self.ch, self.gch
        self.n += n_inc
        self.gn += g_inc
        self.q[eng].append(lambda e, fn=fn, need=need, gneed=gneed, ch=ch_now, gch=gch_now:
                           fn(e, ch, need, gch, gneed))

    def raw(self, eng, fn):
        self.q[eng].append(fn)

    def barrier_val(self):
        return self.n

    def finale(self):
        fin, gfin = self.n, self.gn
        ch, gch = self.ch, self.gch
        self.raw("sync", lambda e: (e.wait_ge(ch, fin), e.wait_ge(gch, gfin)))

    def emit(self, block):
        for name, reg in (("sync", block.sync), ("scalar", block.scalar),
                          ("vector", block.vector), ("tensor", block.tensor),
                          ("gpsimd", block.gpsimd)):
            ops = self.q[name]

            def body(e, ops=ops):
                for f in ops:
                    f(e)
            reg(body)


def full(ap):
    return ap[tuple(slice(None) for _ in ap.shape)]



def load_cols(em, dst_sb, hbm_ap, n):
    em.step("sync", lambda e: e.dma_start(out=dst_sb[:, 0:n], in_=hbm_ap), dma=True)


def mm_ones_sums(em, psum_row, ones_col, rhs_blocks, start=True):
    def fn(e):
        r = None
        for i, blk in enumerate(rhs_blocks):
            r = e.matmul(psum_row[:, :], ones_col[:, 0:1], blk,
                         start=(start and i == 0), stop=(i == len(rhs_blocks) - 1))
        return r
    em.step("tensor", fn)


def bcast_row(em, psum_tile, ones_row, row_ap):
    em.step("tensor", lambda e: e.matmul(psum_tile, ones_row[0:1, :], row_ap,
                                         start=True, stop=True))


def rstd_from_var(em, rstd_row, var_row, eps, p1=True):
    b = em.cf1(eps) if p1 else em.cf(eps)
    s = em.cf1(-0.5) if p1 else em.cf(-0.5)
    em.step("scalar", lambda e: e.activation(out=rstd_row, in_=var_row, func=AF.Ln, bias=b))
    em.step("scalar", lambda e: e.activation(out=rstd_row, in_=rstd_row, func=AF.Exp, scale=s))



def emit_s1(em, io, tag_l1=False):
    nc = em.nc
    with em.arena():
        x_fm = io["x_fm"]
        xn_hbm = io["xn_bf"]
        masks_hbm = io["masks"]
        scores_hbm = io["scores"]
        identf = io["identf"]

        TT = 512
        ntile = T // TT
        ntb = TT // 128

        x_sb = em.sb("x", [128, 8 * TT])
        sq_sb = em.sb("sq", [128, 8 * TT])
        xnf = em.sb("xnf", [128, 8 * TT])
        xnb = em.sb("xnb", [128, 8 * TT], BF16)
        ones_col = em.sb("onesc", [128, 1])
        ones_row = em.sb("onesr", [1, 128])
        sid = em.sb("sid", [128, 128])
        rows = em.sb("rows", [1, 6 * TT])
        masks_sb = em.sb("masks", [8, T])
        scores_tm = em.sb("scorestm", [128, T // 128])
        scores_fm = em.sb("scoresfm", [T // 128, 128])
        logit_sb = em.sb("logit", [128, 8])
        lstat = em.sb("lstat", [128, 4])

        p_sum = em.ps("psum", [1, TT])
        p_sq = em.ps("psq", [1, TT])
        p_mu = em.ps("pmu", [128, TT])
        p_rstd = em.ps("prstd", [128, TT])
        p_xr = em.ps("pxr", [16, TT])
        p_log = em.ps("plog", [128, 8])
        p_tr = em.ps("ptr", [128, 128])
        xr_sb = em.sb("xr", [16, 512])
        if "xn_tm" in io:
            tm_sb = em.sb("tmsb", [128, 1024], BF16)
            sidb1 = em.sb("sidb1", [128, 128], BF16)
            p_trb1 = em.ps("ptrb1", [128, 8 * 128], BF16)

        if tag_l1:
            gateT_hbm = io["gateT"]
            gss = em.sb("gateT", [128, 8 * 8])
        else:
            wgredT_hbm = io["wg_redT"]
            wg_hbm = io["wg"]
            gss = em.sb("wgredT", [128, 8 * 16])
            wg_sb = em.sb("wg", [8, 16])
            wgu_sb = em.sb("wgu", [8, 16])
            wguT_sb = em.sb("wguT", [16, 8])


        em.step("vector", lambda e: e.memset(ones_col[:, :], 1.0))
        em.step("vector", lambda e: e.memset(ones_row[:, :], 1.0))
        em.step("sync", lambda e: e.dma_start(out=sid[:, :], in_=identf), dma=True)
        if "xn_tm" in io:
            em.step("sync", lambda e: e.dma_start(out=sidb1[:, :], in_=io["identb"]), dma=True)

        gw = 8 if tag_l1 else 16
        for k in range(8):
            src = (gateT_hbm if tag_l1 else wgredT_hbm)
            em.step("sync", lambda e, k=k, src=src: e.dma_start(
                out=gss[:, gw * k:gw * (k + 1)], in_=src[128 * k:128 * (k + 1), :]), dma=True)
        if not tag_l1:
            em.step("sync", lambda e: e.dma_start(out=wg_sb[:, :], in_=wg_hbm), dma=True)
            em.step("scalar", lambda e: e.activation(out=wgu_sb[:, :], in_=wg_sb[:, :], func=AF.Square))
            em.step("vector", lambda e: e.reduce_sum(lstat[0:8, 0:1], wgu_sb[0:8, :], axis=AX.X))
            em.step("scalar", lambda e: e.activation(out=lstat[0:8, 1:2], in_=lstat[0:8, 0:1], func=AF.Sqrt))
            em.step("vector", lambda e: e.tensor_scalar_max(lstat[0:8, 1:2], lstat[0:8, 1:2], 1e-4))
            em.step("vector", lambda e: e.reciprocal(lstat[0:8, 2:3], lstat[0:8, 1:2]))
            em.step("vector", lambda e: e.tensor_scalar_mul(wgu_sb[0:8, :], wg_sb[0:8, :], lstat[0:8, 2:3]))
            em.step("tensor", lambda e: e.transpose(p_tr[0:16, 0:8], wgu_sb[0:8, 0:16], sid[0:8, 0:8]))
            em.step("vector", lambda e: e.tensor_copy(wguT_sb[:, :], p_tr[0:16, 0:8]))

        eps = 1e-5
        resid = "s_bf" in io
        if resid:
            s_hbm, g_hbm, x1_hbm = io["s_bf"], io["g_bf"], io["x1_fm"]
            sres_sb = em.sb("sres", [128, 8 * TT], BF16)
            gres_sb = em.sb("gres", [128, 8 * TT], BF16)
        for it in range(ntile):
            t0 = it * TT
            def ldx(e, t0=t0):
                rs = []
                for k in range(8):
                    rs.append(e.dma_start(out=x_sb[:, TT * k:TT * (k + 1)],
                                          in_=x_fm[128 * k:128 * (k + 1), t0:t0 + TT]))
                return rs
            em.step("sync", ldx, dma=True, n_dma=8)
            if resid:
                def ldsg(e, t0=t0):
                    rs = []
                    for k in range(8):
                        rs.append(e.dma_start(out=sres_sb[:, TT * k:TT * (k + 1)],
                                              in_=s_hbm[k, :, t0:t0 + TT]))
                        rs.append(e.dma_start(out=gres_sb[:, TT * k:TT * (k + 1)],
                                              in_=g_hbm[k, :, t0:t0 + TT]))
                    return rs
                em.step("sync", ldsg, dma=True, n_dma=16)
                def mkres(e):
                    r = None
                    for k in range(8):
                        e.tensor_mul(sq_sb[:, TT * k:TT * (k + 1)], sres_sb[:, TT * k:TT * (k + 1)],
                                     gres_sb[:, TT * k:TT * (k + 1)])
                        r = e.tensor_add(x_sb[:, TT * k:TT * (k + 1)], x_sb[:, TT * k:TT * (k + 1)],
                                         sq_sb[:, TT * k:TT * (k + 1)])
                    return r
                em.step("vector", mkres)
                def stx1(e, t0=t0):
                    return [e.dma_start(out=x1_hbm[128 * k:128 * (k + 1), t0:t0 + TT],
                                        in_=x_sb[:, TT * k:TT * (k + 1)]) for k in range(8)]
                em.step("sync", stx1, dma=True, n_dma=8)
            em.step("scalar", lambda e: e.activation(out=sq_sb[:, :], in_=x_sb[:, :], func=AF.Square))
            mm_ones_sums(em, p_sum, ones_col, [x_sb[:, TT * k:TT * (k + 1)] for k in range(8)])
            mm_ones_sums(em, p_sq, ones_col, [sq_sb[:, TT * k:TT * (k + 1)] for k in range(8)])
            mu = rows[0:1, 0 * TT:1 * TT]
            var = rows[0:1, 1 * TT:2 * TT]
            rstd = rows[0:1, 2 * TT:3 * TT]
            scr = rows[0:1, 3 * TT:4 * TT]
            em.step("vector", lambda e: e.tensor_scalar_mul(mu, p_sum[:, :], 1.0 / D))
            em.step("vector", lambda e: e.tensor_scalar_mul(var, p_sq[:, :], 1.0 / D))
            em.step("scalar", lambda e: e.activation(out=scr, in_=mu, func=AF.Square))
            em.step("vector", lambda e: e.tensor_sub(var, var, scr))
            rstd_from_var(em, rstd, var, eps)
            bcast_row(em, p_mu[:, :], ones_row, mu)
            bcast_row(em, p_rstd[:, :], ones_row, rstd)
            for k in range(8):
                em.step("vector", lambda e, k=k: e.tensor_sub(
                    xnf[:, TT * k:TT * (k + 1)], x_sb[:, TT * k:TT * (k + 1)], p_mu[:, :]))
                em.step("vector", lambda e, k=k: e.tensor_mul(
                    xnf[:, TT * k:TT * (k + 1)], xnf[:, TT * k:TT * (k + 1)], p_rstd[:, :]))
            em.step("vector", lambda e: e.tensor_copy(xnb[:, :], xnf[:, :]))
            def stx(e, t0=t0):
                rs = []
                for k in range(8):
                    rs.append(e.dma_start(out=xn_hbm[k, :, t0:t0 + TT],
                                          in_=xnb[:, TT * k:TT * (k + 1)]))
                return rs
            em.step("sync", stx, dma=True, n_dma=8)

            if "xn_tm" in io:
                for tb4 in range(4):
                    tloc = tb4 * 128
                    def trtm(e, tloc=tloc):
                        r = None
                        for kf in range(8):
                            r = e.transpose(p_trb1[:, 128 * kf:128 * (kf + 1)],
                                            xnb[:, TT * kf + tloc:TT * kf + tloc + 128], sidb1[:, :])
                        return r
                    em.step("tensor", trtm)
                    def evtm(e):
                        r = None
                        for kf in range(8):
                            r = e.tensor_copy(tm_sb[:, 128 * kf:128 * (kf + 1)],
                                              p_trb1[:, 128 * kf:128 * (kf + 1)])
                        return r
                    em.step("vector", evtm)
                    em.step("sync", lambda e, t0=t0, tloc=tloc: e.dma_start(
                        out=io["xn_tm"][t0 + tloc:t0 + tloc + 128, :], in_=tm_sb[:, :]), dma=True)

            def mmg(e):
                r = None
                for k in range(8):
                    r = e.matmul(p_xr[0:gw, :], gss[:, gw * k:gw * (k + 1)],
                                 xnf[:, TT * k:TT * (k + 1)],
                                 start=(k == 0), stop=(k == 7))
                return r
            em.step("tensor", mmg)
            em.step("vector", lambda e: e.tensor_copy(xr_sb[0:gw, :], p_xr[0:gw, :]))
            for tb in range(ntb):
                gt = it * ntb + tb
                if tag_l1:
                    em.step("tensor", lambda e, tb=tb: e.transpose(
                        p_tr[:, 0:16], xr_sb[0:16, 128 * tb:128 * (tb + 1)], sid[0:16, 0:16]))
                    em.step("vector", lambda e: e.tensor_copy(logit_sb[:, 0:8], p_tr[:, 0:8]))
                else:
                    em.step("tensor", lambda e, tb=tb: e.matmul(
                        p_log[:, :], xr_sb[0:16, 128 * tb:128 * (tb + 1)], wguT_sb[:, :],
                        start=True, stop=True))
                    em.step("vector", lambda e: e.tensor_copy(logit_sb[:, :], p_log[:, :]))
                em.step("vector", lambda e: e.reduce_max(lstat[:, 0:1], logit_sb[:, 0:8], axis=AX.X))
                em.step("vector", lambda e: e.tensor_scalar(
                    logit_sb[:, 0:8], logit_sb[:, 0:8], lstat[:, 0:1], None, ALU.subtract))
                em.step("vector", lambda e: e.tensor_scalar(
                    sq_sb[:, 0:8], logit_sb[:, 0:8], 0.0, None, ALU.is_equal))
                em.step("scalar", lambda e: e.activation(out=logit_sb[:, 0:8], in_=logit_sb[:, 0:8], func=AF.Exp))
                em.step("vector", lambda e: e.reduce_sum(lstat[:, 1:2], logit_sb[:, 0:8], axis=AX.X))
                if tag_l1:
                    em.step("vector", lambda e: e.tensor_scalar(
                        lstat[:, 2:3], lstat[:, 1:2], 1e-8, 1.0, ALU.mult, ALU.add))
                    em.step("vector", lambda e, gt=gt: e.reciprocal(scores_tm[:, gt:gt + 1], lstat[:, 2:3]))
                else:
                    em.step("vector", lambda e, gt=gt: e.reciprocal(scores_tm[:, gt:gt + 1], lstat[:, 1:2]))
                em.step("tensor", lambda e: e.transpose(p_tr[:, :], sq_sb[:, 0:128], sid[:, :]))
                em.step("vector", lambda e, gt=gt: e.tensor_copy(
                    masks_sb[0:8, 128 * gt:128 * (gt + 1)], p_tr[0:8, 0:128]))

        em.step("tensor", lambda e: e.transpose(p_tr[0:T // 128, :], scores_tm[:, 0:T // 128], sid[:, :]))
        em.step("vector", lambda e: e.tensor_copy(scores_fm[:, :], p_tr[0:T // 128, 0:128]))
        em.step("sync", lambda e: e.dma_start(out=masks_hbm, in_=masks_sb[0:8, :]), dma=True)
        em.step("sync", lambda e: e.dma_start(out=scores_hbm, in_=scores_fm[0:T // 128, :]), dma=True)

        if "idxf" in io:
            capoff = io["capoff"]
            cs_sb = em.sb("cs", [8, T])
            t1_sb = em.sb("t1x", [8, T])
            ov_sb = em.sb("ovx", [8, T])
            idxrow = em.sb("idxrow", [1, T])
            capc = em.sb("capc", [8, 1])
            ones8 = em.sb("ones8", [8, 1])
            onesT8 = em.sb("onesT8", [8, T])
            p_ix = p_sum
            em.step("sync", lambda e: e.dma_start(out=capc[:, :], in_=capoff), dma=True)
            em.step("vector", lambda e: e.memset(ones8[:, :], 1.0))
            em.step("vector", lambda e: e.memset(onesT8[:, :], 1.0))
            em.step("vector", lambda e: e.tensor_tensor_scan(
                cs_sb[:, :], onesT8[:, :], masks_sb[0:8, :], 0.0, ALU.mult, ALU.add))
            def mkidx(e):
                e.tensor_scalar_add(t1_sb[:, :], cs_sb[:, :], capc[:, 0:1])
                e.tensor_scalar(ov_sb[:, :], cs_sb[:, :], CAP + 0.5, None, ALU.is_ge)
                e.scalar_tensor_tensor(t1_sb[:, :], ov_sb[:, :], 1e9, t1_sb[:, :], ALU.mult, ALU.add)
                return e.tensor_mul(t1_sb[:, :], t1_sb[:, :], masks_sb[0:8, :])
            em.step("vector", mkidx)
            for tt in range(T // 512):
                em.step("tensor", lambda e, tt=tt: e.matmul(
                    p_ix[:, :], ones8[0:8, 0:1], t1_sb[:, 512 * tt:512 * (tt + 1)], start=True, stop=True))
                em.step("vector", lambda e, tt=tt: e.tensor_copy(idxrow[0:1, 512 * tt:512 * (tt + 1)], p_ix[:, :]))
            em.step("sync", lambda e: e.dma_start(out=io["idxf"], in_=idxrow[0:1, :]), dma=True)



        if "xh_fm" in io:
            xh = io["xh_fm"]
            xnh_hbm = io["xnh_bf"]
            hx = em.sb("hx", [128, 8 * 3])
            hsq = em.sb("hsq", [128, 8 * 3])
            hb = em.sb("hb", [128, 8 * 3], BF16)
            p_hs = p_sum
            p_hq = p_sq
            p_hmu = p_mu
            p_hrstd = p_rstd
            def ldh(e):
                rs = []
                for k in range(8):
                    rs.append(e.dma_start(out=hx[:, 3 * k:3 * (k + 1)],
                                          in_=xh[128 * k:128 * (k + 1), :]))
                return rs
            em.step("sync", ldh, dma=True, n_dma=8)
            em.step("scalar", lambda e: e.activation(out=hsq[:, :], in_=hx[:, :], func=AF.Square))
            mm_ones_sums(em, p_hs[:, 0:3], ones_col, [hx[:, 3 * k:3 * (k + 1)] for k in range(8)])
            mm_ones_sums(em, p_hq[:, 0:3], ones_col, [hsq[:, 3 * k:3 * (k + 1)] for k in range(8)])
            hmu = rows[0:1, 0:3]
            hvar = rows[0:1, 8:11]
            hrstd = rows[0:1, 16:19]
            hscr = rows[0:1, 24:27]
            em.step("vector", lambda e: e.tensor_scalar_mul(hmu, p_hs[:, 0:3], 1.0 / D))
            em.step("vector", lambda e: e.tensor_scalar_mul(hvar, p_hq[:, 0:3], 1.0 / D))
            em.step("scalar", lambda e: e.activation(out=hscr, in_=hmu, func=AF.Square))
            em.step("vector", lambda e: e.tensor_sub(hvar, hvar, hscr))
            rstd_from_var(em, hrstd, hvar, eps)
            bcast_row(em, p_hmu[:, 0:3], ones_row, hmu)
            bcast_row(em, p_hrstd[:, 0:3], ones_row, hrstd)
            for k in range(8):
                em.step("vector", lambda e, k=k: e.tensor_sub(
                    hx[:, 3 * k:3 * (k + 1)], hx[:, 3 * k:3 * (k + 1)], p_hmu[:, 0:3]))
                em.step("vector", lambda e, k=k: e.tensor_mul(
                    hx[:, 3 * k:3 * (k + 1)], hx[:, 3 * k:3 * (k + 1)], p_hrstd[:, 0:3]))
            em.step("vector", lambda e: e.tensor_copy(hb[:, :], hx[:, :]))
            def sth(e):
                rs = []
                for k in range(8):
                    rs.append(e.dma_start(out=xnh_hbm[k, :, :], in_=hb[:, 3 * k:3 * (k + 1)]))
                return rs
            em.step("sync", sth, dma=True, n_dma=8)





S2_STOP = 99

def emit_s2(em, io):
    nc = em.nc
    with em.arena():
        xn = io["xn_bf"]
        xnh = io["xnh_bf"]
        W_in = io["W_in"]
        convw = io["convw"]
        convb = io["convb"]
        dtbias = io["dtbias"]
        alog = io["alog"]
        identb = io["identb"]
        identf = io["identf"]
        o_zs = io["zs"]
        o_xfm = io["xfm"]
        o_xt = io["xt_tm"]
        o_baseT = io["baseT"]
        o_cfm = io["C_fm"]
        o_dacs = io["dAcs"]
        o_dacstm = io["dAcs_tm"]
        o_alast = io["alast"]
        o_states = io["states"]
        o_sout = io["S_out"]

        G = 512
        NG = T // G
        NB_Z, NB_X = 16, 18
        MB = 4

        w_sb = em.sb("win", [128, 8 * DIP], BF16)
        xn_g = em.sb("xng", [128, 8 * G], BF16)
        xbc_g = em.sb("xbcg", [128, NB_X * (G + 3)])
        zstage = em.sb("zstage", [128, MB * G], BF16)
        conv_bf = em.sb("convbf", [128, 6 * G], BF16)
        xtm_g = em.sb("xtmg", [128, 4 * DI], BF16)
        xtd_g = em.sb("xtdg", [128, 4 * DI], BF16)
        btm_g = em.sb("btmg", [128, 4 * 128], BF16)
        bfm_g = em.sb("bfmg", [128, G], BF16)
        cfm_g = em.sb("cfmg", [128, G], BF16)
        dt_g = em.sb("dtg", [32, G])
        da_g = em.sb("dag", [32, G])
        dacs_g = em.sb("dacsg", [32, G])
        dttm_g = em.sb("dttmg", [128, 4 * 32])
        dacstm_g = em.sb("dacstmg", [128, 4 * 32])
        w_g = em.sb("wg2", [128, 4 * 32])
        decay_g = em.sb("decayg", [128, 4 * 32])
        cdbc = em.sb("cdbc", [128, 32])
        alast_sb = em.sb("alast", [32, NCH])
        prev = em.sb("prev", [128, DI])
        states_sb = em.sb("statessb", [128, DI])
        base_sb = em.sb("basesb", [128, 2 * 256], BF16)
        sidb = em.sb("sidb", [128, 128], BF16)
        sidf = em.sb("sidf", [128, 128])
        onesc = em.sb("onesc2", [32, 256])
        onesr = em.sb("onesr2", [1, 128])
        cw_sb = em.sb("cw", [128, NB_X * 4])
        cb_sb = em.sb("cb", [128, NB_X])
        dtb_sb = em.sb("dtb", [32, 1])
        nega_sb = em.sb("nega", [32, 1])
        tmp_sb = em.sb("tmp", [128, G])
        tmp2_sb = em.sb("tmp2", [128, G])
        convf_sb = em.sb("convf", [128, 6 * G])
        xnh_sb = em.sb("xnh", [128, 8 * 3], BF16)

        p_mm = em.ps("pmm2", [128, MB * G])
        p_st = em.ps("pst", [128, DI // 2])
        p_trb = em.ps("ptrb", [128, 8 * 128], BF16)
        p_trf = em.ps("ptrf", [128, 128])

        em.step("vector", lambda e: e.memset(onesc[:, :], 1.0))
        em.step("vector", lambda e: e.memset(onesr[:, :], 1.0))
        em.step("vector", lambda e: e.memset(prev[:, :], 0.0))
        em.step("sync", lambda e: e.dma_start(out=sidb[:, :], in_=identb), dma=True)
        em.step("sync", lambda e: e.dma_start(out=sidf[:, :], in_=identf), dma=True)
        em.step("sync", lambda e: e.dma_start(out=cw_sb[:, 0:NB_X * 4], in_=convw), dma=True)
        em.step("sync", lambda e: e.dma_start(out=cb_sb[:, 0:NB_X], in_=convb), dma=True)
        em.step("sync", lambda e: e.dma_start(out=dtb_sb[:, :], in_=dtbias), dma=True)
        em.step("sync", lambda e: e.dma_start(out=nega_sb[:, :], in_=alog), dma=True)
        em.step("scalar", lambda e: e.activation(out=nega_sb[:, :], in_=nega_sb[:, :], func=AF.Exp))
        em.step("vector", lambda e: e.tensor_scalar_mul(nega_sb[:, :], nega_sb[:, :], -1.0))
        def ldwin(e):
            return [e.dma_start(out=w_sb[:, DIP * k:DIP * (k + 1)], in_=W_in[128 * k:128 * (k + 1), :])
                    for k in range(8)]
        em.step("gpsimd", ldwin, dma=True, n_dma=8)

        def ldh(e):
            return [e.dma_start(out=xnh_sb[:, 3 * k:3 * (k + 1)], in_=xnh[k, :, :]) for k in range(8)]
        em.step("sync", ldh, dma=True, n_dma=8)
        for bb in range(0, NB_X, 6):
            bs = list(range(bb, min(bb + 6, NB_X)))
            def mmh(e, bs=bs):
                r = None
                for bi, b in enumerate(bs):
                    mcol = (NB_Z + b) * 128
                    for k in range(8):
                        r = e.matmul(p_mm[:, 512 * (bi % 4):512 * (bi % 4) + 3],
                                     w_sb[:, DIP * k + mcol:DIP * k + mcol + 128],
                                     xnh_sb[:, 3 * k:3 * (k + 1)], start=(k == 0), stop=(k == 7)) if bi < 4 else r
                return r
            for sub in range(0, len(bs), 4):
                sbs = bs[sub:sub + 4]
                def mmh2(e, sbs=sbs):
                    r = None
                    for bi, b in enumerate(sbs):
                        mcol = (NB_Z + b) * 128
                        for k in range(8):
                            r = e.matmul(p_mm[:, 512 * bi:512 * bi + 3],
                                         w_sb[:, DIP * k + mcol:DIP * k + mcol + 128],
                                         xnh_sb[:, 3 * k:3 * (k + 1)], start=(k == 0), stop=(k == 7))
                    return r
                em.step("tensor", mmh2)
                def evh(e, sbs=sbs):
                    r = None
                    for bi, b in enumerate(sbs):
                        r = e.tensor_copy(xbc_g[:, b * (G + 3):b * (G + 3) + 3], p_mm[:, 512 * bi:512 * bi + 3])
                    return r
                em.step("vector", evh)

        for g in range(NG):
            t0 = g * G
            def ldx(e, t0=t0):
                return [e.dma_start(out=xn_g[:, G * k:G * (k + 1)], in_=xn[k, :, t0:t0 + G])
                        for k in range(8)]
            em.step("sync", ldx, dma=True, n_dma=8)

            for mb in range(0, NB_Z, MB):
                def mmz(e, mb=mb):
                    r = None
                    for mi in range(MB):
                        m = mb + mi
                        for k in range(8):
                            r = e.matmul(p_mm[:, G * mi:G * (mi + 1)],
                                         w_sb[:, DIP * k + 128 * m:DIP * k + 128 * (m + 1)],
                                         xn_g[:, G * k:G * (k + 1)], start=(k == 0), stop=(k == 7))
                    return r
                em.step("tensor", mmz)
                def evz(e):
                    r = None
                    for mi in range(MB):
                        r = e.activation(out=zstage[:, G * mi:G * (mi + 1)],
                                         in_=p_mm[:, G * mi:G * (mi + 1)], func=AF.Silu)
                    return r
                em.step("scalar", evz)
                def stz(e, mb=mb, t0=t0):
                    return [e.dma_start(out=o_zs[mb + mi, :, t0:t0 + G], in_=zstage[:, G * mi:G * (mi + 1)])
                            for mi in range(MB)]
                em.step("sync", stz, dma=True, n_dma=MB)

            for mb in range(0, NB_X, MB):
                ms = list(range(mb, min(mb + MB, NB_X)))
                def mmx(e, ms=ms):
                    r = None
                    for mi, b in enumerate(ms):
                        mcol = (NB_Z + b) * 128
                        for k in range(8):
                            r = e.matmul(p_mm[:, G * mi:G * (mi + 1)],
                                         w_sb[:, DIP * k + mcol:DIP * k + mcol + 128],
                                         xn_g[:, G * k:G * (k + 1)], start=(k == 0), stop=(k == 7))
                    return r
                em.step("tensor", mmx)
                def evx(e, ms=ms):
                    r = None
                    for mi, b in enumerate(ms):
                        r = e.tensor_copy(xbc_g[:, b * (G + 3) + 3:b * (G + 3) + 3 + G],
                                          p_mm[:, G * mi:G * (mi + 1)])
                    return r
                em.step("vector", evx)

            def mmd(e):
                r = None
                for k in range(8):
                    r = e.matmul(p_mm[0:32, 0:G], w_sb[:, DIP * k + 4352:DIP * k + 4384],
                                 xn_g[:, G * k:G * (k + 1)], start=(k == 0), stop=(k == 7))
                return r
            em.step("tensor", mmd)
            em.step("vector", lambda e: e.tensor_scalar_add(dt_g[:, :], p_mm[0:32, 0:G], dtb_sb[:, 0:1]))
            em.step("vector", lambda e: e.tensor_relu(da_g[:, :], dt_g[:, :]))
            def actdt(e):
                e.activation(out=tmp_sb[0:32, :], in_=dt_g[:, :], func=AF.Abs)
                e.activation(out=tmp_sb[0:32, :], in_=tmp_sb[0:32, :], func=AF.Exp, scale=-1.0)
                return e.activation(out=tmp_sb[0:32, :], in_=tmp_sb[0:32, :], func=AF.Ln, bias=1.0)
            em.step("scalar", actdt)
            def dvdt(e):
                e.tensor_add(dt_g[:, :], da_g[:, :], tmp_sb[0:32, :])
                e.tensor_scalar_mul(da_g[:, :], dt_g[:, :], nega_sb[:, 0:1])
                e.tensor_tensor_scan(dacs_g[:, 0:256], onesc[:, :], da_g[:, 0:256], 0.0, ALU.mult, ALU.add)
                return e.tensor_tensor_scan(dacs_g[:, 256:512], onesc[:, :], da_g[:, 256:512], 0.0, ALU.mult, ALU.add)
            em.step("vector", dvdt)
            def cpalast(e, g=g):
                e.tensor_copy(alast_sb[:, 2 * g:2 * g + 1], dacs_g[:, 255:256])
                return e.tensor_copy(alast_sb[:, 2 * g + 1:2 * g + 2], dacs_g[:, 511:512])
            em.step("vector", cpalast)
            em.step("sync", lambda e, t0=t0: e.dma_start(out=o_dacs[:, t0:t0 + G], in_=dacs_g[:, :]), dma=True)

            for bb in range(0, NB_X, 6):
                bs = list(range(bb, min(bb + 6, NB_X)))
                def conv(e, bs=bs):
                    r = None
                    for bi, b in enumerate(bs):
                        base = b * (G + 3)
                        e.tensor_scalar_mul(convf_sb[:, G * bi:G * (bi + 1)],
                                            xbc_g[:, base:base + G], cw_sb[:, 4 * b:4 * b + 1])
                        for kk in range(1, 4):
                            r = e.scalar_tensor_tensor(convf_sb[:, G * bi:G * (bi + 1)],
                                                       xbc_g[:, base + kk:base + kk + G],
                                                       cw_sb[:, 4 * b + kk:4 * b + kk + 1],
                                                       convf_sb[:, G * bi:G * (bi + 1)],
                                                       ALU.mult, ALU.add)
                    return r
                em.step("vector", conv)
                def silub(e, bs=bs):
                    r = None
                    for bi, b in enumerate(bs):
                        r = e.activation(out=conv_bf[:, G * bi:G * (bi + 1)],
                                         in_=convf_sb[:, G * bi:G * (bi + 1)],
                                         func=AF.Silu, bias=cb_sb[:, b:b + 1])
                    return r
                em.step("scalar", silub)
                def stconv(e, bs=bs, t0=t0):
                    rs = []
                    for bi, b in enumerate(bs):
                        if b < 16:
                            rs.append(e.dma_start(out=o_xfm[b, :, t0:t0 + G], in_=conv_bf[:, G * bi:G * (bi + 1)]))
                    return rs
                nst = sum(1 for b in bs if b < 16)
                if nst:
                    em.step("sync", stconv, dma=True, n_dma=nst)
                def keepbc(e, bs=bs):
                    r = None
                    for bi, b in enumerate(bs):
                        if b == 16:
                            r = e.tensor_copy(bfm_g[:, :], conv_bf[:, G * bi:G * (bi + 1)])
                        elif b == 17:
                            r = e.tensor_copy(cfm_g[:, :], conv_bf[:, G * bi:G * (bi + 1)])
                    return r
                if any(b >= 16 for b in bs):
                    em.step("vector", keepbc)
                    em.step("sync", lambda e, t0=t0: e.dma_start(out=o_cfm[:, t0:t0 + G], in_=cfm_g[:, :]), dma=True)
                xbs = [b for b in bs if b < 16]
                for pairs in range(0, len(xbs), 2):
                    pb2 = xbs[pairs:pairs + 2]
                    def trx(e, pb2=pb2, bs=bs):
                        r = None
                        for oi, b in enumerate(pb2):
                            bi = bs.index(b)
                            for jb in range(4):
                                r = e.transpose(p_trb[:, 128 * (4 * oi + jb):128 * (4 * oi + jb + 1)],
                                                conv_bf[:, G * bi + 128 * jb:G * bi + 128 * (jb + 1)], sidb[:, :])
                        return r
                    em.step("tensor", trx)
                    def evtx(e, pb2=pb2):
                        r = None
                        for oi, b in enumerate(pb2):
                            for jb in range(4):
                                r = e.tensor_copy(xtm_g[:, jb * DI + 128 * b:jb * DI + 128 * (b + 1)],
                                                  p_trb[:, 128 * (4 * oi + jb):128 * (4 * oi + jb + 1)])
                        return r
                    em.step("vector", evtx)
                if 16 in bs:
                    bi16 = bs.index(16)
                    def trb16(e, bi16=bi16):
                        r = None
                        for jb in range(4):
                            r = e.transpose(p_trb[:, 128 * jb:128 * (jb + 1)],
                                            conv_bf[:, G * bi16 + 128 * jb:G * bi16 + 128 * (jb + 1)], sidb[:, :])
                        return r
                    em.step("tensor", trb16)
                    def evb16(e):
                        r = None
                        for jb in range(4):
                            r = e.tensor_copy(btm_g[:, 128 * jb:128 * (jb + 1)], p_trb[:, 128 * jb:128 * (jb + 1)])
                        return r
                    em.step("vector", evb16)
                def carry(e, bs=bs):
                    r = None
                    for b in bs:
                        base = b * (G + 3)
                        r = e.tensor_copy(xbc_g[:, base:base + 3], xbc_g[:, base + G:base + G + 3])
                    return r
                em.step("vector", carry)

            def trdt(e):
                r = None
                for jb in range(4):
                    r = e.transpose(p_trf[:, 0:32], dt_g[:, 128 * jb:128 * (jb + 1)], sidf[0:32, 0:32])
                    r = e.transpose(p_st[:, 32 * jb:32 * (jb + 1)] if False else r, dt_g[:, 0:1], sidf[0:1, 0:1]) if False else r
                return r
            def trdt4(e):
                r = None
                for jb in range(4):
                    r = e.transpose(p_st[0:128, 128 * jb:128 * jb + 32].bitcast(F32) if False else p_st[:, 128 * jb:128 * jb + 32],
                                    dt_g[:, 128 * jb:128 * (jb + 1)], sidf[0:32, 0:32])
                return r
            em.step("tensor", trdt4)
            def evdt4(e):
                r = None
                for jb in range(4):
                    r = e.tensor_copy(dttm_g[:, 32 * jb:32 * (jb + 1)], p_st[:, 128 * jb:128 * jb + 32])
                return r
            em.step("vector", evdt4)
            def trda4(e):
                r = None
                for jb in range(4):
                    r = e.transpose(p_st[:, 128 * jb:128 * jb + 32], dacs_g[:, 128 * jb:128 * (jb + 1)], sidf[0:32, 0:32])
                return r
            em.step("tensor", trda4)
            def evda4(e):
                r = None
                for jb in range(4):
                    r = e.tensor_copy(dacstm_g[:, 32 * jb:32 * (jb + 1)], p_st[:, 128 * jb:128 * jb + 32])
                return r
            em.step("vector", evda4)
            def stdacstm(e, g=g):
                rs = []
                for jb in range(4):
                    c = 2 * g + jb // 2
                    rs.append(e.dma_start(out=o_dacstm[c, 128 * (jb % 2):128 * (jb % 2 + 1), :],
                                          in_=dacstm_g[:, 32 * jb:32 * (jb + 1)]))
                return rs
            em.step("sync", stdacstm, dma=True, n_dma=4)

            for cc in range(2):
                c = 2 * g + cc
                em.step("tensor", lambda e, c=c: e.transpose(
                    p_trf[0:1, 0:32], alast_sb[:, c:c + 1], sidf[0:32, 0:32]))
                em.step("vector", lambda e: e.tensor_copy(tmp_sb[0:1, 0:32], p_trf[0:1, 0:32]))
                em.step("tensor", lambda e: e.matmul(p_trf[:, 0:32], onesr[0:1, :], tmp_sb[0:1, 0:32], start=True, stop=True))
                em.step("scalar", lambda e: e.activation(out=cdbc[:, :], in_=p_trf[:, 0:32], func=AF.Exp))
                def decw(e, cc=cc):
                    r = None
                    for jb2 in range(2):
                        jb = 2 * cc + jb2
                        e.tensor_sub(decay_g[:, 32 * jb:32 * (jb + 1)], p_trf[:, 0:32], dacstm_g[:, 32 * jb:32 * (jb + 1)])
                    return e.tensor_copy(tmp_sb[0:1, 0:1], tmp_sb[0:1, 0:1])
                em.step("vector", decw)
                def decexp(e, cc=cc):
                    r = None
                    for jb2 in range(2):
                        jb = 2 * cc + jb2
                        r = e.activation(out=decay_g[:, 32 * jb:32 * (jb + 1)],
                                         in_=decay_g[:, 32 * jb:32 * (jb + 1)], func=AF.Exp)
                    return r
                em.step("scalar", decexp)
                def wxt(e, cc=cc):
                    r = None
                    for jb2 in range(2):
                        jb = 2 * cc + jb2
                        e.tensor_mul(w_g[:, 32 * jb:32 * (jb + 1)], decay_g[:, 32 * jb:32 * (jb + 1)],
                                     dttm_g[:, 32 * jb:32 * (jb + 1)])
                    for jb2 in range(2):
                        jb = 2 * cc + jb2
                        for h in range(NH):
                            e.tensor_scalar_mul(xtd_g[:, jb * DI + 64 * h:jb * DI + 64 * (h + 1)],
                                                xtm_g[:, jb * DI + 64 * h:jb * DI + 64 * (h + 1)],
                                                w_g[:, 32 * jb + h:32 * jb + h + 1])
                            r = e.tensor_scalar_mul(xtm_g[:, jb * DI + 64 * h:jb * DI + 64 * (h + 1)],
                                                    xtm_g[:, jb * DI + 64 * h:jb * DI + 64 * (h + 1)],
                                                    dttm_g[:, 32 * jb + h:32 * jb + h + 1])
                    return r
                em.step("vector", wxt)
                def stxt(e, cc=cc, c=c):
                    rs = []
                    for jb2 in range(2):
                        jb = 2 * cc + jb2
                        rs.append(e.dma_start(out=o_xt[c, 128 * jb2:128 * (jb2 + 1), :],
                                              in_=xtm_g[:, jb * DI:(jb + 1) * DI]))
                    return rs
                em.step("sync", stxt, dma=True, n_dma=2)
                for hh in range(2):
                    def mmstates(e, cc=cc, hh=hh):
                        r = None
                        for h2 in range(16):
                            h = 16 * hh + h2
                            for jb2 in range(2):
                                jb = 2 * cc + jb2
                                r = e.matmul(p_st[:, 64 * h2:64 * (h2 + 1)],
                                             btm_g[:, 128 * jb:128 * (jb + 1)],
                                             xtd_g[:, jb * DI + 64 * h:jb * DI + 64 * (h + 1)],
                                             start=(jb2 == 0), stop=(jb2 == 1))
                        return r
                    em.step("tensor", mmstates)
                    em.step("vector", lambda e, hh=hh: e.tensor_copy(
                        states_sb[:, 1024 * hh:1024 * (hh + 1)], p_st[:, :]))
                em.step("sync", lambda e, c=c: e.dma_start(out=o_states[c, :, :], in_=states_sb[:, :]), dma=True)
                def scanup(e):
                    r = None
                    for h in range(NH):
                        r = e.scalar_tensor_tensor(prev[:, 64 * h:64 * (h + 1)], prev[:, 64 * h:64 * (h + 1)],
                                                   cdbc[:, h:h + 1], states_sb[:, 64 * h:64 * (h + 1)],
                                                   ALU.mult, ALU.add)
                    return r
                em.step("vector", scanup)
                def mmbase(e, cc=cc):
                    r = None
                    for jb2 in range(2):
                        r = e.matmul(p_mm[:, 512 * jb2:512 * jb2 + 256],
                                     bfm_g[:, 256 * cc + 128 * jb2:256 * cc + 128 * (jb2 + 1)],
                                     cfm_g[:, 256 * cc:256 * (cc + 1)], start=True, stop=True)
                    return r
                em.step("tensor", mmbase)
                def evbase(e):
                    e.tensor_copy(base_sb[:, 0:256], p_mm[:, 0:256])
                    return e.tensor_copy(base_sb[:, 256:512], p_mm[:, 512:768])
                em.step("vector", evbase)
                def stbase(e, c=c):
                    return [e.dma_start(out=o_baseT[c, 0:128, :], in_=base_sb[:, 0:256]),
                            e.dma_start(out=o_baseT[c, 128:256, :], in_=base_sb[:, 256:512])]
                em.step("sync", stbase, dma=True, n_dma=2)

        em.step("sync", lambda e: e.dma_start(out=o_alast, in_=alast_sb[:, :]), dma=True)
        em.step("sync", lambda e: e.dma_start(out=o_sout, in_=prev[:, :]), dma=True)


def emit_s4(em, io):
    nc = em.nc
    GH = 8
    NG_H = NH // GH
    with em.arena():
        i_xt = io["xt_tm"]
        i_base = io["baseT"]
        i_cfm = io["C_fm"]
        i_dacs = io["dAcs"]
        i_dacstm = io["dAcs_tm"]
        i_alast = io["alast"]
        i_states = io["states"]
        i_sin = io["S_in"]
        i_zs = io["zs"]
        i_xfm = io["xfm"]
        i_wout = io["W_out"]
        i_nw = io["normw"]
        i_dbc = io["d_bc"]
        i_tril = io["trilT"]
        o_s0 = io["s0"]

        w2_sb = em.sb("wout", [128, 16 * 1024], BF16)
        xt_sb = em.sb("xt", [128, 2 * DI], BF16)
        base_sb = em.sb("base", [128, 2 * 256], BF16)
        st_sb = em.sb("st", [128, DI])
        prev = em.sb("prev4", [128, DI])
        prev_bf = em.sb("prevbf", [128, DI], BF16)
        cfm_sb = em.sb("cfm", [128, T], BF16)
        zs_sb = em.sb("zs", [128, 16 * 256], BF16)
        xf_sb = em.sb("xf", [128, 16 * 256], BF16)
        dacsr = em.sb("dacsr", [1, NH * 256])
        dtm_sb = em.sb("dtm", [128, 2 * 32])
        alr = em.sb("alr", [1, 32])
        cd_sb = em.sb("cd", [128, 32])
        tril_sb = em.sb("tril", [128, 2 * 256])
        nw_sb = em.sb("nw", [128, 16])
        dbc_sb = em.sb("dbc", [128, 32])
        onesr = em.sb("onesr4", [1, 128])
        onesc = em.sb("onesc4", [128, 1])
        e_sb = em.sb("esb", [128, GH * 256])
        csc_sb = em.sb("csc", [128, GH * 256], BF16)
        dif_sb = em.sb("dif", [128, 2 * GH * 256])
        expd_sb = em.sb("expd", [128, 2 * GH * 256], BF16)
        rhs_sb = em.sb("rhs", [128, 2 * GH * 256], BF16)
        y2_sb = em.sb("y2", [128, 16 * 256])
        sq_sb = em.sb("sq4", [128, 16 * 256])
        y3_sb = em.sb("y3", [128, 16 * 256], BF16)
        rows = em.sb("rows4", [1, 4 * 256])
        outst = em.sb("outst", [128, 2 * 256], BF16)
        t64 = em.sb("t64", [128, 2 * 256])

        p_sh = em.ps("psh", [128, GH * 256])
        p_out = em.ps("pout4", [128, 2 * 512])
        p_misc = em.ps("pmisc4", [128, 512])
        p_bc = em.ps("pbc4", [128, 32])

        b_eps = em.cf1(1e-5)
        b_half = em.cf1(-0.5)

        em.step("vector", lambda e: e.memset(onesr[:, :], 1.0))
        em.step("vector", lambda e: e.memset(onesc[:, :], 1.0))
        em.step("sync", lambda e: e.dma_start(out=nw_sb[:, :], in_=i_nw), dma=True)
        em.step("sync", lambda e: e.dma_start(out=dbc_sb[:, :], in_=i_dbc), dma=True)
        em.step("sync", lambda e: e.dma_start(out=prev[:, :], in_=i_sin), dma=True)
        if "singate" in io:
            sg_sb = em.sb("sgate", [128, 1])
            em.step("sync", lambda e: e.dma_start(out=sg_sb[:, :], in_=io["singate"]), dma=True)
            em.step("vector", lambda e: e.tensor_scalar_mul(prev[:, :], prev[:, :], sg_sb[:, 0:1]))
        em.step("sync", lambda e: e.dma_start(out=cfm_sb[:, :], in_=i_cfm), dma=True)
        def ldtril(e):
            return [e.dma_start(out=tril_sb[:, 256 * jb:256 * (jb + 1)], in_=i_tril[jb, :, :])
                    for jb in range(2)]
        em.step("sync", ldtril, dma=True, n_dma=2)
        def ldw(e):
            return [e.dma_start(out=w2_sb[:, 1024 * k:1024 * (k + 1)],
                                in_=i_wout[128 * k:128 * (k + 1), :]) for k in range(16)]
        em.step("gpsimd", ldw, dma=True, n_dma=16)

        for c in range(NCH):
            t0 = c * 256
            def ldchunk(e, c=c, t0=t0):
                rs = []
                for jb in range(2):
                    rs.append(e.dma_start(out=xt_sb[:, DI * jb:DI * (jb + 1)], in_=i_xt[c, 128 * jb:128 * (jb + 1), :]))
                    rs.append(e.dma_start(out=base_sb[:, 256 * jb:256 * (jb + 1)], in_=i_base[c, 128 * jb:128 * (jb + 1), :]))
                    rs.append(e.dma_start(out=dtm_sb[:, 32 * jb:32 * (jb + 1)], in_=i_dacstm[c, 128 * jb:128 * (jb + 1), :]))
                rs.append(e.dma_start(out=st_sb[:, :], in_=i_states[c, :, :]))
                rs.append(e.dma_start(out=dacsr[0:1, :].rearrange("p (a b) -> p a b", a=32), in_=i_dacs[:, t0:t0 + 256]))
                rs.append(e.dma_start(out=alr[0:1, :].rearrange("p (a b) -> p a b", a=32), in_=i_alast[:, c:c + 1], allow_slow_non_contiguous=True))
                for k in range(16):
                    rs.append(e.dma_start(out=zs_sb[:, 256 * k:256 * (k + 1)], in_=i_zs[k, :, t0:t0 + 256]))
                    rs.append(e.dma_start(out=xf_sb[:, 256 * k:256 * (k + 1)], in_=i_xfm[k, :, t0:t0 + 256]))
                return rs
            em.step("sync", ldchunk, dma=True, n_dma=41)

            em.step("tensor", lambda e: e.matmul(p_bc[:, :], onesr[0:1, :], alr[0:1, :], start=True, stop=True))
            em.step("scalar", lambda e: e.activation(out=cd_sb[:, :], in_=p_bc[:, :], func=AF.Exp))
            em.step("vector", lambda e: e.tensor_copy(prev_bf[:, :], prev[:, :]))

            for g in range(NG_H):
                h0 = g * GH
                def mmab(e, h0=h0):
                    r = None
                    for j in range(GH):
                        r = e.matmul(p_sh[:, 256 * j:256 * (j + 1)], onesr[0:1, :],
                                     dacsr[0:1, 256 * (h0 + j):256 * (h0 + j + 1)], start=True, stop=True)
                    return r
                em.step("tensor", mmab)
                def acte(e):
                    r = None
                    for j in range(GH):
                        r = e.activation(out=e_sb[:, 256 * j:256 * (j + 1)],
                                         in_=p_sh[:, 256 * j:256 * (j + 1)], func=AF.Exp)
                    return r
                em.step("scalar", acte)
                def dv1(e, h0=h0, t0=t0):
                    r = None
                    for j in range(GH):
                        h = h0 + j
                        for jb in range(2):
                            r = e.scalar_tensor_tensor(
                                dif_sb[:, 256 * (2 * j + jb):256 * (2 * j + jb + 1)],
                                p_sh[:, 256 * j:256 * (j + 1)],
                                dtm_sb[:, 32 * jb + h:32 * jb + h + 1],
                                tril_sb[:, 256 * jb:256 * (jb + 1)], ALU.subtract, ALU.add)
                        r = e.tensor_mul(csc_sb[:, 256 * j:256 * (j + 1)],
                                         cfm_sb[:, t0:t0 + 256], e_sb[:, 256 * j:256 * (j + 1)])
                    return r
                em.step("vector", dv1)
                def act2(e):
                    r = None
                    for j in range(2 * GH):
                        r = e.activation(out=expd_sb[:, 256 * j:256 * (j + 1)],
                                         in_=dif_sb[:, 256 * j:256 * (j + 1)], func=AF.Exp)
                    return r
                em.step("scalar", act2)
                def dv2(e):
                    r = None
                    for j in range(GH):
                        for jb in range(2):
                            r = e.tensor_mul(rhs_sb[:, 256 * (2 * j + jb):256 * (2 * j + jb + 1)],
                                             expd_sb[:, 256 * (2 * j + jb):256 * (2 * j + jb + 1)],
                                             base_sb[:, 256 * jb:256 * (jb + 1)])
                    return r
                em.step("vector", dv2)
                def mmy(e, h0=h0):
                    r = None
                    for j in range(GH):
                        h = h0 + j
                        pb = (h % 2) * 64
                        for jb in range(2):
                            e.matmul(p_sh[pb:pb + 64, 256 * j:256 * (j + 1)],
                                     xt_sb[:, DI * jb + 64 * h:DI * jb + 64 * (h + 1)],
                                     rhs_sb[:, 256 * (2 * j + jb):256 * (2 * j + jb + 1)],
                                     start=(jb == 0), stop=False)
                        r = e.matmul(p_sh[pb:pb + 64, 256 * j:256 * (j + 1)],
                                     prev_bf[:, 64 * h:64 * (h + 1)],
                                     csc_sb[:, 256 * j:256 * (j + 1)], start=False, stop=True)
                    return r
                em.step("tensor", mmy)
                def dv3(e, h0=h0):
                    r = None
                    for j in range(GH):
                        h = h0 + j
                        pb = (h % 2) * 64
                        kb = h // 2
                        e.scalar_tensor_tensor(t64[pb:pb + 64, 0:256],
                                               xf_sb[pb:pb + 64, 256 * kb:256 * (kb + 1)],
                                               dbc_sb[pb:pb + 64, h:h + 1],
                                               p_sh[pb:pb + 64, 256 * j:256 * (j + 1)], ALU.mult, ALU.add)
                        r = e.tensor_mul(y2_sb[pb:pb + 64, 256 * kb:256 * (kb + 1)],
                                         t64[pb:pb + 64, 0:256],
                                         zs_sb[pb:pb + 64, 256 * kb:256 * (kb + 1)])
                    return r
                em.step("vector", dv3)

            def scanup(e):
                r = None
                for h in range(NH):
                    r = e.scalar_tensor_tensor(prev[:, 64 * h:64 * (h + 1)], prev[:, 64 * h:64 * (h + 1)],
                                               cd_sb[:, h:h + 1], st_sb[:, 64 * h:64 * (h + 1)],
                                               ALU.mult, ALU.add)
                return r
            em.step("vector", scanup)

            em.step("scalar", lambda e: e.activation(out=sq_sb[:, :], in_=y2_sb[:, :], func=AF.Square))
            def mmsum(e):
                r = None
                for k in range(16):
                    r = e.matmul(p_misc[0:1, 0:256], onesc[:, 0:1], sq_sb[:, 256 * k:256 * (k + 1)],
                                 start=(k == 0), stop=(k == 15))
                return r
            em.step("tensor", mmsum)
            ms = rows[0:1, 0:256]
            rstd = rows[0:1, 256:512]
            em.step("vector", lambda e: e.tensor_scalar_mul(ms, p_misc[0:1, 0:256], 1.0 / DI))
            em.step("scalar", lambda e: e.activation(out=rstd, in_=ms, func=AF.Ln, bias=b_eps))
            em.step("scalar", lambda e: e.activation(out=rstd, in_=rstd, func=AF.Exp, scale=b_half))
            em.step("tensor", lambda e: e.matmul(p_misc[:, 256:512], onesr[0:1, :], rstd, start=True, stop=True))
            def y3ops(e):
                r = None
                for k in range(16):
                    r = e.scalar_tensor_tensor(y3_sb[:, 256 * k:256 * (k + 1)], y2_sb[:, 256 * k:256 * (k + 1)],
                                               nw_sb[:, k:k + 1], p_misc[:, 256:512], ALU.mult, ALU.mult)
                return r
            em.step("vector", y3ops)
            for mp2 in range(4):
                def mmo2(e, mp2=mp2):
                    r = None
                    for mi in range(2):
                        m = 2 * mp2 + mi
                        for k in range(16):
                            r = e.matmul(p_out[:, 512 * mi:512 * mi + 256],
                                         w2_sb[:, 1024 * k + 128 * m:1024 * k + 128 * (m + 1)],
                                         y3_sb[:, 256 * k:256 * (k + 1)], start=(k == 0), stop=(k == 15))
                    return r
                em.step("tensor", mmo2)
                def ev2(e):
                    e.tensor_copy(outst[:, 0:256], p_out[:, 0:256])
                    return e.tensor_copy(outst[:, 256:512], p_out[:, 512:768])
                em.step("vector", ev2)
                def st2(e, mp2=mp2, t0=t0):
                    return [e.dma_start(out=o_s0[2 * mp2, :, t0:t0 + 256], in_=outst[:, 0:256]),
                            e.dma_start(out=o_s0[2 * mp2 + 1, :, t0:t0 + 256], in_=outst[:, 256:512])]
                em.step("sync", st2, dma=True, n_dma=2)


def emit_moe(em, io, HID, has_bias, post, HSPLIT=1, part="all", mode="dense"):
    nc = em.nc
    if part in ("all", "experts"):
        i_xn = io["xn_bf"]
        i_masks = io["masks"]
        i_scores = io["scores"]
        i_w1 = io["w1"]
        i_w2 = io["w2"]
        HH = HID // HSPLIT
        NKH = HH // 128
        NKO = HH // 128
        NSPL = 8 * HSPLIT

        acc_sb = em.sb("accm", [128, 8 * T], BF16)
        s_bc = em.sb("sbc", [128, 256])
        tstage = em.sb("tstage", [128, 512])
        gstage = em.sb("gstage", [128, 256], BF16)
        m_bc = em.sb("mbc", [128, 512])
        onesr = em.sb("onesrm", [1, 128])
        p_bc = em.ps("pbcm", [128, 512])

        em.step("vector", lambda e: e.memset(onesr[:, :], 1.0))
        em.step("vector", lambda e: e.memset(acc_sb[:, :], 0.0))
        ch0 = (em.ch, em.n)
        gch0 = (em.gch, em.gn)

        em._moe_acc = acc_sb
        if mode == "top1":
            _moe_top1_experts(em, io, HID, has_bias, HSPLIT)
        else:
            _moe_dense_experts(em, io, HID, has_bias, HSPLIT, acc_sb, m_bc, tstage, onesr, p_bc, ch0, gch0)
        em._moe_st = dict(acc_sb=acc_sb, s_bc=s_bc, tstage=tstage, gstage=gstage,
                          m_bc=m_bc, onesr=onesr, p_bc=p_bc)

    if part in ("all", "post"):
        _moe_post(em, io, has_bias and mode != "top1", post)


def _moe_top1_experts(em, io, HID, has_bias, HSPLIT):
    nc = em.nc
    i_idxf = io["idxf"]
    i_xtm = io["xn_tm"]
    go = io["gout_tm"]
    i_w1 = io["w1"]
    i_w2 = io["w2"]
    acc_sb = em._moe_acc
    HH = HID // HSPLIT
    NKH = HH // 128
    NKO = HH // 128
    NT2 = CAP // 512
    NSLOT = 8 * CAP
    NTT = T // 128

    with em.arena():
        sidb = em.sb("sidb5", [128, 128], BF16)
        sidf = em.sb("sidf5", [128, 128])
        idxst = em.sb("idxst", [16, 128])
        idxc = em.sb("idxc", [128, NTT], U32)
        idxcf = em.sb("idxcf", [128, NTT])
        idt = em.sb("idt", [128, 512])
        iot = em.sb("iot", [128, 512])
        xtm = em.sb("xtm5", [128, 16 * 1024], BF16)
        qb = em.sb("qb5", [128, 16 * 512], BF16)
        xin = em.sb("xin", [128, 2 * 1024], BF16)
        xout = em.sb("xout", [128, 4 * 1024], BF16)
        gxf = em.sb("gxf", [128, 8 * 512], BF16)
        h_sb = em.sb("hm5", [128, NKH * 512], BF16)
        obuf = em.sb("obuf", [128, 8 * 512], BF16)
        if HSPLIT > 1:
            oacc = em.sb("oacc", [128, 8 * 512])
        w1_sb = em.sb("w1t", [128, 2 * 8 * HH], BF16)
        w2_sb = em.sb("w2t", [128, NKO * 1024], BF16)
        if has_bias:
            b1c = em.sb("b1c5", [128, HID // 128])
            b2c = em.sb("b2c5", [128, 8])
        p_h = em.ps("ph5", [128, 2 * 512])
        p_out = em.ps("po5", [128, 2 * 512])
        p_gx = em.ps("pgx5", [128, 2 * 512])
        p_trb = em.ps("ptr5", [128, 8 * 128], BF16)

        em.step("sync", lambda e: e.dma_start(out=sidb[:, :], in_=io["identb"]), dma=True)
        em.step("sync", lambda e: e.dma_start(out=sidf[:, :], in_=io["identf"]), dma=True)
        em.step("sync", lambda e: e.dma_start(
            out=idxst[:, :], in_=i_idxf[0:1, :].rearrange("p (a b) -> (p a) b", a=16)), dma=True)
        em.step("tensor", lambda e: e.transpose(p_h[:, 0:16], idxst[0:16, :], sidf[0:16, 0:16]))
        em.step("vector", lambda e: e.tensor_copy(idxc[:, :], p_h[:, 0:16]))
        em.step("vector", lambda e: e.tensor_copy(idxcf[:, :], p_h[:, 0:16]))
        onesr5 = em.sb("onesr5", [1, 128])
        em.step("vector", lambda e: e.memset(onesr5[:, :], 1.0))
        em.step("sync", lambda e: e.dma_start(out=iot[0:1, :], in_=io["iota512"]), dma=True)
        em.step("tensor", lambda e: e.matmul(p_gx[:, 0:512], onesr5[0:1, :], iot[0:1, :],
                                             start=True, stop=True))
        em.step("vector", lambda e: e.tensor_copy(iot[:, :], p_gx[:, 0:512]))
        def ldxtm(e):
            return [e.dma_start(out=xtm[:, 1024 * c:1024 * (c + 1)],
                                in_=i_xtm[128 * c:128 * (c + 1), :]) for c in range(16)]
        em.step("sync", ldxtm, dma=True, n_dma=16)
        ch1 = (em.ch, em.n)
        gch1 = (em.gch, em.gn)

        SW = em.mksem("SW5")
        SB = em.mksem("SB5")
        SQB = em.mksem("SQB")
        SGE = em.mksem("SGE")
        STR = em.mksem("STR")
        SEV = em.mksem("SEV")
        SMMG = em.mksem("SMG5")
        SMMH = em.mksem("SMH5")
        SH = em.mksem("SH5")
        SMMO = em.mksem("SMO5")
        SOA = em.mksem("SOA")
        SST = em.mksem("SST")

        w1slot_free = [0, 0]
        w2_free = 0
        b1_free = 0
        b2_free = 0
        b_ready = 0
        gxf_free = 0
        qb_free = 0
        ptrb_free = 0
        xout_free = 0
        hsb_free = 0
        obuf_free = 0
        gxf_ready = 0
        gslot = 0

        for vi in range(8 * NT2):
            e_i, t2 = divmod(vi, NT2)
            base = float(e_i * CAP + 512 * t2)
            g0 = e_i * CAP + 512 * t2
            if has_bias and t2 == 0:
                em.stepd("sync", lambda e, e_i=e_i: e.dma_start(
                    out=b1c[:, :], in_=io["b1c"][e_i, :, :]),
                    deps=[(SH, b1_free), ch1], inc=SB, dma=True)
                em.stepd("sync", lambda e, e_i=e_i: e.dma_start(
                    out=b2c[:, :], in_=io["b2cc"][e_i, :, :]),
                    deps=[(SOA, b2_free)], inc=SB, dma=True)
                b_ready = SB.n
            em.stepd("vector", lambda e, base=base: e.tensor_scalar_add(idt[:, :], iot[:, :], base),
                     deps=[(SQB, max(SQB.n - 2, 0)), ch1], inc=SQB)
            for ch8 in range(2):
                def mkq(e, ch8=ch8):
                    r = None
                    for cc in range(8):
                        c = 8 * ch8 + cc
                        r = e.tensor_scalar(qb[:, 512 * c:512 * (c + 1)], idt[:, :],
                                            idxcf[:, c:c + 1], None, ALU.is_equal)
                    return r
                em.stepd("vector", mkq, deps=[(SMMG, qb_free), (SQB, SQB.n)], inc=SQB)
            q_ready = SQB.n
            for m in range(8):
                ps0 = 512 * (m % 2)
                def mmg(e, m=m, ps0=ps0):
                    r = None
                    for c in range(16):
                        r = e.matmul(p_gx[:, ps0:ps0 + 512],
                                     xtm[:, 1024 * c + 128 * m:1024 * c + 128 * (m + 1)],
                                     qb[:, 512 * c:512 * (c + 1)],
                                     start=(c == 0), stop=(c == 15))
                    return r
                em.stepd("tensor", mmg,
                         deps=[(SQB, q_ready), (SGE, max(SGE.n - 1, 0)), (SMMH, gxf_free), ch1],
                         inc=SMMG)
                def evgx(e, m=m, ps0=ps0):
                    return e.activation(out=gxf[:, 512 * m:512 * (m + 1)],
                                        in_=p_gx[:, ps0:ps0 + 512], func=AF.Copy)
                em.stepd("scalar", evgx, deps=[(SMMG, SMMG.n), (SMMH, gxf_free)], inc=SGE)
            qb_free = SMMG.n
            gxf_ready = SGE.n

            for hs in range(HSPLIT):
                slot = gslot % 2
                gslot += 1
                h0 = hs * HH

                def ldw1(e, e_i=e_i, h0=h0, slot=slot):
                    return [e.dma_start(out=w1_sb[:, slot * 8 * HH + HH * k:slot * 8 * HH + HH * (k + 1)],
                                        in_=i_w1[e_i, 128 * k:128 * (k + 1), h0:h0 + HH]) for k in range(8)]
                em.stepd("gpsimd", ldw1, deps=[(SMMH, w1slot_free[slot]), ch1, gch1],
                         inc=SW, dma=True, n_dma=8)
                w1_ready = SW.n

                def ldw2(e, e_i=e_i, h0=h0):
                    return [e.dma_start(out=w2_sb[:, 1024 * k:1024 * (k + 1)],
                                        in_=i_w2[e_i, h0 + 128 * k:h0 + 128 * (k + 1), :]) for k in range(NKO)]
                em.stepd("gpsimd", ldw2, deps=[(SMMO, w2_free)], inc=SW, dma=True, n_dma=NKO)
                w2_ready = SW.n

                for mh in range(NKH):
                    ps0 = 512 * (mh % 2)
                    def mmh(e, mh=mh, slot=slot, ps0=ps0):
                        r = None
                        for k in range(8):
                            c0 = slot * 8 * HH + HH * k + 128 * mh
                            r = e.matmul(p_h[:, ps0:ps0 + 512],
                                         w1_sb[:, c0:c0 + 128],
                                         gxf[:, 512 * k:512 * (k + 1)],
                                         start=(k == 0), stop=(k == 7))
                        return r
                    em.stepd("tensor", mmh,
                             deps=[(SW, w1_ready), (SGE, gxf_ready), (SH, max(SH.n - 1, 0))],
                             inc=SMMH)

                    def evh(e, mh=mh, hs=hs, ps0=ps0):
                        if has_bias:
                            return e.activation(out=h_sb[:, 512 * mh:512 * (mh + 1)],
                                                in_=p_h[:, ps0:ps0 + 512], func=AF.Gelu,
                                                bias=b1c[:, hs * NKH + mh:hs * NKH + mh + 1])
                        return e.activation(out=h_sb[:, 512 * mh:512 * (mh + 1)],
                                            in_=p_h[:, ps0:ps0 + 512], func=AF.Gelu)
                    em.stepd("scalar", evh,
                             deps=[(SMMH, SMMH.n), (SB, b_ready), (SMMO, hsb_free)], inc=SH)
                h_ready = SH.n

                for mo in range(8):
                    ps0 = 512 * (mo % 2)
                    def mmo(e, mo=mo, ps0=ps0):
                        r = None
                        for k in range(NKO):
                            r = e.matmul(p_out[:, ps0:ps0 + 512],
                                         w2_sb[:, 1024 * k + 128 * mo:1024 * k + 128 * (mo + 1)],
                                         h_sb[:, 512 * k:512 * (k + 1)],
                                         start=(k == 0), stop=(k == NKO - 1))
                        return r
                    em.stepd("tensor", mmo,
                             deps=[(SW, w2_ready), (SH, h_ready), (SOA, max(SOA.n - 1, 0))],
                             inc=SMMO)

                    def evo(e, mo=mo, e_i=e_i, hs=hs, ps0=ps0):
                        ob = obuf[:, 512 * mo:512 * (mo + 1)]
                        po = p_out[:, ps0:ps0 + 512]
                        if HSPLIT > 1 and hs == 0:
                            return e.tensor_copy(oacc[:, 512 * mo:512 * (mo + 1)], po)
                        elif HSPLIT > 1 and hs < HSPLIT - 1:
                            oa = oacc[:, 512 * mo:512 * (mo + 1)]
                            return e.tensor_add(oa, oa, po)
                        elif HSPLIT > 1:
                            oa = oacc[:, 512 * mo:512 * (mo + 1)]
                            e.tensor_add(oa, oa, po)
                            if has_bias:
                                return e.tensor_scalar_add(ob, oa, b2c[:, e_i:e_i + 1])
                            return e.tensor_copy(ob, oa)
                        else:
                            if has_bias:
                                return e.tensor_scalar_add(ob, po, b2c[:, e_i:e_i + 1])
                            return e.tensor_copy(ob, po)
                    deps_evo = [(SMMO, SMMO.n)]
                    if hs == HSPLIT - 1:
                        deps_evo.append((STR, obuf_free))
                    em.stepd("vector", evo, deps=deps_evo, inc=SOA)

                if hs == HSPLIT - 1:
                    obuf_ready = SOA.n
                    for qt in range(4):
                        def trout(e, qt=qt):
                            r = None
                            for mo in range(8):
                                r = e.transpose(
                                    p_trb[:, 128 * mo:128 * (mo + 1)],
                                    obuf[:, 512 * mo + 128 * qt:512 * mo + 128 * (qt + 1)],
                                    sidb[:, :])
                            return r
                        em.stepd("tensor", trout, deps=[(SOA, obuf_ready), (SEV, ptrb_free)], inc=STR)

                        def evout(e, qt=qt):
                            r = None
                            for mo in range(8):
                                r = e.tensor_copy(
                                    xout[:, 1024 * qt + 128 * mo:1024 * qt + 128 * (mo + 1)],
                                    p_trb[:, 128 * mo:128 * (mo + 1)])
                            return r
                        em.stepd("vector", evout, deps=[(STR, STR.n), (SST, xout_free)], inc=SEV)
                        ptrb_free = SEV.n
                    obuf_free = STR.n

                    def sto(e, g0=g0):
                        return [e.dma_start(out=go[g0 + 128 * j:g0 + 128 * (j + 1), :],
                                            in_=xout[:, 1024 * j:1024 * (j + 1)]) for j in range(4)]
                    em.stepd("sync", sto, deps=[(SEV, SEV.n)], inc=SST, dma=True, n_dma=4)
                    xout_free = SST.n
                hsb_free = SMMO.n
                w1slot_free[slot] = SMMH.n
                w2_free = SMMO.n
                if _T1DBG >= 2:
                    em.join([SW, SB, SQB, SGE, STR, SEV, SMMG, SMMH, SH, SMMO, SOA, SST])
            if _T1DBG == 1:
                em.join([SW, SB, SQB, SGE, STR, SEV, SMMG, SMMH, SH, SMMO, SOA, SST])
            gxf_free = SMMH.n
            if t2 == NT2 - 1:
                b1_free = SH.n
                b2_free = SOA.n

        em.join([SW, SB, SQB, SGE, STR, SEV, SMMG, SMMH, SH, SMMO, SOA, SST])

        for g2 in range(NTT // 2):
            em.step("vector", lambda e: e.memset(xin[:, 0:2048], 0.0))

            def gat(e, g2=g2):
                return [e.indirect_dma_start(
                    xin[0:128, 1024 * j:1024 * (j + 1)], None, go[0:NSLOT, :],
                    IndirectOffsetOnAxis(idxc[:, 2 * g2 + j:2 * g2 + j + 1], 0),
                    bounds_check=NSLOT - 1, oob_is_err=False) for j in range(2)]
            em.step("gpsimd", gat, dma=True, n_dma=2)

            for j2 in range(2):
                def trg(e, j2=j2):
                    r = None
                    for kf in range(8):
                        r = e.transpose(p_trb[:, 128 * kf:128 * (kf + 1)],
                                        xin[:, 1024 * j2 + 128 * kf:1024 * j2 + 128 * (kf + 1)],
                                        sidb[:, :])
                    return r
                em.step("tensor", trg)

                def evg(e, g2=g2, j2=j2):
                    r = None
                    t = 2 * g2 + j2
                    for kf in range(8):
                        r = e.tensor_copy(acc_sb[:, T * kf + 128 * t:T * kf + 128 * (t + 1)],
                                          p_trb[:, 128 * kf:128 * (kf + 1)])
                    return r
                em.step("vector", evg)


def _moe_dense_experts(em, io, HID, has_bias, HSPLIT, acc_sb, m_bc, tstage, onesr, p_bc, ch0, gch0):
    nc = em.nc
    i_xn = io["xn_bf"]
    i_masks = io["masks"]
    i_w1 = io["w1"]
    i_w2 = io["w2"]
    HH = HID // HSPLIT
    NKH = HH // 128
    NKO = HH // 128
    NSPL = 8 * HSPLIT
    if True:
        with em.arena():
            w1_sb = em.sb("w1", [128, 2 * 8 * HH], BF16)
            w2_sb = em.sb("w2", [128, 2 * NKO * 1024], BF16)
            xn_sb = em.sb("xnm", [128, 2 * 8 * 512], BF16)
            h_sb = em.sb("hm", [128, NKH * 512], BF16)
            mrow = em.sb("mrow", [1, T])
            if has_bias:
                b1c = em.sb("b1c", [128, HID // 128])
            p_h = em.ps("ph", [128, 2 * 512])
            p_out = em.ps("poutm", [128, 2 * 512])

            SW = em.mksem("SW")
            SX = em.mksem("SX")
            SMR = em.mksem("SMR")
            SMMH = em.mksem("SMMH")
            SMMO = em.mksem("SMMO")
            SH = em.mksem("SH")
            SACC = em.mksem("SACC")
            SBC = em.mksem("SBC")
            SMBC = em.mksem("SMBC")

            wslot_free_h = [0, 0]
            wslot_free_o = [0, 0]
            xslot_free = [0, 0]
            mrow_free = 0
            b1c_free = 0
            smr_ready = 0
            gt = 0

            for si in range(NSPL):
                e_i, hs = divmod(si, HSPLIT)
                slot = si % 2
                h0 = hs * HH
                def ldw1(e, e_i=e_i, h0=h0, slot=slot):
                    return [e.dma_start(out=w1_sb[:, slot * 8 * HH + HH * k:slot * 8 * HH + HH * (k + 1)],
                                        in_=i_w1[e_i, 128 * k:128 * (k + 1), h0:h0 + HH]) for k in range(8)]
                em.stepd("gpsimd", ldw1, deps=[(SMMH, wslot_free_h[slot]), gch0, ch0], inc=SW, dma=True, n_dma=8)
                w1_ready = SW.n
                def ldw2(e, e_i=e_i, h0=h0, slot=slot):
                    return [e.dma_start(out=w2_sb[:, slot * NKO * 1024 + 1024 * k:slot * NKO * 1024 + 1024 * (k + 1)],
                                        in_=i_w2[e_i, h0 + 128 * k:h0 + 128 * (k + 1), :]) for k in range(NKO)]
                em.stepd("gpsimd", ldw2, deps=[(SMMO, wslot_free_o[slot])], inc=SW, dma=True, n_dma=NKO)
                w2_ready = SW.n
                if hs == 0:
                    em.stepd("sync", lambda e, e_i=e_i: e.dma_start(
                        out=mrow[0:1, :].rearrange("p (a b) -> p a b", a=1),
                        in_=i_masks[e_i:e_i + 1, :]), deps=[(SBC, mrow_free), ch0], inc=SMR, dma=True)
                    smr_ready = SMR.n
                    if has_bias:
                        em.stepd("sync", lambda e, e_i=e_i: e.dma_start(
                            out=b1c[:, :], in_=io["b1c"][e_i, :, :]),
                            deps=[(SH, b1c_free)], inc=SMR, dma=True)
                        smr_ready = SMR.n

                for t4 in range(4):
                    t0 = 512 * t4
                    xslot = gt % 2
                    def ldxn(e, t0=t0, xslot=xslot):
                        return [e.dma_start(out=xn_sb[:, xslot * 4096 + 512 * k:xslot * 4096 + 512 * (k + 1)],
                                            in_=i_xn[k, :, t0:t0 + 512]) for k in range(8)]
                    em.stepd("sync", ldxn, deps=[(SMMH, xslot_free[xslot]), ch0], inc=SX, dma=True, n_dma=8)
                    x_ready = SX.n
                    out_prev = SMMO.n
                    for mh in range(NKH):
                        pslot = (mh % 2) * 512
                        def mmh(e, mh=mh, pslot=pslot, slot=slot, xslot=xslot):
                            r = None
                            for k in range(8):
                                r = e.matmul(p_h[:, pslot:pslot + 512],
                                             w1_sb[:, slot * 8 * HH + HH * k + 128 * mh:slot * 8 * HH + HH * k + 128 * (mh + 1)],
                                             xn_sb[:, xslot * 4096 + 512 * k:xslot * 4096 + 512 * (k + 1)],
                                             start=(k == 0), stop=(k == 7))
                            return r
                        em.stepd("tensor", mmh,
                                 deps=[(SW, w1_ready), (SX, x_ready), (SH, max(SH.n - 1, 0))],
                                 inc=SMMH)
                        bia = hs * NKH + mh
                        if has_bias:
                            em.stepd("scalar", lambda e, mh=mh, pslot=pslot, bia=bia: e.activation(
                                out=h_sb[:, 512 * mh:512 * (mh + 1)], in_=p_h[:, pslot:pslot + 512],
                                func=AF.Gelu, bias=b1c[:, bia:bia + 1]),
                                deps=[(SMMH, SMMH.n), (SMMO, out_prev), (SMR, smr_ready)], inc=SH)
                        else:
                            em.stepd("scalar", lambda e, mh=mh, pslot=pslot: e.activation(
                                out=h_sb[:, 512 * mh:512 * (mh + 1)], in_=p_h[:, pslot:pslot + 512],
                                func=AF.Gelu),
                                deps=[(SMMH, SMMH.n), (SMMO, out_prev)], inc=SH)
                    xslot_free[xslot] = SMMH.n
                    h_ready = SH.n
                    em.stepd("tensor", lambda e, t0=t0: e.matmul(
                        p_bc[:, :], onesr[0:1, :], mrow[0:1, t0:t0 + 512], start=True, stop=True),
                        deps=[(SMR, smr_ready), (SMBC, SMBC.n), ch0], inc=SBC)
                    em.stepd("vector", lambda e: e.tensor_copy(m_bc[:, :], p_bc[:, :]),
                             deps=[(SBC, SBC.n)], inc=SMBC)
                    for mo in range(8):
                        pslot = (mo % 2) * 512
                        def mmo(e, mo=mo, pslot=pslot, slot=slot):
                            r = None
                            for k in range(NKO):
                                r = e.matmul(p_out[:, pslot:pslot + 512],
                                             w2_sb[:, slot * NKO * 1024 + 1024 * k + 128 * mo:slot * NKO * 1024 + 1024 * k + 128 * (mo + 1)],
                                             h_sb[:, 512 * k:512 * (k + 1)], start=(k == 0), stop=(k == NKO - 1))
                            return r
                        em.stepd("tensor", mmo,
                                 deps=[(SW, w2_ready), (SH, h_ready), (SACC, max(SACC.n - 1, 0))],
                                 inc=SMMO)
                        def accop(e, mo=mo, t0=t0, pslot=pslot):
                            e.tensor_mul(tstage[:, :], p_out[:, pslot:pslot + 512], m_bc[:, :])
                            return e.tensor_add(acc_sb[:, T * mo + t0:T * mo + t0 + 512],
                                                acc_sb[:, T * mo + t0:T * mo + t0 + 512], tstage[:, :])
                        em.stepd("vector", accop, deps=[(SMMO, SMMO.n), (SMBC, SMBC.n)], inc=SACC)
                    gt += 1
                wslot_free_h[slot] = SMMH.n
                wslot_free_o[slot] = SMMO.n
                mrow_free = SBC.n
                b1c_free = SH.n
            em.join([SW, SX, SMR, SMMH, SMMO, SH, SACC, SBC, SMBC])


def _moe_post(em, io, has_bias, post):
    nc = em.nc
    _st = em._moe_st
    acc_sb = _st["acc_sb"]; s_bc = _st["s_bc"]; tstage = _st["tstage"]
    gstage = _st["gstage"]; m_bc = _st["m_bc"]; onesr = _st["onesr"]
    p_bc = _st["p_bc"]
    i_masks = io["masks"]
    i_scores = io["scores"]
    if True:
        with em.arena():
            masks8 = em.sb("masks8", [8, T])
            srow = em.sb("srow", [1, T])
            p_out2 = em.ps("pout2", [128, 256])
            em.step("sync", lambda e: e.dma_start(out=masks8[0:8, :], in_=i_masks), dma=True)
            em.step("sync", lambda e: e.dma_start(out=srow[0:1, :].rearrange("p (a b) -> p a b", a=T // 128),
                                                  in_=i_scores), dma=True)
            if has_bias:
                b2r = em.sb("b2r", [8, 1024])
                em.step("sync", lambda e: e.dma_start(out=b2r[:, :], in_=io["b2"]), dma=True)
            if post == "l0":
                o_g = io["g"]
                for t8 in range(8):
                    t0 = 256 * t8
                    em.step("tensor", lambda e, t0=t0: e.matmul(
                        p_bc[:, 0:256], onesr[0:1, :], srow[0:1, t0:t0 + 256], start=True, stop=True))
                    em.step("vector", lambda e: e.tensor_copy(s_bc[:, :], p_bc[:, 0:256]))
                    for mo in range(8):
                        if has_bias:
                            def mmb2(e, mo=mo, t0=t0):
                                return e.matmul(p_out2[:, :], b2r[0:8, 128 * mo:128 * (mo + 1)],
                                                masks8[0:8, t0:t0 + 256], start=True, stop=True)
                            em.step("tensor", mmb2)
                            em.step("vector", lambda e, mo=mo, t0=t0: e.tensor_add(
                                tstage[:, 0:256], acc_sb[:, T * mo + t0:T * mo + t0 + 256], p_out2[:, :]))
                            em.step("vector", lambda e: e.tensor_mul(tstage[:, 0:256], tstage[:, 0:256], s_bc[:, :]))
                        else:
                            em.step("vector", lambda e, mo=mo, t0=t0: e.tensor_mul(
                                tstage[:, 0:256], acc_sb[:, T * mo + t0:T * mo + t0 + 256], s_bc[:, :]))
                        em.step("scalar", lambda e: e.activation(out=gstage[:, :], in_=tstage[:, 0:256], func=AF.Silu))
                        em.step("sync", lambda e, mo=mo, t0=t0: e.dma_start(
                            out=o_g[mo, :, t0:t0 + 256], in_=gstage[:, :]), dma=True)
            else:
                o_x2 = io["x2_fm"]
                i_x1 = io["x1_fm"]
                i_s1 = io["s1"]
                gam = em.sb("gam", [128, 8])
                bet = em.sb("bet", [128, 8])
                onesc = em.sb("onescm", [128, 1])
                x1_sb = em.sb("x1sb", [128, 256])
                s1_sb = em.sb("s1sb", [128, 256], BF16)
                as_sb = em.sb("assb", [128, 8 * 256])
                sq_sb = em.sb("sqm", [128, 8 * 256])
                rows = em.sb("rowsm", [1, 6 * 256])
                p_st1 = em.ps("pst1", [1, 256])
                p_st2 = em.ps("pst2", [1, 256])
                em.step("vector", lambda e: e.memset(onesc[:, :], 1.0))
                em.step("sync", lambda e: e.dma_start(out=gam[:, :], in_=io["gamma"]), dma=True)
                em.step("sync", lambda e: e.dma_start(out=bet[:, :], in_=io["beta"]), dma=True)
                for t8 in range(8):
                    t0 = 256 * t8
                    em.step("tensor", lambda e, t0=t0: e.matmul(
                        p_bc[:, 0:256], onesr[0:1, :], srow[0:1, t0:t0 + 256], start=True, stop=True))
                    em.step("vector", lambda e: e.tensor_copy(s_bc[:, :], p_bc[:, 0:256]))
                    for mo in range(8):
                        em.step("vector", lambda e, mo=mo, t0=t0: e.tensor_mul(
                            as_sb[:, 256 * mo:256 * (mo + 1)],
                            acc_sb[:, T * mo + t0:T * mo + t0 + 256], s_bc[:, :]))
                    em.step("scalar", lambda e: e.activation(out=sq_sb[:, :], in_=as_sb[:, :], func=AF.Square))
                    mm_ones_sums(em, p_st1, onesc, [as_sb[:, 256 * k:256 * (k + 1)] for k in range(8)])
                    mm_ones_sums(em, p_st2, onesc, [sq_sb[:, 256 * k:256 * (k + 1)] for k in range(8)])
                    mu = rows[0:1, 0:256]
                    var = rows[0:1, 256:512]
                    rstd = rows[0:1, 512:768]
                    scr = rows[0:1, 768:1024]
                    em.step("vector", lambda e: e.tensor_scalar_mul(mu, p_st1[:, :], 1.0 / D))
                    em.step("vector", lambda e: e.tensor_scalar_mul(var, p_st2[:, :], 1.0 / D))
                    em.step("scalar", lambda e: e.activation(out=scr, in_=mu, func=AF.Square))
                    em.step("vector", lambda e: e.tensor_sub(var, var, scr))
                    rstd_from_var(em, rstd, var, 1e-5)
                    bcast_row(em, p_bc[:, 0:256], onesr, mu)
                    em.step("vector", lambda e: e.tensor_copy(m_bc[:, 0:256], p_bc[:, 0:256]))
                    bcast_row(em, p_bc[:, 0:256], onesr, rstd)
                    for mo in range(8):
                        def post1(e, mo=mo, t0=t0):
                            e.tensor_sub(tstage[:, 0:256], as_sb[:, 256 * mo:256 * (mo + 1)], m_bc[:, 0:256])
                            e.tensor_mul(tstage[:, 0:256], tstage[:, 0:256], p_bc[:, 0:256])
                            return e.tensor_scalar(tstage[:, 0:256], tstage[:, 0:256], gam[:, mo:mo + 1],
                                                   bet[:, mo:mo + 1], ALU.mult, ALU.add)
                        em.step("vector", post1)
                        em.step("scalar", lambda e: e.activation(out=gstage[:, :], in_=tstage[:, 0:256], func=AF.Silu))
                        def fin(e, mo=mo, t0=t0):
                            rs = [e.dma_start(out=x1_sb[:, :], in_=i_x1[128 * mo:128 * (mo + 1), t0:t0 + 256]),
                                  e.dma_start(out=s1_sb[:, :], in_=i_s1[mo, :, t0:t0 + 256])]
                            return rs
                        em.step("sync", fin, dma=True, n_dma=2)
                        def resid(e, mo=mo):
                            e.tensor_mul(tstage[:, 0:256], s1_sb[:, :], gstage[:, :])
                            return e.tensor_add(tstage[:, 0:256], tstage[:, 0:256], x1_sb[:, :])
                        em.step("vector", resid)
                        em.step("sync", lambda e, mo=mo, t0=t0: e.dma_start(
                            out=o_x2[128 * mo:128 * (mo + 1), t0:t0 + 256], in_=tstage[:, 0:256]), dma=True)



def emit_s7(em, io):
    nc = em.nc
    with em.arena():
        i_xn = io["xn_bf"]
        i_wq, i_wk, i_wv = io["WqT"], io["WkT"], io["WvT"]
        i_cw = io["cwc"]
        i_cb = io["cbc"]
        identb = io["identb"]
        o_q = io["q_fm"]
        o_kv = io["kv_part"]
        o_kst = io["kst_part"]
        o_loc = io["local"]

        wq_sb = em.sb("wq", [128, 8 * 1024], BF16)
        wk_sb = em.sb("wk", [128, 8 * 1024], BF16)
        wv_sb = em.sb("wv", [128, 8 * 1024], BF16)
        xn_sb = em.sb("xn7", [128, 8 * 514], BF16)
        q_sb = em.sb("q7", [128, 8 * 512], BF16)
        k_sb = em.sb("k7", [128, 8 * 512], BF16)
        v_sb = em.sb("v7", [128, 8 * 512], BF16)
        ktm_sb = em.sb("ktm", [128, 4 * 1024], BF16)
        vtm_sb = em.sb("vtm", [128, 4 * 1024], BF16)
        ks_acc = em.sb("ksacc", [128, 8])
        kred = em.sb("kred", [128, 8])
        kv_sb = em.sb("kvsb", [128, 512])
        cw_sb = em.sb("cw7", [128, 24])
        cb_sb = em.sb("cb7", [128, 8])
        sid = em.sb("sid7", [128, 128], BF16)
        ta = em.sb("t7a", [128, 4 * 512])
        tb = em.sb("t7b", [128, 4 * 512])
        lstage = em.sb("lst", [128, 8 * 512], BF16)

        p_mm = em.ps("pmm7", [128, 4 * 512])
        p_kv = em.ps("pkv", [128, 512])
        p_trb = em.ps("ptr7", [128, 8 * 128], BF16)

        em.step("vector", lambda e: e.memset(ks_acc[:, :], 0.0))
        em.step("vector", lambda e: e.memset(xn_sb[:, :], 0.0))
        em.step("sync", lambda e: e.dma_start(out=sid[:, :], in_=identb), dma=True)
        em.step("sync", lambda e: e.dma_start(out=cw_sb[:, 0:24], in_=i_cw), dma=True)
        em.step("sync", lambda e: e.dma_start(out=cb_sb[:, :], in_=i_cb), dma=True)
        def ldw(e):
            rs = []
            for k in range(8):
                rs.append(e.dma_start(out=wq_sb[:, 1024 * k:1024 * (k + 1)], in_=i_wq[128 * k:128 * (k + 1), :]))
                rs.append(e.dma_start(out=wk_sb[:, 1024 * k:1024 * (k + 1)], in_=i_wk[128 * k:128 * (k + 1), :]))
                rs.append(e.dma_start(out=wv_sb[:, 1024 * k:1024 * (k + 1)], in_=i_wv[128 * k:128 * (k + 1), :]))
            return rs
        em.step("gpsimd", ldw, dma=True, n_dma=24)

        for it in range(4):
            t0 = it * 512
            lo = max(t0 - 1, 0)
            hi = min(t0 + 513, T)
            off = lo - (t0 - 1)
            def ldxn(e, t0=t0, lo=lo, hi=hi, off=off):
                return [e.dma_start(out=xn_sb[:, 514 * k + off:514 * k + off + (hi - lo)],
                                    in_=i_xn[k, :, lo:hi]) for k in range(8)]
            em.step("sync", ldxn, dma=True, n_dma=8)
            def conv(e):
                r = None
                for m in range(8):
                    base = 514 * m
                    e.tensor_scalar_mul(ta[:, 512 * (m % 4):512 * (m % 4 + 1)] if False else tb[:, 0:512],
                                        xn_sb[:, base:base + 512], cw_sb[:, 3 * m:3 * m + 1]) if False else None
                for m in range(8):
                    base = 514 * m
                    e.tensor_scalar_mul(ta[:, 0:512], xn_sb[:, base:base + 512], cw_sb[:, 3 * m:3 * m + 1])
                    e.scalar_tensor_tensor(ta[:, 0:512], xn_sb[:, base + 1:base + 513],
                                           cw_sb[:, 3 * m + 1:3 * m + 2], ta[:, 0:512], ALU.mult, ALU.add)
                    e.scalar_tensor_tensor(ta[:, 0:512], xn_sb[:, base + 2:base + 514],
                                           cw_sb[:, 3 * m + 2:3 * m + 3], ta[:, 0:512], ALU.mult, ALU.add)
                    r = e.tensor_scalar_add(lstage[:, 512 * m:512 * (m + 1)], ta[:, 0:512], cb_sb[:, m:m + 1])
                return r
            em.step("vector", conv)
            def stloc(e, t0=t0):
                return [e.dma_start(out=o_loc[m, :, t0:t0 + 512], in_=lstage[:, 512 * m:512 * (m + 1)])
                        for m in range(8)]
            em.step("sync", stloc, dma=True, n_dma=8)
            for which, wsb, dst in (("q", wq_sb, q_sb), ("k", wk_sb, k_sb), ("v", wv_sb, v_sb)):
                for mb in range(0, 8, 4):
                    def mmp(e, wsb=wsb, mb=mb):
                        r = None
                        for mi in range(4):
                            m = mb + mi
                            for k in range(8):
                                r = e.matmul(p_mm[:, 512 * mi:512 * (mi + 1)],
                                             wsb[:, 1024 * k + 128 * m:1024 * k + 128 * (m + 1)],
                                             xn_sb[:, 514 * k + 1:514 * k + 513], start=(k == 0), stop=(k == 7))
                        return r
                    em.step("tensor", mmp)
                    if which == "v":
                        def evv(e, mb=mb, dst=dst):
                            r = None
                            for mi in range(4):
                                r = e.tensor_copy(dst[:, 512 * (mb + mi):512 * (mb + mi + 1)],
                                                  p_mm[:, 512 * mi:512 * (mi + 1)])
                            return r
                        em.step("vector", evv)
                    else:
                        def ev1(e):
                            r = None
                            for mi in range(4):
                                e.tensor_scalar_min(ta[:, 512 * mi:512 * (mi + 1)], p_mm[:, 512 * mi:512 * (mi + 1)], 0.0)
                                r = e.tensor_relu(tb[:, 512 * mi:512 * (mi + 1)], p_mm[:, 512 * mi:512 * (mi + 1)])
                            return r
                        em.step("vector", ev1)
                        def ev2(e):
                            r = None
                            for mi in range(4):
                                r = e.activation(out=ta[:, 512 * mi:512 * (mi + 1)],
                                                 in_=ta[:, 512 * mi:512 * (mi + 1)], func=AF.Exp)
                            return r
                        em.step("scalar", ev2)
                        def ev3(e, mb=mb, dst=dst):
                            r = None
                            for mi in range(4):
                                r = e.tensor_add(dst[:, 512 * (mb + mi):512 * (mb + mi + 1)],
                                                 ta[:, 512 * mi:512 * (mi + 1)], tb[:, 512 * mi:512 * (mi + 1)])
                            return r
                        em.step("vector", ev3)
            def stq(e, t0=t0):
                return [e.dma_start(out=o_q[m, :, t0:t0 + 512], in_=q_sb[:, 512 * m:512 * (m + 1)])
                        for m in range(8)]
            em.step("sync", stq, dma=True, n_dma=8)
            def kst(e, it=it):
                r = None
                for m in range(8):
                    e.reduce_sum(kred[:, m:m + 1], k_sb[:, 512 * m:512 * (m + 1)], axis=AX.X)
                for m in range(8):
                    r = e.tensor_add(ks_acc[:, m:m + 1], ks_acc[:, m:m + 1], kred[:, m:m + 1])
                return r
            em.step("vector", kst)
            for srcb, dstb in ((k_sb, ktm_sb), (v_sb, vtm_sb)):
                for half in range(4):
                    idx = [(m, jb) for m in range(8) for jb in range(4)][8 * half:8 * (half + 1)]
                    def trx(e, idx=idx, srcb=srcb):
                        r = None
                        for oi, (m, jb) in enumerate(idx):
                            r = e.transpose(p_trb[:, 128 * oi:128 * (oi + 1)],
                                            srcb[:, 512 * m + 128 * jb:512 * m + 128 * (jb + 1)], sid[:, :])
                        return r
                    em.step("tensor", trx)
                    def evt(e, idx=idx, dstb=dstb):
                        r = None
                        for oi, (m, jb) in enumerate(idx):
                            r = e.tensor_copy(dstb[:, 1024 * jb + 128 * m:1024 * jb + 128 * (m + 1)],
                                              p_trb[:, 128 * oi:128 * (oi + 1)])
                        return r
                    em.step("vector", evt)
            def mmkv(e, it=it):
                r = None
                for h in range(16):
                    pb = (h % 2) * 64
                    cb_ = 64 * (h // 2)
                    for jb in range(4):
                        r = e.matmul(p_kv[pb:pb + 64, cb_:cb_ + 64],
                                     ktm_sb[:, 1024 * jb + 64 * h:1024 * jb + 64 * (h + 1)],
                                     vtm_sb[:, 1024 * jb + 64 * h:1024 * jb + 64 * (h + 1)],
                                     start=(jb == 0), stop=(jb == 3))
                return r
            em.step("tensor", mmkv)
            if it == 0:
                em.step("vector", lambda e: e.tensor_copy(kv_sb[:, :], p_kv[:, :]))
            else:
                em.step("vector", lambda e: e.tensor_add(kv_sb[:, :], kv_sb[:, :], p_kv[:, :]))

        em.step("sync", lambda e: e.dma_start(out=o_kv, in_=kv_sb[:, :]), dma=True)
        em.step("sync", lambda e: e.dma_start(out=o_kst, in_=ks_acc[:, :]), dma=True)


def emit_s9(em, io):
    nc = em.nc
    with em.arena():
        i_q = io["q_fm"]
        i_kvm = io["kv_mine"]
        i_kvp = io["kv_peer"]
        i_ksm = io["kst_mine"]
        i_ksp = io["kst_peer"]
        i_loc = io["local"]
        i_halo = io.get("halo_cols")
        i_cw = io["cwc"]
        i_wp = io["WpT"]
        i_bp = io["bpc"]
        o_s1 = io["s1"]

        wp_sb = em.sb("wp", [128, 8 * 1024], BF16)
        q_sb = em.sb("q9", [128, 8 * 512], BF16)
        kv_sb = em.sb("kv9", [128, 512])
        kv_bf = em.sb("kvbf", [128, 512], BF16)
        ks_sb = em.sb("ks9", [128, 8])
        ksp_sb = em.sb("ksp9", [128, 8])
        ksd = em.sb("ksd", [128, 8 * 16], BF16)
        halo_sb = em.sb("halo", [128, 8 * 2], BF16)
        cw_sb = em.sb("cw9", [128, 24])
        loc_sb = em.sb("loc9", [128, 8 * 512], BF16)
        r_sb = em.sb("r9", [16, 512])
        r_flat = em.sb("rflat", [1, 8192])
        u_sb = em.sb("u9", [128, 512])
        att_sb = em.sb("att9", [128, 8 * 512], BF16)
        bp_sb = em.sb("bp9", [128, 8])
        onesr = em.sb("onesr9", [1, 128])
        sstage = em.sb("sst9", [128, 512], BF16)

        p_den = em.ps("pden", [16, 512])
        p_at = em.ps("pat", [128, 512])
        p_rb = em.ps("prb9", [128, 512])
        p_s1 = em.ps("ps19", [128, 512])

        em.step("vector", lambda e: e.memset(onesr[:, :], 1.0))
        em.step("vector", lambda e: e.memset(ksd[:, :], 0.0))
        em.step("sync", lambda e: e.dma_start(out=kv_sb[:, :], in_=i_kvm), dma=True)
        em.step("sync", lambda e: e.dma_start(out=u_sb[:, :], in_=i_kvp), dma=True)
        em.step("vector", lambda e: e.tensor_add(kv_sb[:, :], kv_sb[:, :], u_sb[:, :]))
        em.step("vector", lambda e: e.tensor_copy(kv_bf[:, :], kv_sb[:, :]))
        em.step("sync", lambda e: e.dma_start(out=ks_sb[:, :], in_=i_ksm), dma=True)
        em.step("sync", lambda e: e.dma_start(out=ksp_sb[:, :], in_=i_ksp), dma=True)
        em.step("vector", lambda e: e.tensor_add(ks_sb[:, :], ks_sb[:, :], ksp_sb[:, :]))
        if "halo_raw" in io:
            raw_sb = em.sb("haloraw", [128, 16], BF16)
            fo_sb = em.sb("fodd", [128, 1])
            fe_sb = em.sb("feven", [128, 1])
            em.step("sync", lambda e: e.dma_start(out=raw_sb[:, :], in_=io["halo_raw"]), dma=True)
            em.step("sync", lambda e: e.dma_start(out=fo_sb[:, :], in_=io["fodd"]), dma=True)
            em.step("sync", lambda e: e.dma_start(out=fe_sb[:, :], in_=io["feven"]), dma=True)
            def mkhalo(e):
                r = None
                for m in range(8):
                    e.tensor_scalar_mul(halo_sb[:, 2 * m:2 * m + 1], raw_sb[:, 2 * m + 1:2 * m + 2], fo_sb[:, 0:1])
                    r = e.tensor_scalar_mul(halo_sb[:, 2 * m + 1:2 * m + 2], raw_sb[:, 2 * m:2 * m + 1], fe_sb[:, 0:1])
                return r
            em.step("vector", mkhalo)
        else:
            em.step("sync", lambda e: e.dma_start(out=halo_sb[:, 0:16].rearrange("b (a c) -> b a c", a=8), in_=i_halo.rearrange("a b c -> b a c")), dma=True)
        em.step("sync", lambda e: e.dma_start(out=cw_sb[:, 0:24], in_=i_cw), dma=True)
        em.step("sync", lambda e: e.dma_start(out=bp_sb[:, :], in_=i_bp), dma=True)
        for k in range(8):
            em.step("gpsimd", lambda e, k=k: e.dma_start(
                out=wp_sb[:, 1024 * k:1024 * (k + 1)], in_=i_wp[128 * k:128 * (k + 1), :]), dma=True)
        def mkksd(e):
            r = None
            for kb in range(8):
                e.tensor_copy(ksd[0:64, kb * 16 + 2 * kb:kb * 16 + 2 * kb + 1], ks_sb[0:64, kb:kb + 1])
                r = e.tensor_copy(ksd[64:128, kb * 16 + 2 * kb + 1:kb * 16 + 2 * kb + 2], ks_sb[64:128, kb:kb + 1])
            return r
        em.step("vector", mkksd)

        for it in range(4):
            t0 = it * 512
            def ldq(e, t0=t0):
                return [e.dma_start(out=q_sb[:, 512 * m:512 * (m + 1)], in_=i_q[m, :, t0:t0 + 512])
                        for m in range(8)]
            em.step("sync", ldq, dma=True, n_dma=8)
            def ldloc(e, t0=t0):
                return [e.dma_start(out=loc_sb[:, 512 * m:512 * (m + 1)], in_=i_loc[m, :, t0:t0 + 512])
                        for m in range(8)]
            em.step("sync", ldloc, dma=True, n_dma=8)
            if it == 0:
                def fixl(e):
                    r = None
                    for m in range(8):
                        r = e.scalar_tensor_tensor(loc_sb[:, 512 * m:512 * m + 1],
                                                   halo_sb[:, 2 * m:2 * m + 1], cw_sb[:, 3 * m:3 * m + 1],
                                                   loc_sb[:, 512 * m:512 * m + 1], ALU.mult, ALU.add)
                    return r
                em.step("vector", fixl)
            if it == 3:
                def fixr(e):
                    r = None
                    for m in range(8):
                        r = e.scalar_tensor_tensor(loc_sb[:, 512 * m + 511:512 * m + 512],
                                                   halo_sb[:, 2 * m + 1:2 * m + 2], cw_sb[:, 3 * m + 2:3 * m + 3],
                                                   loc_sb[:, 512 * m + 511:512 * m + 512], ALU.mult, ALU.add)
                    return r
                em.step("vector", fixr)
            def mmden(e):
                r = None
                for kb in range(8):
                    r = e.matmul(p_den[:, :], ksd[:, 16 * kb:16 * (kb + 1)],
                                 q_sb[:, 512 * kb:512 * (kb + 1)], start=(kb == 0), stop=(kb == 7))
                return r
            em.step("tensor", mmden)
            em.step("vector", lambda e: e.tensor_scalar_max(r_sb[:, :], p_den[:, :], 1e-6))
            em.step("vector", lambda e: e.reciprocal(r_sb[:, :], r_sb[:, :]))
            em.step("sync", lambda e: e.dma_start(
                out=r_flat[0:1, :].rearrange("p (a b) -> p a b", a=16), in_=r_sb[:, :]), dma=True)
            for kb in range(8):
                def mmat(e, kb=kb):
                    h1, h2 = 2 * kb, 2 * kb + 1
                    e.matmul(p_at[0:64, :], kv_bf[0:64, 64 * kb:64 * (kb + 1)],
                             q_sb[0:64, 512 * kb:512 * (kb + 1)], start=True, stop=True)
                    return e.matmul(p_at[64:128, :], kv_bf[64:128, 64 * kb:64 * (kb + 1)],
                                    q_sb[64:128, 512 * kb:512 * (kb + 1)], start=True, stop=True)
                em.step("tensor", mmat)
                def mmrb(e, kb=kb):
                    e.matmul(p_rb[0:64, :], onesr[0:1, 0:64], r_flat[0:1, 512 * (2 * kb):512 * (2 * kb) + 512],
                             start=True, stop=True)
                    return e.matmul(p_rb[64:128, :], onesr[0:1, 0:64], r_flat[0:1, 512 * (2 * kb + 1):512 * (2 * kb + 1) + 512],
                                    start=True, stop=True)
                em.step("tensor", mmrb)
                em.step("vector", lambda e: e.tensor_copy(u_sb[:, :], p_at[:, :]))
                em.step("vector", lambda e: e.tensor_mul(u_sb[:, :], u_sb[:, :], p_rb[:, :]))
                em.step("vector", lambda e, kb=kb: e.tensor_add(
                    att_sb[:, 512 * kb:512 * (kb + 1)], u_sb[:, :], loc_sb[:, 512 * kb:512 * (kb + 1)]))
            for m in range(8):
                def mms1(e, m=m):
                    r = None
                    for k in range(8):
                        r = e.matmul(p_s1[:, :], wp_sb[:, 1024 * k + 128 * m:1024 * k + 128 * (m + 1)],
                                     att_sb[:, 512 * k:512 * (k + 1)], start=(k == 0), stop=(k == 7))
                    return r
                em.step("tensor", mms1)
                em.step("vector", lambda e, m=m: e.tensor_scalar_add(sstage[:, :], p_s1[:, :], bp_sb[:, m:m + 1]))
                em.step("sync", lambda e, m=m, t0=t0: e.dma_start(
                    out=o_s1[m, :, t0:t0 + 512], in_=sstage[:, :]), dma=True)




MAGIC1, MAGIC2 = 0xA5A50001, 0xA5A50002


_RVC = [0]


def _rv(e, tile, col, mx):
    from concourse.expressions import RuntimeValue
    _RVC[0] += 1
    r = e.alloc_register("xrv%d" % _RVC[0])
    e.reg_load(r, tile[0:1, col:col + 1])
    return RuntimeValue(val=r, min_val=0, max_val=mx)


def emit_exchange(em, io, phase, part="all"):
    nc = em.nc
    if not hasattr(em, "_xch"):
        em._xch = {}
        em._xb = {"fo": em.sb("xfo", [128, 1]), "fe": em.sb("xfe", [128, 1])}
        em.step("sync", lambda e: e.dma_start(out=em._xb["fo"][:, :], in_=io["fodd"]), dma=True)
        em.step("sync", lambda e: e.dma_start(out=em._xb["fe"][:, :], in_=io["feven"]), dma=True)
    st = em._xch.get(phase)
    if st is None:
        st = {}
        st["psem"] = em.sem("psem")
        st["par_sb"] = em.sb("par", [1, 8], U32)
        st["mg"] = em.sb("mg", [1, 1], U32)
        st["flg"] = em.sb("flg", [1, 4], U32)
        if phase == 2:
            st["hb"] = em.sb("xhalo", [128, 16], BF16)
            st["hf"] = em.sb("xhf", [128, 16])
        em._xch[phase] = st
        xpar = io["xpar"]
        em.step("sync", lambda e: e.dma_start(out=st["par_sb"][:, :], in_=xpar), dma=True)
        em.step("vector", lambda e: e.memset(st["mg"][:, :], MAGIC1 if phase == 1 else MAGIC2))
    shf = io["shflags"]
    psem = st["psem"]
    par_sb = st["par_sb"]
    mg = st["mg"]
    flg = st["flg"]
    fo, fe = em._xb["fo"], em._xb["fe"]
    magic = MAGIC1 if phase == 1 else MAGIC2
    fcols = (0, 1) if phase == 1 else (2, 3)
    fcol_par = 2 if phase == 1 else 4

    def xpoll(e, ch, need, gch, gneed):
        e.wait_ge(ch, need)
        e.wait_ge(gch, gneed)
        tag = f"{phase}p"
        with e.register("nd" + tag) as nd, e.register("pc" + tag) as pc, \
                e.register("fl" + tag) as fl:
            e.reg_mov(pc, 0)
            for fc in fcols:
                e.reg_mov(nd, 1)
                with e.While(nd):
                    e.reg_add(pc, pc, 16)
                    e.dma_start(out=flg[:, :], in_=shf[0:1, 0:4]).then_inc(psem, 16)
                    e.wait_ge(psem, pc)
                    e.reg_load(fl, flg[0:1, fc:fc + 1])
                    e.reg_alu(nd, fl, magic, ALU.not_equal)
        return e.sem_inc(ch, 1)

    def selchunk(e, cw):
        e.tensor_scalar(xa[:, 0:cw], xa[:, 0:cw], fo[:, 0:1], None, ALU.mult)
        e.tensor_scalar(xb[:, 0:cw], xb[:, 0:cw], fe[:, 0:1], None, ALU.mult)
        return e.tensor_add(xa[:, 0:cw], xa[:, 0:cw], xb[:, 0:cw])

    if phase == 1:
        if part in ("all", "send"):
            def xsend(e, ch, need, gch, gneed):
                e.wait_ge(ch, need)
                e.wait_ge(gch, gneed)
                myrow = _rv(e, par_sb, 0, 128)
                myf = _rv(e, par_sb, fcol_par, 3)
                e.dma_start(out=io["sh_sout"][DS(myrow, 128), :], in_=io["S_out"]).then_inc(psem, 16)
                e.wait_ge(psem, 16)
                e.dma_start(out=shf[0:1, DS(myf, 1)], in_=mg[:, :]).then_inc(psem, 16)
                e.wait_ge(psem, 32)
                return e.sem_inc(ch, 1)
            em.step_manual("gpsimd", xsend, n_inc=1)
        if part in ("all", "recv"):
            em.step_manual("gpsimd", xpoll, n_inc=1)
            shs = io["sh_sout"]
            with em.arena():
                xa = em.sb("xba", [128, 512])
                xb = em.sb("xbb", [128, 512])
                _emit_sel1(em, io, shs, xa, xb, fo, fe)
    else:
        if part in ("all", "send"):
            hb = st["hb"]
            hf = st["hf"]
            i_xn1 = io["xn1_bf"]
            def ldedge(e):
                rs = []
                for m in range(8):
                    rs.append(e.dma_start(out=hb[:, 2 * m:2 * m + 1], in_=i_xn1[m, :, 0:1],
                                          allow_slow_non_contiguous=True))
                    rs.append(e.dma_start(out=hb[:, 2 * m + 1:2 * m + 2], in_=i_xn1[m, :, T - 1:T],
                                          allow_slow_non_contiguous=True))
                return rs
            em.step("sync", ldedge, dma=True, n_dma=16)
            em.step("vector", lambda e: e.tensor_copy(hf[:, :], hb[:, :]))
            em.step("sync", lambda e: e.dma_start(out=io["pay2"][0:128, 520:536], in_=hf[:, :]), dma=True)

            def xsend(e, ch, need, gch, gneed):
                e.wait_ge(ch, need)
                e.wait_ge(gch, gneed)
                myrow = _rv(e, par_sb, 0, 128)
                myf = _rv(e, par_sb, fcol_par, 3)
                e.dma_start(out=io["sh_x2"][DS(myrow, 128), :], in_=io["pay2"]).then_inc(psem, 16)
                e.wait_ge(psem, 16)
                e.dma_start(out=shf[0:1, DS(myf, 1)], in_=mg[:, :]).then_inc(psem, 16)
                e.wait_ge(psem, 32)
                return e.sem_inc(ch, 1)
            em.step_manual("gpsimd", xsend, n_inc=1)
        if part in ("all", "recv"):
            em.step_manual("gpsimd", xpoll, n_inc=1)
            shx = io["sh_x2"]
            with em.arena():
                xa = em.sb("xba2", [128, 512])
                xb = em.sb("xbb2", [128, 512])
                h16 = em.sb("xh16", [128, 16], BF16)
                _emit_sel2(em, io, shx, xa, xb, h16, fo, fe)


def _emit_sel1(em, io, shs, xa, xb, fo, fe):
    def selchunk(e, cw):
        e.tensor_scalar(xa[:, 0:cw], xa[:, 0:cw], fo[:, 0:1], None, ALU.mult)
        e.tensor_scalar(xb[:, 0:cw], xb[:, 0:cw], fe[:, 0:1], None, ALU.mult)
        return e.tensor_add(xa[:, 0:cw], xa[:, 0:cw], xb[:, 0:cw])
    if True:
        if True:
            for cc in range(4):
                c0 = 512 * cc
                def ld(e, c0=c0, shs=shs):
                    return [e.dma_start(out=xa[:, 0:512], in_=shs[0:128, c0:c0 + 512]),
                            e.dma_start(out=xb[:, 0:512], in_=shs[128:256, c0:c0 + 512])]
                em.step("sync", ld, dma=True, n_dma=2)
                em.step("vector", lambda e: selchunk(e, 512))
                em.step("sync", lambda e, c0=c0: e.dma_start(
                    out=io["S_in"][0:128, c0:c0 + 512], in_=xa[:, 0:512]), dma=True)


def _emit_sel2(em, io, shx, xa, xb, h16, fo, fe):
    def selchunk(e, cw):
        e.tensor_scalar(xa[:, 0:cw], xa[:, 0:cw], fo[:, 0:1], None, ALU.mult)
        e.tensor_scalar(xb[:, 0:cw], xb[:, 0:cw], fe[:, 0:1], None, ALU.mult)
        return e.tensor_add(xa[:, 0:cw], xa[:, 0:cw], xb[:, 0:cw])
    if True:
        if True:
            def ld0(e, shx=shx):
                return [e.dma_start(out=xa[:, 0:512], in_=shx[0:128, 0:512]),
                        e.dma_start(out=xb[:, 0:512], in_=shx[128:256, 0:512])]
            em.step("sync", ld0, dma=True, n_dma=2)
            em.step("vector", lambda e: selchunk(e, 512))
            em.step("sync", lambda e: e.dma_start(
                out=io["pay2_peer"][0:128, 0:512], in_=xa[:, 0:512]), dma=True)
            def ld1(e, shx=shx):
                return [e.dma_start(out=xa[:, 0:24], in_=shx[0:128, 512:536]),
                        e.dma_start(out=xb[:, 0:24], in_=shx[128:256, 512:536])]
            em.step("sync", ld1, dma=True, n_dma=2)
            em.step("vector", lambda e: selchunk(e, 24))
            em.step("sync", lambda e: e.dma_start(
                out=io["pay2_peer"][0:128, 512:520], in_=xa[:, 0:8]), dma=True)
            em.step("vector", lambda e: e.tensor_copy(h16[:, :], xa[:, 8:24]))
            em.step("sync", lambda e: e.dma_start(
                out=io["halo_raw"], in_=h16[:, :]), dma=True)


def build_fused(nc, A):
    em = Emit(nc)
    with em.ctx:
        em.init_chain()
        io1 = {"x_fm": A["x0_fm"], "identf": A["identf"], "xn_bf": A["xn0_bf"],
               "masks": A["masks0"], "scores": A["scores0"], "wg_redT": A["wg_redT"],
               "wg": A["wg"], "xh_fm": A["x0h_fm"], "xnh_bf": A["xnh_bf"],
               "identb": A["identb"], "capoff": A["capoff"],
               "xn_tm": A["xn_tm"], "idxf": A["idxf"]}
        with em.arena():
            emit_s1(em, io1, tag_l1=False)
        em.new_chain()
        io2 = {"xn_bf": A["xn0_bf"], "xnh_bf": A["xnh_bf"], "W_in": A["W_in"],
               "convw": A["convw"], "convb": A["convb"], "dtbias": A["dtbias"],
               "alog": A["alog"], "identb": A["identb"], "identf": A["identf"],
               "zs": A["zs"], "xfm": A["xfm"], "xt_tm": A["xt_tm"], "baseT": A["baseT"],
               "C_fm": A["C_fm"], "dAcs": A["dAcs"], "dAcs_tm": A["dAcs_tm"],
               "alast": A["alast"], "states": A["states"], "S_out": A["S_out"]}
        with em.arena():
            emit_s2(em, io2)
        em.new_chain()
        xio1 = {"xpar": A["xpar"], "shflags": A["shflags"],
                "S_out": A["S_out"], "S_in": A["S_in"], "sh_sout": A["sh_sout"],
                "fodd": A["fodd"], "feven": A["feven"]}
        emit_exchange(em, xio1, 1, part="send")
        em.new_chain()
        io5 = {"xn_bf": A["xn0_bf"], "masks": A["masks0"], "scores": A["scores0"],
               "w1": A["l0w1"], "w2": A["l0w2"], "b1c": A["b1c"], "b2cc": A["b2cc"], "b2": A["b2"],
               "xn_tm": A["xn_tm"], "idxf": A["idxf"], "iota512": A["iota512"],
               "gout_tm": A["gout"], "identb": A["identb"], "identf": A["identf"],
               "g": A["g0"]}
        with em.arena():
            emit_moe(em, io5, HID=4096, has_bias=True, post="l0", HSPLIT=4, mode="top1")
        em.new_chain()
        emit_exchange(em, xio1, 1, part="recv")
        em.new_chain()
        io4 = {"xt_tm": A["xt_tm"], "baseT": A["baseT"], "C_fm": A["C_fm"],
               "dAcs": A["dAcs"], "dAcs_tm": A["dAcs_tm"], "alast": A["alast"],
               "states": A["states"], "S_in": A["S_in"], "zs": A["zs"], "xfm": A["xfm"],
               "W_out": A["W_out"], "normw": A["normw"], "d_bc": A["d_bc"],
               "trilT": A["trilT"], "s0": A["s0"], "singate": A["singate"]}
        with em.arena():
            emit_s4(em, io4)
        em.new_chain()
        io6 = {"x_fm": A["x0_fm"], "identf": A["identf"], "s_bf": A["s0"],
               "g_bf": A["g0"], "x1_fm": A["x1_fm"], "gateT": A["gateT"],
               "xn_bf": A["xn1_bf"], "masks": A["masks1"], "scores": A["scores1"],
               "identb": A["identb"], "capoff": A["capoff"],
               "xn_tm": A["xn_tm"], "idxf": A["idxf"]}
        with em.arena():
            emit_s1(em, io6, tag_l1=True)
        em.new_chain()
        io7 = {"xn_bf": A["xn1_bf"], "WqT": A["WqT"], "WkT": A["WkT"], "WvT": A["WvT"],
               "cwc": A["cwc"], "cbc": A["cbc"], "identb": A["identb"],
               "q_fm": A["q_fm"], "kv_part": A["pay2"][:, 0:512],
               "kst_part": A["pay2"][:, 512:520], "local": A["local"]}
        with em.arena():
            emit_s7(em, io7)
        em.new_chain()
        xio2 = {"xpar": A["xpar"], "shflags": A["shflags"],
                "xn1_bf": A["xn1_bf"], "pay2": A["pay2"],
                "pay2_peer": A["pay2_peer"], "sh_x2": A["sh_x2"],
                "halo_raw": A["halo_raw"],
                "fodd": A["fodd"], "feven": A["feven"]}
        emit_exchange(em, xio2, 2, part="send")
        em.new_chain()
        io10 = {"xn_bf": A["xn1_bf"], "masks": A["masks1"], "scores": A["scores1"],
                "w1": A["l1w1"], "w2": A["l1w2"], "x1_fm": A["x1_fm"], "s1": A["s1"],
                "gamma": A["gamma"], "beta": A["beta"], "x2_fm": A["x2_fm"],
                "xn_tm": A["xn_tm"], "idxf": A["idxf"], "iota512": A["iota512"],
                "gout_tm": A["gout"], "identb": A["identb"], "identf": A["identf"]}
        io9 = {"q_fm": A["q_fm"], "kv_mine": A["pay2"][:, 0:512],
               "kv_peer": A["pay2_peer"][:, 0:512],
               "kst_mine": A["pay2"][:, 512:520],
               "kst_peer": A["pay2_peer"][:, 512:520], "local": A["local"],
               "halo_raw": A["halo_raw"], "fodd": A["fodd"], "feven": A["feven"],
               "cwc": A["cwc"], "WpT": A["WpT"], "bpc": A["bpc"], "s1": A["s1"]}
        with em.arena():
            emit_moe(em, io10, HID=2048, has_bias=False, post="l1", HSPLIT=2, part="experts", mode="top1")
            em.new_chain()
            emit_exchange(em, xio2, 2, part="recv")
            em.new_chain()
            emit_s9(em, io9)
            em.new_chain()
            emit_moe(em, io10, HID=2048, has_bias=False, post="l1", HSPLIT=2, part="post", mode="top1")
        em.finale()
        with nc.Block() as block:
            em.emit(block)
    return nc


def _standalone(emitfn):
    def build(nc, io, **kw):
        em = Emit(nc)
        with em.ctx:
            em.init_chain()
            with em.arena():
                emitfn(em, io, **kw)
            em.finale()
            with nc.Block() as block:
                em.emit(block)
        return nc
    return build


build_s1 = _standalone(emit_s1)
build_s2 = _standalone(emit_s2)
build_s4 = _standalone(emit_s4)
build_moe = _standalone(emit_moe)
build_s7 = _standalone(emit_s7)
build_s9 = _standalone(emit_s9)


def _cols(v, nb):
    return np.ascontiguousarray(np.asarray(v, np.float32).reshape(nb, 128).T)


def _fm(x):
    return np.ascontiguousarray(np.asarray(x, np.float32).T)


def run_stage(build_fn, hbm_specs, per_core_inputs, out_names, n_cores=8):
    nc = bass.Bass("TRN2", target_bir_lowering=False)
    io = {}
    for (name, shape, dt, kind) in hbm_specs:
        if kind == "Internal":
            io[name] = nc.dram_tensor(name, list(shape), dt).ap()
        else:
            io[name] = nc.dram_tensor(name, list(shape), dt, kind=kind).ap()
    build_fn(nc, io)
    res = run_bass_kernel_spmd(nc, per_core_inputs, core_ids=list(range(n_cores)))
    return [{k: r[k] for k in out_names} for r in res.results]


IDENT_F32 = np.eye(128, dtype=np.float32)
IDENT_BF16 = np.eye(128).astype(ml_dtypes.bfloat16)


def stage1_run(x0, tag_l1=False, gate_w=None, wg_red=None, wg=None, halo=None):
    specs = [
        ("x_fm", [D, T], F32, "ExternalInput"),
        ("identf", [128, 128], F32, "ExternalInput"),
        ("xn_bf", [8, 128, T], BF16, "ExternalOutput"),
        ("masks", [8, T], F32, "ExternalOutput"),
        ("scores", [T // 128, 128], F32, "ExternalOutput"),
    ]
    if tag_l1:
        specs.append(("gateT", [D, 8], F32, "ExternalInput"))
    else:
        specs.append(("wg_redT", [D, 16], F32, "ExternalInput"))
        specs.append(("wg", [8, 16], F32, "ExternalInput"))
    if halo is not None:
        specs.append(("xh_fm", [D, 3], F32, "ExternalInput"))
        specs.append(("xnh_bf", [8, 128, 3], BF16, "ExternalOutput"))
    specs += [("identb", [128, 128], BF16, "ExternalInput"),
              ("capoff", [8, 1], F32, "ExternalInput"),
              ("xn_tm", [T, 1024], BF16, "ExternalOutput"),
              ("idxf", [1, T], F32, "ExternalOutput")]

    ins = []
    capoff = (np.arange(8, dtype=np.float32) * CAP - 1.0)[:, None]
    for c in range(8):
        d = {"x_fm": x0[c], "identf": IDENT_F32, "identb": IDENT_BF16, "capoff": capoff}
        if tag_l1:
            d["gateT"] = np.ascontiguousarray(gate_w.T)
        else:
            d["wg_redT"] = np.ascontiguousarray(wg_red.T)
            d["wg"] = np.ascontiguousarray(wg)
        if halo is not None:
            d["xh_fm"] = halo[c]
        ins.append(d)
    outs = ["xn_bf", "masks", "scores", "xn_tm", "idxf"] + (["xnh_bf"] if halo is not None else [])
    return run_stage(lambda nc, io: build_s1(nc, io, tag_l1=tag_l1), specs, ins, outs)




def stage4_run(s2_outs, s_in, d):
    specs = [
        ("xt_tm", [NCH, 256, DI], BF16, "ExternalInput"),
        ("baseT", [NCH, 256, 256], BF16, "ExternalInput"),
        ("C_fm", [128, T], BF16, "ExternalInput"),
        ("dAcs", [32, T], F32, "ExternalInput"),
        ("dAcs_tm", [NCH, 256, 32], F32, "ExternalInput"),
        ("alast", [32, NCH], F32, "ExternalInput"),
        ("states", [NCH, 128, DI], F32, "ExternalInput"),
        ("S_in", [128, DI], F32, "ExternalInput"),
        ("zs", [16, 128, T], BF16, "ExternalInput"),
        ("xfm", [16, 128, T], BF16, "ExternalInput"),
        ("W_out", [DI, D], F32, "ExternalInput"),
        ("normw", [128, 16], F32, "ExternalInput"),
        ("d_bc", [128, 32], F32, "ExternalInput"),
        ("trilT", [2, 128, 256], F32, "ExternalInput"),
        ("s0", [8, 128, T], BF16, "ExternalOutput"),
    ]
    tril = np.zeros((2, 128, 256), np.float32)
    for jb in range(2):
        for jj in range(128):
            tril[jb, jj, :jj + 128 * jb] = -1e30
    ins = []
    for c in range(8):
        o = s2_outs[c]
        ins.append({
            "xt_tm": o["xt_tm"], "baseT": o["baseT"], "C_fm": o["C_fm"],
            "dAcs": o["dAcs"], "dAcs_tm": o["dAcs_tm"], "alast": o["alast"],
            "states": o["states"], "S_in": s_in[c], "zs": o["zs"], "xfm": o["xfm"],
            "W_out": np.asarray(d["L0_attn_W_out"], np.float32),
            "normw": _cols(d["L0_attn_norm_w"], 16),
            "d_bc": np.ascontiguousarray(np.broadcast_to(
                np.asarray(d["L0_attn_D"], np.float32)[None, :], (128, 32))),
            "trilT": tril,
        })
    return run_stage(build_s4, specs, ins, ["s0"])


def stage2_run(xn_bf, xnh_bf, d):
    specs = [
        ("xn_bf", [8, 128, T], BF16, "ExternalInput"),
        ("xnh_bf", [8, 128, 3], BF16, "ExternalInput"),
        ("W_in", [D, DIP], F32, "ExternalInput"),
        ("convw", [128, 18 * 4], F32, "ExternalInput"),
        ("convb", [128, 18], F32, "ExternalInput"),
        ("dtbias", [32, 1], F32, "ExternalInput"),
        ("alog", [32, 1], F32, "ExternalInput"),
        ("identb", [128, 128], BF16, "ExternalInput"),
        ("identf", [128, 128], F32, "ExternalInput"),
        ("zs", [16, 128, T], BF16, "ExternalOutput"),
        ("xfm", [16, 128, T], BF16, "ExternalOutput"),
        ("xt_tm", [NCH, 256, DI], BF16, "ExternalOutput"),
        ("baseT", [NCH, 256, 256], BF16, "ExternalOutput"),
        ("C_fm", [128, T], BF16, "ExternalOutput"),
        ("dAcs", [32, T], F32, "ExternalOutput"),
        ("dAcs_tm", [NCH, 256, 32], F32, "ExternalOutput"),
        ("alast", [32, NCH], F32, "ExternalOutput"),
        ("states", [NCH, 128, DI], F32, "ExternalOutput"),
        ("S_out", [128, DI], F32, "ExternalOutput"),
    ]
    cw = np.ascontiguousarray(np.asarray(d["L0_attn_conv_w"], np.float32)
                              .reshape(18, 128, 4).transpose(1, 0, 2).reshape(128, 72))
    cb = _cols(d["L0_attn_conv_b"], 18)
    ins = []
    for c in range(8):
        ins.append({
            "xn_bf": xn_bf[c], "xnh_bf": xnh_bf[c],
            "W_in": np.asarray(d["L0_attn_W_in"], np.float32),
            "convw": cw, "convb": cb,
            "dtbias": np.asarray(d["L0_attn_dt_bias"], np.float32)[:, None],
            "alog": np.asarray(d["L0_attn_A_log"], np.float32)[:, None],
            "identb": IDENT_BF16, "identf": IDENT_F32,
        })
    outs = ["zs", "xfm", "xt_tm", "baseT", "C_fm", "dAcs", "dAcs_tm", "alast", "states", "S_out"]
    return run_stage(build_s2, specs, ins, outs)




def stage6_run(x0_fm, s0, g0, d):
    specs = [
        ("x_fm", [D, T], F32, "ExternalInput"),
        ("identf", [128, 128], F32, "ExternalInput"),
        ("s_bf", [8, 128, T], BF16, "ExternalInput"),
        ("g_bf", [8, 128, T], BF16, "ExternalInput"),
        ("gateT", [D, 8], F32, "ExternalInput"),
        ("xn_bf", [8, 128, T], BF16, "ExternalOutput"),
        ("masks", [8, T], F32, "ExternalOutput"),
        ("scores", [T // 128, 128], F32, "ExternalOutput"),
        ("x1_fm", [D, T], F32, "ExternalOutput"),
    ]
    ins = []
    for c in range(8):
        ins.append({"x_fm": x0_fm[c], "identf": IDENT_F32, "s_bf": s0[c], "g_bf": g0[c],
                    "gateT": np.ascontiguousarray(np.asarray(d["L1_ffn_gate_w"], np.float32).T)})
    return run_stage(lambda nc, io: build_s1(nc, io, tag_l1=True), specs, ins,
                     ["xn_bf", "masks", "scores", "x1_fm"])


def stage7_run(xn_bf, d):
    specs = [
        ("xn_bf", [8, 128, T], BF16, "ExternalInput"),
        ("WqT", [D, D], F32, "ExternalInput"),
        ("WkT", [D, D], F32, "ExternalInput"),
        ("WvT", [D, D], F32, "ExternalInput"),
        ("cwc", [128, 24], F32, "ExternalInput"),
        ("cbc", [128, 8], F32, "ExternalInput"),
        ("identb", [128, 128], BF16, "ExternalInput"),
        ("q_fm", [8, 128, T], BF16, "ExternalOutput"),
        ("kv_part", [128, 512], F32, "ExternalOutput"),
        ("kst_part", [128, 8], F32, "ExternalOutput"),
        ("local", [8, 128, T], BF16, "ExternalOutput"),
    ]
    cw = np.ascontiguousarray(np.asarray(d["L1_attn_conv_w"], np.float32)
                              .reshape(8, 128, 3).transpose(1, 0, 2).reshape(128, 24))
    ins = []
    for c in range(8):
        ins.append({"xn_bf": xn_bf[c],
                    "WqT": np.ascontiguousarray(np.asarray(d["L1_attn_Wq"], np.float32).T),
                    "WkT": np.ascontiguousarray(np.asarray(d["L1_attn_Wk"], np.float32).T),
                    "WvT": np.ascontiguousarray(np.asarray(d["L1_attn_Wv"], np.float32).T),
                    "cwc": cw, "cbc": _cols(d["L1_attn_conv_b"], 8),
                    "identb": IDENT_BF16})
    return run_stage(build_s7, specs, ins, ["q_fm", "kv_part", "kst_part", "local"])


def stage9_run(q_fm, kv, kst, local, halo, d):
    specs = [
        ("q_fm", [8, 128, T], BF16, "ExternalInput"),
        ("kv_mine", [128, 512], F32, "ExternalInput"),
        ("kv_peer", [128, 512], F32, "ExternalInput"),
        ("kst_mine", [128, 8], F32, "ExternalInput"),
        ("kst_peer", [128, 8], F32, "ExternalInput"),
        ("local", [8, 128, T], BF16, "ExternalInput"),
        ("halo_cols", [8, 128, 2], BF16, "ExternalInput"),
        ("cwc", [128, 24], F32, "ExternalInput"),
        ("WpT", [D, D], F32, "ExternalInput"),
        ("bpc", [128, 8], F32, "ExternalInput"),
        ("s1", [8, 128, T], BF16, "ExternalOutput"),
    ]
    cw = np.ascontiguousarray(np.asarray(d["L1_attn_conv_w"], np.float32)
                              .reshape(8, 128, 3).transpose(1, 0, 2).reshape(128, 24))
    ins = []
    for c in range(8):
        p = c ^ 1
        ins.append({"q_fm": q_fm[c], "kv_mine": kv[c], "kv_peer": kv[p],
                    "kst_mine": kst[c], "kst_peer": kst[p],
                    "local": local[c], "halo_cols": halo[c], "cwc": cw,
                    "WpT": np.ascontiguousarray(np.asarray(d["L1_attn_Wp"], np.float32).T),
                    "bpc": _cols(d["L1_attn_bp"], 8)})
    return run_stage(build_s9, specs, ins, ["s1"])


def stage10_run(xn_bf, masks, scores, x1_fm, s1, d):
    specs = [
        ("xn_bf", [8, 128, T], BF16, "ExternalInput"),
        ("masks", [8, T], F32, "ExternalInput"),
        ("scores", [T // 128, 128], F32, "ExternalInput"),
        ("w1", [8, D, 2048], F32, "ExternalInput"),
        ("w2", [8, 2048, D], F32, "ExternalInput"),
        ("x1_fm", [D, T], F32, "ExternalInput"),
        ("s1", [8, 128, T], BF16, "ExternalInput"),
        ("gamma", [128, 8], F32, "ExternalInput"),
        ("beta", [128, 8], F32, "ExternalInput"),
        ("x2_fm", [D, T], F32, "ExternalOutput"),
    ]
    ins = []
    for c in range(8):
        ins.append({"xn_bf": xn_bf[c], "masks": masks[c], "scores": scores[c],
                    "w1": np.asarray(d["L1_ffn_expert_w1"], np.float32),
                    "w2": np.asarray(d["L1_ffn_expert_w2"], np.float32),
                    "x1_fm": x1_fm[c], "s1": s1[c],
                    "gamma": _cols(d["L1_ffn_ln_gamma"], 8),
                    "beta": _cols(d["L1_ffn_ln_beta"], 8)})
    return run_stage(lambda nc, io: build_moe(nc, io, HID=2048, has_bias=False, post="l1", HSPLIT=1),
                     specs, ins, ["x2_fm"])


FUSED = True


def kernel(**inputs):
    if FUSED:
        return kernel_fused(**inputs)
    return kernel_unfused(**inputs)


def kernel_fused(**inputs):
    d = inputs
    x0 = np.asarray(d["L0_attn_hidden_states"], np.float32)

    nc = bass.Bass("TRN2", target_bir_lowering=False)
    A = {}

    def inp(name, shape, dt=F32):
        A[name] = nc.dram_tensor(name, list(shape), dt, kind="ExternalInput").ap()

    def internal(name, shape, dt=F32):
        A[name] = nc.dram_tensor(name, list(shape), dt).ap()

    def shared(name, shape, dt=F32):
        A[name] = nc.dram_tensor(name, list(shape), dt, addr_space="Shared").ap()

    inp("x0_fm", [D, T]); inp("x0h_fm", [D, 3])
    inp("identf", [128, 128]); inp("identb", [128, 128], BF16)
    inp("wg_redT", [D, 16]); inp("wg", [8, 16])
    inp("W_in", [D, DIP]); inp("convw", [128, 72]); inp("convb", [128, 18])
    inp("dtbias", [32, 1]); inp("alog", [32, 1])
    inp("W_out", [DI, D]); inp("normw", [128, 16]); inp("d_bc", [128, 32])
    inp("trilT", [2, 128, 256])
    inp("l0w1", [8, D, 4096]); inp("l0w2", [8, 4096, D])
    inp("b1c", [8, 128, 32]); inp("b2", [8, D])
    inp("gateT", [D, 8])
    inp("WqT", [D, D]); inp("WkT", [D, D]); inp("WvT", [D, D]); inp("WpT", [D, D])
    inp("cwc", [128, 24]); inp("cbc", [128, 8]); inp("bpc", [128, 8])
    inp("l1w1", [8, D, 2048]); inp("l1w2", [8, 2048, D])
    inp("gamma", [128, 8]); inp("beta", [128, 8])
    inp("xpar", [1, 8], U32); inp("singate", [128, 1])
    inp("fodd", [128, 1]); inp("feven", [128, 1])
    inp("capoff", [8, 1]); inp("b2cc", [8, 128, 8]); inp("iota512", [1, 512])

    internal("xn0_bf", [8, 128, T], BF16); internal("xnh_bf", [8, 128, 3], BF16)
    internal("masks0", [8, T]); internal("scores0", [T // 128, 128])
    internal("zs", [16, 128, T], BF16); internal("xfm", [16, 128, T], BF16)
    internal("xt_tm", [NCH, 256, DI], BF16); internal("baseT", [NCH, 256, 256], BF16)
    internal("C_fm", [128, T], BF16); internal("dAcs", [32, T])
    internal("dAcs_tm", [NCH, 256, 32]); internal("alast", [32, NCH])
    internal("states", [NCH, 128, DI]); internal("S_out", [128, DI]); internal("S_in", [128, DI])
    internal("s0", [8, 128, T], BF16); internal("g0", [8, 128, T], BF16)
    internal("x1_fm", [D, T]); internal("xn1_bf", [8, 128, T], BF16)
    internal("masks1", [8, T]); internal("scores1", [T // 128, 128])
    internal("q_fm", [8, 128, T], BF16); internal("local", [8, 128, T], BF16)
    internal("pay2", [128, 536]); internal("pay2_peer", [128, 536])
    internal("halo_raw", [128, 16], BF16); internal("s1", [8, 128, T], BF16)
    internal("xn_tm", [T, D], BF16); internal("idxf", [1, T])
    internal("gout", [8 * CAP, D], BF16)

    shared("sh_sout", [256, DI]); shared("sh_x2", [256, 536])
    shared("shflags", [1, 4], U32)

    A["x2_fm"] = nc.dram_tensor("x2_fm", [D, T], F32, kind="ExternalOutput").ap()

    build_fused(nc, A)

    tril = np.zeros((2, 128, 256), np.float32)
    for jb in range(2):
        for jj in range(128):
            tril[jb, jj, :jj + 128 * jb] = -1e30
    cw0 = np.ascontiguousarray(np.asarray(d["L0_attn_conv_w"], np.float32)
                               .reshape(18, 128, 4).transpose(1, 0, 2).reshape(128, 72))
    cw1 = np.ascontiguousarray(np.asarray(d["L1_attn_conv_w"], np.float32)
                               .reshape(8, 128, 3).transpose(1, 0, 2).reshape(128, 24))
    common = {
        "identf": IDENT_F32, "identb": IDENT_BF16,
        "wg_redT": np.ascontiguousarray(np.asarray(d["L0_ffn_wg_red"], np.float32).T),
        "wg": np.ascontiguousarray(np.asarray(d["L0_ffn_wg"], np.float32)),
        "W_in": np.asarray(d["L0_attn_W_in"], np.float32),
        "convw": cw0, "convb": _cols(d["L0_attn_conv_b"], 18),
        "dtbias": np.asarray(d["L0_attn_dt_bias"], np.float32)[:, None],
        "alog": np.asarray(d["L0_attn_A_log"], np.float32)[:, None],
        "W_out": np.asarray(d["L0_attn_W_out"], np.float32),
        "normw": _cols(d["L0_attn_norm_w"], 16),
        "d_bc": np.ascontiguousarray(np.broadcast_to(
            np.asarray(d["L0_attn_D"], np.float32)[None, :], (128, 32))),
        "trilT": tril,
        "l0w1": np.asarray(d["L0_ffn_w1"], np.float32),
        "l0w2": np.asarray(d["L0_ffn_w2"], np.float32),
        "b1c": np.ascontiguousarray(np.asarray(d["L0_ffn_b1"], np.float32)
                                    .reshape(8, 32, 128).transpose(0, 2, 1)),
        "b2": np.asarray(d["L0_ffn_b2"], np.float32),
        "gateT": np.ascontiguousarray(np.asarray(d["L1_ffn_gate_w"], np.float32).T),
        "WqT": np.ascontiguousarray(np.asarray(d["L1_attn_Wq"], np.float32).T),
        "WkT": np.ascontiguousarray(np.asarray(d["L1_attn_Wk"], np.float32).T),
        "WvT": np.ascontiguousarray(np.asarray(d["L1_attn_Wv"], np.float32).T),
        "WpT": np.ascontiguousarray(np.asarray(d["L1_attn_Wp"], np.float32).T),
        "cwc": cw1, "cbc": _cols(d["L1_attn_conv_b"], 8),
        "bpc": _cols(d["L1_attn_bp"], 8),
        "l1w1": np.asarray(d["L1_ffn_expert_w1"], np.float32),
        "l1w2": np.asarray(d["L1_ffn_expert_w2"], np.float32),
        "gamma": _cols(d["L1_ffn_ln_gamma"], 8),
        "beta": _cols(d["L1_ffn_ln_beta"], 8),
        "capoff": (np.arange(8, dtype=np.float32) * CAP - 1.0)[:, None],
        "iota512": np.arange(512, dtype=np.float32)[None, :],
        "b2cc": np.ascontiguousarray(np.asarray(d["L0_ffn_b2"], np.float32)
                                     .reshape(8, 8, 128).transpose(0, 2, 1)),
    }
    in_maps = []
    ones_col = np.ones((128, 1), np.float32)
    zeros_col = np.zeros((128, 1), np.float32)
    for c in range(8):
        b, h = c // 2, c % 2
        m = dict(common)
        m["x0_fm"] = np.ascontiguousarray(x0[b, h * 2048:(h + 1) * 2048].T)
        m["x0h_fm"] = (np.ascontiguousarray(x0[b, 2045:2048].T) if h == 1
                       else np.zeros((D, 3), np.float32))
        s = h
        m["xpar"] = np.array([[s * 128, (1 - s) * 128, s, 1 - s, 2 + s, 3 - s,
                               s * 8, (1 - s) * 8]], dtype=np.uint32)
        m["singate"] = ones_col if h == 1 else zeros_col
        m["fodd"] = ones_col if h == 1 else zeros_col
        m["feven"] = ones_col if h == 0 else zeros_col
        in_maps.append(m)

    res = run_bass_kernel_spmd(nc, in_maps, core_ids=list(range(8)))
    out = np.empty((4, 4096, D), np.float32)
    for c in range(8):
        b, h = c // 2, c % 2
        out[b, h * 2048:(h + 1) * 2048, :] = np.asarray(res.results[c]["x2_fm"], np.float32).T
    return out


def kernel_unfused(**inputs):
    d = inputs
    x0 = np.asarray(d["L0_attn_hidden_states"], np.float32)

    x0_fm, halo = [], []
    for c in range(8):
        b, h = c // 2, c % 2
        x0_fm.append(np.ascontiguousarray(x0[b, h * 2048:(h + 1) * 2048].T))
        if h == 1:
            halo.append(np.ascontiguousarray(x0[b, 2045:2048].T))
        else:
            halo.append(np.zeros((D, 3), np.float32))

    s1o = stage1_run(x0_fm, tag_l1=False, wg_red=d["L0_ffn_wg_red"], wg=d["L0_ffn_wg"], halo=halo)
    xn0_bf = [o["xn_bf"] for o in s1o]
    masks0 = [o["masks"] for o in s1o]
    scores0 = [o["scores"] for o in s1o]
    xnh_bf = [o["xnh_bf"] for o in s1o]

    s2o = stage2_run(xn0_bf, xnh_bf, d)

    s_in = []
    for c in range(8):
        if c % 2 == 1:
            s_in.append(np.ascontiguousarray(np.asarray(s2o[c - 1]["S_out"], np.float32)))
        else:
            s_in.append(np.zeros((128, DI), np.float32))

    s4o = stage4_run(s2o, s_in, d)
    s0 = [o["s0"] for o in s4o]

    s5o = stage5_run(xn0_bf, masks0, scores0, d)
    g0 = [o["g"] for o in s5o]

    s6o = stage6_run(x0_fm, s0, g0, d)
    xn1_bf = [o["xn_bf"] for o in s6o]
    masks1 = [o["masks"] for o in s6o]
    scores1 = [o["scores"] for o in s6o]
    x1_fm = [o["x1_fm"] for o in s6o]

    s7o = stage7_run(xn1_bf, d)
    q_fm = [o["q_fm"] for o in s7o]
    kv = [o["kv_part"] for o in s7o]
    kst = [o["kst_part"] for o in s7o]
    local = [o["local"] for o in s7o]

    halo_cols = []
    for c in range(8):
        p = c ^ 1
        hc = np.zeros((8, 128, 2), dtype=xn1_bf[0].dtype)
        if c % 2 == 1:
            hc[:, :, 0] = xn1_bf[p][:, :, 2047]
        else:
            hc[:, :, 1] = xn1_bf[p][:, :, 0]
        halo_cols.append(np.ascontiguousarray(hc))

    s9o = stage9_run(q_fm, kv, kst, local, halo_cols, d)
    s1 = [o["s1"] for o in s9o]

    s10o = stage10_run(xn1_bf, masks1, scores1, x1_fm, s1, d)

    out = np.empty((4, 4096, D), np.float32)
    for c in range(8):
        b, h = c // 2, c % 2
        out[b, h * 2048:(h + 1) * 2048, :] = np.asarray(s10o[c]["x2_fm"], np.float32).T
    return out


# version 20
# speedup vs baseline: 1.3193x; 1.3193x over previous
import numpy as np
import ml_dtypes
from contextlib import ExitStack
import concourse.bass as bass
import concourse.mybir as mybir
from concourse.bass_utils import run_bass_kernel_spmd
from concourse.bass import IndirectOffsetOnAxis

F32 = mybir.dt.float32
BF16 = mybir.dt.bfloat16
U32 = mybir.dt.uint32
U8 = mybir.dt.uint8
AF = mybir.ActivationFunctionType
ALU = mybir.AluOpType
AX = mybir.AxisListType
DS = bass.ds

T = 2048
D = 1024
DI = 2048
NH = 32
HD = 64
NST = 128
CH = 256
NCH = T // CH
CONVD = 2304
DIP = 4384
CAP = 1024
_T1DBG = 0



class Emit:

    ENGS = ("sync", "scalar", "vector", "tensor", "gpsimd")

    def __init__(self, nc):
        self.nc = nc
        self.q = {k: [] for k in self.ENGS}
        self.ctx = ExitStack()
        self._arenas = []
        self.ch = None
        self.n = 0
        self.gch = None
        self.gn = 0
        self._chain_id = 0
        self._names = {}

    def name(self, base):
        i = self._names.get(base, 0)
        self._names[base] = i + 1
        return f"{base}{i}"

    def _alloc_ctx(self):
        return self._arenas[-1] if self._arenas else self.ctx

    def sb(self, base, shape, dt=F32):
        return self._alloc_ctx().enter_context(self.nc.sbuf_tensor(self.name("sb_" + base), shape, dt))

    def ps(self, base, shape, dt=F32):
        return self._alloc_ctx().enter_context(self.nc.psum_tensor(self.name("ps_" + base), shape, dt))

    def sem(self, base):
        return self.ctx.enter_context(self.nc.semaphore(self.name("sem_" + base)))

    def arena(self):
        from contextlib import contextmanager

        @contextmanager
        def _a():
            st = ExitStack()
            self._arenas.append(st)
            try:
                with st:
                    yield
            finally:
                self._arenas.pop()
        return _a()

    def init_chain(self):
        self.ch = self.sem("ch")
        self.gch = self.sem("gch")
        self._consts = {}
        self._ctile = self.ctx.enter_context(self.nc.sbuf_tensor(self.name("consts"), [128, 32], F32))

    def new_chain(self):
        oc, on, og, ogn = self.ch, self.n, self.gch, self.gn
        for eng in self.ENGS:
            self.q[eng].append(lambda e, oc=oc, on=on, og=og, ogn=ogn: (
                e.wait_ge(oc, on), e.wait_ge(og, ogn)))
        self.ch = self.sem("ch")
        self.gch = self.sem("gch")
        self.n = 0
        self.gn = 0

    def cf(self, v):
        v = float(v)
        if v not in self._consts:
            i = len(self._consts)
            assert i < 32
            self._consts[v] = i
            self.step("vector", lambda e, i=i, v=v: e.memset(self._ctile[:, i:i + 1], v))
        return self._ctile[:, self._consts[v]:self._consts[v] + 1]

    def cf1(self, v):
        ap = self.cf(v)
        return self._ctile[0:1, self._consts[float(v)]:self._consts[float(v)] + 1]

    def step(self, eng, fn, dma=False, n_dma=1):
        need = self.n
        gneed = self.gn
        use_g = dma and eng == "gpsimd"
        if use_g:
            self.gn += 16 * n_dma
        elif dma:
            self.n += 16 * n_dma
        else:
            self.n += 1
        tch = self.gch if use_g else self.ch

        def wrapped(e, fn=fn, need=need, gneed=gneed, dma=dma, tch=tch):
            e.wait_ge(self.ch if tch is not self.ch else tch, need) if False else None
            e.wait_ge(self.ch, need)
            if self.gn or gneed:
                e.wait_ge(self.gch, gneed)
            r = fn(e)
            if dma:
                rs = r if isinstance(r, (list, tuple)) else [r]
                for x in rs:
                    x.then_inc(tch, 16)
            else:
                r.then_inc(tch, 1)
        ch_now, gch_now = self.ch, self.gch

        def wrapped2(e, fn=fn, need=need, gneed=gneed, dma=dma, use_g=use_g,
                     ch=ch_now, gch=gch_now):
            e.wait_ge(ch, need)
            e.wait_ge(gch, gneed)
            r = fn(e)
            tgt = gch if use_g else ch
            if dma:
                rs = r if isinstance(r, (list, tuple)) else [r]
                for x in rs:
                    x.then_inc(tgt, 16)
            else:
                r.then_inc(tgt, 1)
        self.q[eng].append(wrapped2)

    def mksem(self, base):
        class _S:
            __slots__ = ("h", "n")
        s = _S()
        s.h = self.sem(base)
        s.n = 0
        return s

    def stepd(self, eng, fn, deps=(), inc=None, dma=False, n_dma=1):
        deps = [(d[0].h if hasattr(d[0], "h") else d[0], d[1]) for d in deps]
        ih = inc.h if inc is not None else None

        def wrapped(e, fn=fn, deps=deps, ih=ih, dma=dma):
            for (h, v) in deps:
                if v > 0:
                    e.wait_ge(h, v)
            r = fn(e)
            if ih is not None:
                if dma:
                    rs = r if isinstance(r, (list, tuple)) else [r]
                    for x in rs:
                        x.then_inc(ih, 16)
                else:
                    r.then_inc(ih, 1)
        self.q[eng].append(wrapped)
        if inc is not None:
            inc.n += 16 * n_dma if dma else 1

    def join(self, sems):
        vals = [(s.h, s.n) for s in sems]
        for eng in self.ENGS:
            def w(e, vals=vals):
                for (h, v) in vals:
                    if v > 0:
                        e.wait_ge(h, v)
            self.q[eng].append(w)

    def step_manual(self, eng, fn, n_inc=1, g_inc=0):
        need, gneed = self.n, self.gn
        ch_now, gch_now = self.ch, self.gch
        self.n += n_inc
        self.gn += g_inc
        self.q[eng].append(lambda e, fn=fn, need=need, gneed=gneed, ch=ch_now, gch=gch_now:
                           fn(e, ch, need, gch, gneed))

    def raw(self, eng, fn):
        self.q[eng].append(fn)

    def barrier_val(self):
        return self.n

    def finale(self):
        fin, gfin = self.n, self.gn
        ch, gch = self.ch, self.gch
        self.raw("sync", lambda e: (e.wait_ge(ch, fin), e.wait_ge(gch, gfin)))

    def emit(self, block):
        for name, reg in (("sync", block.sync), ("scalar", block.scalar),
                          ("vector", block.vector), ("tensor", block.tensor),
                          ("gpsimd", block.gpsimd)):
            ops = self.q[name]

            def body(e, ops=ops):
                for f in ops:
                    f(e)
            reg(body)


def full(ap):
    return ap[tuple(slice(None) for _ in ap.shape)]



def load_cols(em, dst_sb, hbm_ap, n):
    em.step("sync", lambda e: e.dma_start(out=dst_sb[:, 0:n], in_=hbm_ap), dma=True)


def mm_ones_sums(em, psum_row, ones_col, rhs_blocks, start=True):
    def fn(e):
        r = None
        for i, blk in enumerate(rhs_blocks):
            r = e.matmul(psum_row[:, :], ones_col[:, 0:1], blk,
                         start=(start and i == 0), stop=(i == len(rhs_blocks) - 1))
        return r
    em.step("tensor", fn)


def bcast_row(em, psum_tile, ones_row, row_ap):
    em.step("tensor", lambda e: e.matmul(psum_tile, ones_row[0:1, :], row_ap,
                                         start=True, stop=True))


def rstd_from_var(em, rstd_row, var_row, eps, p1=True):
    b = em.cf1(eps) if p1 else em.cf(eps)
    s = em.cf1(-0.5) if p1 else em.cf(-0.5)
    em.step("scalar", lambda e: e.activation(out=rstd_row, in_=var_row, func=AF.Ln, bias=b))
    em.step("scalar", lambda e: e.activation(out=rstd_row, in_=rstd_row, func=AF.Exp, scale=s))



def emit_s1(em, io, tag_l1=False):
    nc = em.nc
    with em.arena():
        x_fm = io["x_fm"]
        xn_hbm = io["xn_bf"]
        masks_hbm = io["masks"]
        scores_hbm = io["scores"]
        identf = io["identf"]

        TT = 512
        ntile = T // TT
        ntb = TT // 128

        x_sb = em.sb("x", [128, 8 * TT])
        sq_sb = em.sb("sq", [128, 8 * TT])
        xnf = em.sb("xnf", [128, 8 * TT])
        xnb = em.sb("xnb", [128, 8 * TT], BF16)
        ones_col = em.sb("onesc", [128, 1])
        ones_row = em.sb("onesr", [1, 128])
        sid = em.sb("sid", [128, 128])
        rows = em.sb("rows", [1, 6 * TT])
        masks_sb = em.sb("masks", [8, T])
        scores_tm = em.sb("scorestm", [128, T // 128])
        scores_fm = em.sb("scoresfm", [T // 128, 128])
        logit_sb = em.sb("logit", [128, 8])
        lstat = em.sb("lstat", [128, 4])

        p_sum = em.ps("psum", [1, TT])
        p_sq = em.ps("psq", [1, TT])
        p_mu = em.ps("pmu", [128, TT])
        p_rstd = em.ps("prstd", [128, TT])
        p_xr = em.ps("pxr", [16, TT])
        p_log = em.ps("plog", [128, 8])
        p_tr = em.ps("ptr", [128, 128])
        xr_sb = em.sb("xr", [16, 512])
        if "xn_tm" in io:
            tm_sb = em.sb("tmsb", [128, 1024], BF16)
            sidb1 = em.sb("sidb1", [128, 128], BF16)
            p_trb1 = em.ps("ptrb1", [128, 8 * 128], BF16)

        if tag_l1:
            gateT_hbm = io["gateT"]
            gss = em.sb("gateT", [128, 8 * 8])
        else:
            wgredT_hbm = io["wg_redT"]
            wg_hbm = io["wg"]
            gss = em.sb("wgredT", [128, 8 * 16])
            wg_sb = em.sb("wg", [8, 16])
            wgu_sb = em.sb("wgu", [8, 16])
            wguT_sb = em.sb("wguT", [16, 8])


        em.step("vector", lambda e: e.memset(ones_col[:, :], 1.0))
        em.step("vector", lambda e: e.memset(ones_row[:, :], 1.0))
        em.step("sync", lambda e: e.dma_start(out=sid[:, :], in_=identf), dma=True)
        if "xn_tm" in io:
            em.step("sync", lambda e: e.dma_start(out=sidb1[:, :], in_=io["identb"]), dma=True)

        gw = 8 if tag_l1 else 16
        for k in range(8):
            src = (gateT_hbm if tag_l1 else wgredT_hbm)
            em.step("sync", lambda e, k=k, src=src: e.dma_start(
                out=gss[:, gw * k:gw * (k + 1)], in_=src[128 * k:128 * (k + 1), :]), dma=True)
        if not tag_l1:
            em.step("sync", lambda e: e.dma_start(out=wg_sb[:, :], in_=wg_hbm), dma=True)
            em.step("scalar", lambda e: e.activation(out=wgu_sb[:, :], in_=wg_sb[:, :], func=AF.Square))
            em.step("vector", lambda e: e.reduce_sum(lstat[0:8, 0:1], wgu_sb[0:8, :], axis=AX.X))
            em.step("scalar", lambda e: e.activation(out=lstat[0:8, 1:2], in_=lstat[0:8, 0:1], func=AF.Sqrt))
            em.step("vector", lambda e: e.tensor_scalar_max(lstat[0:8, 1:2], lstat[0:8, 1:2], 1e-4))
            em.step("vector", lambda e: e.reciprocal(lstat[0:8, 2:3], lstat[0:8, 1:2]))
            em.step("vector", lambda e: e.tensor_scalar_mul(wgu_sb[0:8, :], wg_sb[0:8, :], lstat[0:8, 2:3]))
            em.step("tensor", lambda e: e.transpose(p_tr[0:16, 0:8], wgu_sb[0:8, 0:16], sid[0:8, 0:8]))
            em.step("vector", lambda e: e.tensor_copy(wguT_sb[:, :], p_tr[0:16, 0:8]))

        eps = 1e-5
        resid = "s_bf" in io
        if resid:
            s_hbm, g_hbm, x1_hbm = io["s_bf"], io["g_bf"], io["x1_fm"]
            sres_sb = em.sb("sres", [128, 8 * TT], BF16)
            gres_sb = em.sb("gres", [128, 8 * TT], BF16)
        for it in range(ntile):
            t0 = it * TT
            def ldx(e, t0=t0):
                rs = []
                for k in range(8):
                    rs.append(e.dma_start(out=x_sb[:, TT * k:TT * (k + 1)],
                                          in_=x_fm[128 * k:128 * (k + 1), t0:t0 + TT]))
                return rs
            em.step("sync", ldx, dma=True, n_dma=8)
            if resid:
                def ldsg(e, t0=t0):
                    rs = []
                    for k in range(8):
                        rs.append(e.dma_start(out=sres_sb[:, TT * k:TT * (k + 1)],
                                              in_=s_hbm[k, :, t0:t0 + TT]))
                        rs.append(e.dma_start(out=gres_sb[:, TT * k:TT * (k + 1)],
                                              in_=g_hbm[k, :, t0:t0 + TT]))
                    return rs
                em.step("sync", ldsg, dma=True, n_dma=16)
                def mkres(e):
                    r = None
                    for k in range(8):
                        e.tensor_mul(sq_sb[:, TT * k:TT * (k + 1)], sres_sb[:, TT * k:TT * (k + 1)],
                                     gres_sb[:, TT * k:TT * (k + 1)])
                        r = e.tensor_add(x_sb[:, TT * k:TT * (k + 1)], x_sb[:, TT * k:TT * (k + 1)],
                                         sq_sb[:, TT * k:TT * (k + 1)])
                    return r
                em.step("vector", mkres)
                def stx1(e, t0=t0):
                    return [e.dma_start(out=x1_hbm[128 * k:128 * (k + 1), t0:t0 + TT],
                                        in_=x_sb[:, TT * k:TT * (k + 1)]) for k in range(8)]
                em.step("sync", stx1, dma=True, n_dma=8)
            em.step("scalar", lambda e: e.activation(out=sq_sb[:, :], in_=x_sb[:, :], func=AF.Square))
            mm_ones_sums(em, p_sum, ones_col, [x_sb[:, TT * k:TT * (k + 1)] for k in range(8)])
            mm_ones_sums(em, p_sq, ones_col, [sq_sb[:, TT * k:TT * (k + 1)] for k in range(8)])
            mu = rows[0:1, 0 * TT:1 * TT]
            var = rows[0:1, 1 * TT:2 * TT]
            rstd = rows[0:1, 2 * TT:3 * TT]
            scr = rows[0:1, 3 * TT:4 * TT]
            em.step("vector", lambda e: e.tensor_scalar_mul(mu, p_sum[:, :], 1.0 / D))
            em.step("vector", lambda e: e.tensor_scalar_mul(var, p_sq[:, :], 1.0 / D))
            em.step("scalar", lambda e: e.activation(out=scr, in_=mu, func=AF.Square))
            em.step("vector", lambda e: e.tensor_sub(var, var, scr))
            rstd_from_var(em, rstd, var, eps)
            bcast_row(em, p_mu[:, :], ones_row, mu)
            bcast_row(em, p_rstd[:, :], ones_row, rstd)
            for k in range(8):
                em.step("vector", lambda e, k=k: e.tensor_sub(
                    xnf[:, TT * k:TT * (k + 1)], x_sb[:, TT * k:TT * (k + 1)], p_mu[:, :]))
                em.step("vector", lambda e, k=k: e.tensor_mul(
                    xnf[:, TT * k:TT * (k + 1)], xnf[:, TT * k:TT * (k + 1)], p_rstd[:, :]))
            em.step("vector", lambda e: e.tensor_copy(xnb[:, :], xnf[:, :]))
            def stx(e, t0=t0):
                rs = []
                for k in range(8):
                    rs.append(e.dma_start(out=xn_hbm[k, :, t0:t0 + TT],
                                          in_=xnb[:, TT * k:TT * (k + 1)]))
                return rs
            em.step("sync", stx, dma=True, n_dma=8)

            if "xn_tm" in io:
                for tb4 in range(4):
                    tloc = tb4 * 128
                    def trtm(e, tloc=tloc):
                        r = None
                        for kf in range(8):
                            r = e.transpose(p_trb1[:, 128 * kf:128 * (kf + 1)],
                                            xnb[:, TT * kf + tloc:TT * kf + tloc + 128], sidb1[:, :])
                        return r
                    em.step("tensor", trtm)
                    def evtm(e):
                        r = None
                        for kf in range(8):
                            r = e.tensor_copy(tm_sb[:, 128 * kf:128 * (kf + 1)],
                                              p_trb1[:, 128 * kf:128 * (kf + 1)])
                        return r
                    em.step("vector", evtm)
                    em.step("sync", lambda e, t0=t0, tloc=tloc: e.dma_start(
                        out=io["xn_tm"][t0 + tloc:t0 + tloc + 128, :], in_=tm_sb[:, :]), dma=True)

            def mmg(e):
                r = None
                for k in range(8):
                    r = e.matmul(p_xr[0:gw, :], gss[:, gw * k:gw * (k + 1)],
                                 xnf[:, TT * k:TT * (k + 1)],
                                 start=(k == 0), stop=(k == 7))
                return r
            em.step("tensor", mmg)
            em.step("vector", lambda e: e.tensor_copy(xr_sb[0:gw, :], p_xr[0:gw, :]))
            for tb in range(ntb):
                gt = it * ntb + tb
                if tag_l1:
                    em.step("tensor", lambda e, tb=tb: e.transpose(
                        p_tr[:, 0:16], xr_sb[0:16, 128 * tb:128 * (tb + 1)], sid[0:16, 0:16]))
                    em.step("vector", lambda e: e.tensor_copy(logit_sb[:, 0:8], p_tr[:, 0:8]))
                else:
                    em.step("tensor", lambda e, tb=tb: e.matmul(
                        p_log[:, :], xr_sb[0:16, 128 * tb:128 * (tb + 1)], wguT_sb[:, :],
                        start=True, stop=True))
                    em.step("vector", lambda e: e.tensor_copy(logit_sb[:, :], p_log[:, :]))
                em.step("vector", lambda e: e.reduce_max(lstat[:, 0:1], logit_sb[:, 0:8], axis=AX.X))
                em.step("vector", lambda e: e.tensor_scalar(
                    logit_sb[:, 0:8], logit_sb[:, 0:8], lstat[:, 0:1], None, ALU.subtract))
                em.step("vector", lambda e: e.tensor_scalar(
                    sq_sb[:, 0:8], logit_sb[:, 0:8], 0.0, None, ALU.is_equal))
                em.step("scalar", lambda e: e.activation(out=logit_sb[:, 0:8], in_=logit_sb[:, 0:8], func=AF.Exp))
                em.step("vector", lambda e: e.reduce_sum(lstat[:, 1:2], logit_sb[:, 0:8], axis=AX.X))
                if tag_l1:
                    em.step("vector", lambda e: e.tensor_scalar(
                        lstat[:, 2:3], lstat[:, 1:2], 1e-8, 1.0, ALU.mult, ALU.add))
                    em.step("vector", lambda e, gt=gt: e.reciprocal(scores_tm[:, gt:gt + 1], lstat[:, 2:3]))
                else:
                    em.step("vector", lambda e, gt=gt: e.reciprocal(scores_tm[:, gt:gt + 1], lstat[:, 1:2]))
                em.step("tensor", lambda e: e.transpose(p_tr[:, :], sq_sb[:, 0:128], sid[:, :]))
                em.step("vector", lambda e, gt=gt: e.tensor_copy(
                    masks_sb[0:8, 128 * gt:128 * (gt + 1)], p_tr[0:8, 0:128]))

        em.step("tensor", lambda e: e.transpose(p_tr[0:T // 128, :], scores_tm[:, 0:T // 128], sid[:, :]))
        em.step("vector", lambda e: e.tensor_copy(scores_fm[:, :], p_tr[0:T // 128, 0:128]))
        em.step("sync", lambda e: e.dma_start(out=masks_hbm, in_=masks_sb[0:8, :]), dma=True)
        em.step("sync", lambda e: e.dma_start(out=scores_hbm, in_=scores_fm[0:T // 128, :]), dma=True)

        if "idxf" in io:
            capoff = io["capoff"]
            cs_sb = em.sb("cs", [8, T])
            t1_sb = em.sb("t1x", [8, T])
            ov_sb = em.sb("ovx", [8, T])
            idxrow = em.sb("idxrow", [1, T])
            capc = em.sb("capc", [8, 1])
            ones8 = em.sb("ones8", [8, 1])
            onesT8 = em.sb("onesT8", [8, T])
            p_ix = p_sum
            em.step("sync", lambda e: e.dma_start(out=capc[:, :], in_=capoff), dma=True)
            em.step("vector", lambda e: e.memset(ones8[:, :], 1.0))
            em.step("vector", lambda e: e.memset(onesT8[:, :], 1.0))
            em.step("vector", lambda e: e.tensor_tensor_scan(
                cs_sb[:, :], onesT8[:, :], masks_sb[0:8, :], 0.0, ALU.mult, ALU.add))
            def mkidx(e):
                e.tensor_scalar_add(t1_sb[:, :], cs_sb[:, :], capc[:, 0:1])
                e.tensor_scalar(ov_sb[:, :], cs_sb[:, :], CAP + 0.5, None, ALU.is_ge)
                e.scalar_tensor_tensor(t1_sb[:, :], ov_sb[:, :], 1e9, t1_sb[:, :], ALU.mult, ALU.add)
                return e.tensor_mul(t1_sb[:, :], t1_sb[:, :], masks_sb[0:8, :])
            em.step("vector", mkidx)
            for tt in range(T // 512):
                em.step("tensor", lambda e, tt=tt: e.matmul(
                    p_ix[:, :], ones8[0:8, 0:1], t1_sb[:, 512 * tt:512 * (tt + 1)], start=True, stop=True))
                em.step("vector", lambda e, tt=tt: e.tensor_copy(idxrow[0:1, 512 * tt:512 * (tt + 1)], p_ix[:, :]))
            em.step("sync", lambda e: e.dma_start(out=io["idxf"], in_=idxrow[0:1, :]), dma=True)



        if "xh_fm" in io:
            xh = io["xh_fm"]
            xnh_hbm = io["xnh_bf"]
            hx = em.sb("hx", [128, 8 * 3])
            hsq = em.sb("hsq", [128, 8 * 3])
            hb = em.sb("hb", [128, 8 * 3], BF16)
            p_hs = p_sum
            p_hq = p_sq
            p_hmu = p_mu
            p_hrstd = p_rstd
            def ldh(e):
                rs = []
                for k in range(8):
                    rs.append(e.dma_start(out=hx[:, 3 * k:3 * (k + 1)],
                                          in_=xh[128 * k:128 * (k + 1), :]))
                return rs
            em.step("sync", ldh, dma=True, n_dma=8)
            em.step("scalar", lambda e: e.activation(out=hsq[:, :], in_=hx[:, :], func=AF.Square))
            mm_ones_sums(em, p_hs[:, 0:3], ones_col, [hx[:, 3 * k:3 * (k + 1)] for k in range(8)])
            mm_ones_sums(em, p_hq[:, 0:3], ones_col, [hsq[:, 3 * k:3 * (k + 1)] for k in range(8)])
            hmu = rows[0:1, 0:3]
            hvar = rows[0:1, 8:11]
            hrstd = rows[0:1, 16:19]
            hscr = rows[0:1, 24:27]
            em.step("vector", lambda e: e.tensor_scalar_mul(hmu, p_hs[:, 0:3], 1.0 / D))
            em.step("vector", lambda e: e.tensor_scalar_mul(hvar, p_hq[:, 0:3], 1.0 / D))
            em.step("scalar", lambda e: e.activation(out=hscr, in_=hmu, func=AF.Square))
            em.step("vector", lambda e: e.tensor_sub(hvar, hvar, hscr))
            rstd_from_var(em, hrstd, hvar, eps)
            bcast_row(em, p_hmu[:, 0:3], ones_row, hmu)
            bcast_row(em, p_hrstd[:, 0:3], ones_row, hrstd)
            for k in range(8):
                em.step("vector", lambda e, k=k: e.tensor_sub(
                    hx[:, 3 * k:3 * (k + 1)], hx[:, 3 * k:3 * (k + 1)], p_hmu[:, 0:3]))
                em.step("vector", lambda e, k=k: e.tensor_mul(
                    hx[:, 3 * k:3 * (k + 1)], hx[:, 3 * k:3 * (k + 1)], p_hrstd[:, 0:3]))
            em.step("vector", lambda e: e.tensor_copy(hb[:, :], hx[:, :]))
            def sth(e):
                rs = []
                for k in range(8):
                    rs.append(e.dma_start(out=xnh_hbm[k, :, :], in_=hb[:, 3 * k:3 * (k + 1)]))
                return rs
            em.step("sync", sth, dma=True, n_dma=8)





S2_STOP = 99

def emit_s2(em, io):
    nc = em.nc
    with em.arena():
        xn = io["xn_bf"]
        xnh = io["xnh_bf"]
        W_in = io["W_in"]
        convw = io["convw"]
        convb = io["convb"]
        dtbias = io["dtbias"]
        alog = io["alog"]
        identb = io["identb"]
        identf = io["identf"]
        o_zs = io["zs"]
        o_xfm = io["xfm"]
        o_xt = io["xt_tm"]
        o_baseT = io["baseT"]
        o_cfm = io["C_fm"]
        o_dacs = io["dAcs"]
        o_dacstm = io["dAcs_tm"]
        o_alast = io["alast"]
        o_states = io["states"]
        o_sout = io["S_out"]

        G = 512
        NG = T // G
        NB_Z, NB_X = 16, 18
        MB = 4

        w_sb = em.sb("win", [128, 8 * DIP], BF16)
        xn_g = em.sb("xng", [128, 8 * G], BF16)
        xbc_g = em.sb("xbcg", [128, NB_X * (G + 3)])
        zstage = em.sb("zstage", [128, MB * G], BF16)
        conv_bf = em.sb("convbf", [128, 6 * G], BF16)
        xtm_g = em.sb("xtmg", [128, 4 * DI], BF16)
        xtd_g = em.sb("xtdg", [128, 4 * DI], BF16)
        btm_g = em.sb("btmg", [128, 4 * 128], BF16)
        bfm_g = em.sb("bfmg", [128, G], BF16)
        cfm_g = em.sb("cfmg", [128, G], BF16)
        dt_g = em.sb("dtg", [32, G])
        da_g = em.sb("dag", [32, G])
        dacs_g = em.sb("dacsg", [32, G])
        dttm_g = em.sb("dttmg", [128, 4 * 32])
        dacstm_g = em.sb("dacstmg", [128, 4 * 32])
        w_g = em.sb("wg2", [128, 4 * 32])
        decay_g = em.sb("decayg", [128, 4 * 32])
        cdbc = em.sb("cdbc", [128, 32])
        alast_sb = em.sb("alast", [32, NCH])
        prev = em.sb("prev", [128, DI])
        states_sb = em.sb("statessb", [128, DI])
        base_sb = em.sb("basesb", [128, 2 * 256], BF16)
        sidb = em.sb("sidb", [128, 128], BF16)
        sidf = em.sb("sidf", [128, 128])
        onesc = em.sb("onesc2", [32, 256])
        onesr = em.sb("onesr2", [1, 128])
        cw_sb = em.sb("cw", [128, NB_X * 4])
        cb_sb = em.sb("cb", [128, NB_X])
        dtb_sb = em.sb("dtb", [32, 1])
        nega_sb = em.sb("nega", [32, 1])
        tmp_sb = em.sb("tmp", [128, G])
        tmp2_sb = em.sb("tmp2", [128, G])
        convf_sb = em.sb("convf", [128, 6 * G])
        xnh_sb = em.sb("xnh", [128, 8 * 3], BF16)

        p_mm = em.ps("pmm2", [128, MB * G])
        p_st = em.ps("pst", [128, DI // 2])
        p_trb = em.ps("ptrb", [128, 8 * 128], BF16)
        p_trf = em.ps("ptrf", [128, 128])

        em.step("vector", lambda e: e.memset(onesc[:, :], 1.0))
        em.step("vector", lambda e: e.memset(onesr[:, :], 1.0))
        em.step("vector", lambda e: e.memset(prev[:, :], 0.0))
        em.step("sync", lambda e: e.dma_start(out=sidb[:, :], in_=identb), dma=True)
        em.step("sync", lambda e: e.dma_start(out=sidf[:, :], in_=identf), dma=True)
        em.step("sync", lambda e: e.dma_start(out=cw_sb[:, 0:NB_X * 4], in_=convw), dma=True)
        em.step("sync", lambda e: e.dma_start(out=cb_sb[:, 0:NB_X], in_=convb), dma=True)
        em.step("sync", lambda e: e.dma_start(out=dtb_sb[:, :], in_=dtbias), dma=True)
        em.step("sync", lambda e: e.dma_start(out=nega_sb[:, :], in_=alog), dma=True)
        em.step("scalar", lambda e: e.activation(out=nega_sb[:, :], in_=nega_sb[:, :], func=AF.Exp))
        em.step("vector", lambda e: e.tensor_scalar_mul(nega_sb[:, :], nega_sb[:, :], -1.0))
        def ldwin(e):
            return [e.dma_start(out=w_sb[:, DIP * k:DIP * (k + 1)], in_=W_in[128 * k:128 * (k + 1), :])
                    for k in range(8)]
        em.step("gpsimd", ldwin, dma=True, n_dma=8)

        def ldh(e):
            return [e.dma_start(out=xnh_sb[:, 3 * k:3 * (k + 1)], in_=xnh[k, :, :]) for k in range(8)]
        em.step("sync", ldh, dma=True, n_dma=8)
        for bb in range(0, NB_X, 6):
            bs = list(range(bb, min(bb + 6, NB_X)))
            def mmh(e, bs=bs):
                r = None
                for bi, b in enumerate(bs):
                    mcol = (NB_Z + b) * 128
                    for k in range(8):
                        r = e.matmul(p_mm[:, 512 * (bi % 4):512 * (bi % 4) + 3],
                                     w_sb[:, DIP * k + mcol:DIP * k + mcol + 128],
                                     xnh_sb[:, 3 * k:3 * (k + 1)], start=(k == 0), stop=(k == 7)) if bi < 4 else r
                return r
            for sub in range(0, len(bs), 4):
                sbs = bs[sub:sub + 4]
                def mmh2(e, sbs=sbs):
                    r = None
                    for bi, b in enumerate(sbs):
                        mcol = (NB_Z + b) * 128
                        for k in range(8):
                            r = e.matmul(p_mm[:, 512 * bi:512 * bi + 3],
                                         w_sb[:, DIP * k + mcol:DIP * k + mcol + 128],
                                         xnh_sb[:, 3 * k:3 * (k + 1)], start=(k == 0), stop=(k == 7))
                    return r
                em.step("tensor", mmh2)
                def evh(e, sbs=sbs):
                    r = None
                    for bi, b in enumerate(sbs):
                        r = e.tensor_copy(xbc_g[:, b * (G + 3):b * (G + 3) + 3], p_mm[:, 512 * bi:512 * bi + 3])
                    return r
                em.step("vector", evh)

        for g in range(NG):
            t0 = g * G
            def ldx(e, t0=t0):
                return [e.dma_start(out=xn_g[:, G * k:G * (k + 1)], in_=xn[k, :, t0:t0 + G])
                        for k in range(8)]
            em.step("sync", ldx, dma=True, n_dma=8)

            for mb in range(0, NB_Z, MB):
                def mmz(e, mb=mb):
                    r = None
                    for mi in range(MB):
                        m = mb + mi
                        for k in range(8):
                            r = e.matmul(p_mm[:, G * mi:G * (mi + 1)],
                                         w_sb[:, DIP * k + 128 * m:DIP * k + 128 * (m + 1)],
                                         xn_g[:, G * k:G * (k + 1)], start=(k == 0), stop=(k == 7))
                    return r
                em.step("tensor", mmz)
                def evz(e):
                    r = None
                    for mi in range(MB):
                        r = e.activation(out=zstage[:, G * mi:G * (mi + 1)],
                                         in_=p_mm[:, G * mi:G * (mi + 1)], func=AF.Silu)
                    return r
                em.step("scalar", evz)
                def stz(e, mb=mb, t0=t0):
                    return [e.dma_start(out=o_zs[mb + mi, :, t0:t0 + G], in_=zstage[:, G * mi:G * (mi + 1)])
                            for mi in range(MB)]
                em.step("sync", stz, dma=True, n_dma=MB)

            for mb in range(0, NB_X, MB):
                ms = list(range(mb, min(mb + MB, NB_X)))
                def mmx(e, ms=ms):
                    r = None
                    for mi, b in enumerate(ms):
                        mcol = (NB_Z + b) * 128
                        for k in range(8):
                            r = e.matmul(p_mm[:, G * mi:G * (mi + 1)],
                                         w_sb[:, DIP * k + mcol:DIP * k + mcol + 128],
                                         xn_g[:, G * k:G * (k + 1)], start=(k == 0), stop=(k == 7))
                    return r
                em.step("tensor", mmx)
                def evx(e, ms=ms):
                    r = None
                    for mi, b in enumerate(ms):
                        r = e.tensor_copy(xbc_g[:, b * (G + 3) + 3:b * (G + 3) + 3 + G],
                                          p_mm[:, G * mi:G * (mi + 1)])
                    return r
                em.step("vector", evx)

            def mmd(e):
                r = None
                for k in range(8):
                    r = e.matmul(p_mm[0:32, 0:G], w_sb[:, DIP * k + 4352:DIP * k + 4384],
                                 xn_g[:, G * k:G * (k + 1)], start=(k == 0), stop=(k == 7))
                return r
            em.step("tensor", mmd)
            em.step("vector", lambda e: e.tensor_scalar_add(dt_g[:, :], p_mm[0:32, 0:G], dtb_sb[:, 0:1]))
            em.step("vector", lambda e: e.tensor_relu(da_g[:, :], dt_g[:, :]))
            def actdt(e):
                e.activation(out=tmp_sb[0:32, :], in_=dt_g[:, :], func=AF.Abs)
                e.activation(out=tmp_sb[0:32, :], in_=tmp_sb[0:32, :], func=AF.Exp, scale=-1.0)
                return e.activation(out=tmp_sb[0:32, :], in_=tmp_sb[0:32, :], func=AF.Ln, bias=1.0)
            em.step("scalar", actdt)
            def dvdt(e):
                e.tensor_add(dt_g[:, :], da_g[:, :], tmp_sb[0:32, :])
                e.tensor_scalar_mul(da_g[:, :], dt_g[:, :], nega_sb[:, 0:1])
                e.tensor_tensor_scan(dacs_g[:, 0:256], onesc[:, :], da_g[:, 0:256], 0.0, ALU.mult, ALU.add)
                return e.tensor_tensor_scan(dacs_g[:, 256:512], onesc[:, :], da_g[:, 256:512], 0.0, ALU.mult, ALU.add)
            em.step("vector", dvdt)
            def cpalast(e, g=g):
                e.tensor_copy(alast_sb[:, 2 * g:2 * g + 1], dacs_g[:, 255:256])
                return e.tensor_copy(alast_sb[:, 2 * g + 1:2 * g + 2], dacs_g[:, 511:512])
            em.step("vector", cpalast)
            em.step("sync", lambda e, t0=t0: e.dma_start(out=o_dacs[:, t0:t0 + G], in_=dacs_g[:, :]), dma=True)

            for bb in range(0, NB_X, 6):
                bs = list(range(bb, min(bb + 6, NB_X)))
                def conv(e, bs=bs):
                    r = None
                    for bi, b in enumerate(bs):
                        base = b * (G + 3)
                        e.tensor_scalar_mul(convf_sb[:, G * bi:G * (bi + 1)],
                                            xbc_g[:, base:base + G], cw_sb[:, 4 * b:4 * b + 1])
                        for kk in range(1, 4):
                            r = e.scalar_tensor_tensor(convf_sb[:, G * bi:G * (bi + 1)],
                                                       xbc_g[:, base + kk:base + kk + G],
                                                       cw_sb[:, 4 * b + kk:4 * b + kk + 1],
                                                       convf_sb[:, G * bi:G * (bi + 1)],
                                                       ALU.mult, ALU.add)
                    return r
                em.step("vector", conv)
                def silub(e, bs=bs):
                    r = None
                    for bi, b in enumerate(bs):
                        r = e.activation(out=conv_bf[:, G * bi:G * (bi + 1)],
                                         in_=convf_sb[:, G * bi:G * (bi + 1)],
                                         func=AF.Silu, bias=cb_sb[:, b:b + 1])
                    return r
                em.step("scalar", silub)
                def stconv(e, bs=bs, t0=t0):
                    rs = []
                    for bi, b in enumerate(bs):
                        if b < 16:
                            rs.append(e.dma_start(out=o_xfm[b, :, t0:t0 + G], in_=conv_bf[:, G * bi:G * (bi + 1)]))
                    return rs
                nst = sum(1 for b in bs if b < 16)
                if nst:
                    em.step("sync", stconv, dma=True, n_dma=nst)
                def keepbc(e, bs=bs):
                    r = None
                    for bi, b in enumerate(bs):
                        if b == 16:
                            r = e.tensor_copy(bfm_g[:, :], conv_bf[:, G * bi:G * (bi + 1)])
                        elif b == 17:
                            r = e.tensor_copy(cfm_g[:, :], conv_bf[:, G * bi:G * (bi + 1)])
                    return r
                if any(b >= 16 for b in bs):
                    em.step("vector", keepbc)
                    em.step("sync", lambda e, t0=t0: e.dma_start(out=o_cfm[:, t0:t0 + G], in_=cfm_g[:, :]), dma=True)
                xbs = [b for b in bs if b < 16]
                for pairs in range(0, len(xbs), 2):
                    pb2 = xbs[pairs:pairs + 2]
                    def trx(e, pb2=pb2, bs=bs):
                        r = None
                        for oi, b in enumerate(pb2):
                            bi = bs.index(b)
                            for jb in range(4):
                                r = e.transpose(p_trb[:, 128 * (4 * oi + jb):128 * (4 * oi + jb + 1)],
                                                conv_bf[:, G * bi + 128 * jb:G * bi + 128 * (jb + 1)], sidb[:, :])
                        return r
                    em.step("tensor", trx)
                    def evtx(e, pb2=pb2):
                        r = None
                        for oi, b in enumerate(pb2):
                            for jb in range(4):
                                r = e.tensor_copy(xtm_g[:, jb * DI + 128 * b:jb * DI + 128 * (b + 1)],
                                                  p_trb[:, 128 * (4 * oi + jb):128 * (4 * oi + jb + 1)])
                        return r
                    em.step("vector", evtx)
                if 16 in bs:
                    bi16 = bs.index(16)
                    def trb16(e, bi16=bi16):
                        r = None
                        for jb in range(4):
                            r = e.transpose(p_trb[:, 128 * jb:128 * (jb + 1)],
                                            conv_bf[:, G * bi16 + 128 * jb:G * bi16 + 128 * (jb + 1)], sidb[:, :])
                        return r
                    em.step("tensor", trb16)
                    def evb16(e):
                        r = None
                        for jb in range(4):
                            r = e.tensor_copy(btm_g[:, 128 * jb:128 * (jb + 1)], p_trb[:, 128 * jb:128 * (jb + 1)])
                        return r
                    em.step("vector", evb16)
                def carry(e, bs=bs):
                    r = None
                    for b in bs:
                        base = b * (G + 3)
                        r = e.tensor_copy(xbc_g[:, base:base + 3], xbc_g[:, base + G:base + G + 3])
                    return r
                em.step("vector", carry)

            def trdt(e):
                r = None
                for jb in range(4):
                    r = e.transpose(p_trf[:, 0:32], dt_g[:, 128 * jb:128 * (jb + 1)], sidf[0:32, 0:32])
                    r = e.transpose(p_st[:, 32 * jb:32 * (jb + 1)] if False else r, dt_g[:, 0:1], sidf[0:1, 0:1]) if False else r
                return r
            def trdt4(e):
                r = None
                for jb in range(4):
                    r = e.transpose(p_st[0:128, 128 * jb:128 * jb + 32].bitcast(F32) if False else p_st[:, 128 * jb:128 * jb + 32],
                                    dt_g[:, 128 * jb:128 * (jb + 1)], sidf[0:32, 0:32])
                return r
            em.step("tensor", trdt4)
            def evdt4(e):
                r = None
                for jb in range(4):
                    r = e.tensor_copy(dttm_g[:, 32 * jb:32 * (jb + 1)], p_st[:, 128 * jb:128 * jb + 32])
                return r
            em.step("vector", evdt4)
            def trda4(e):
                r = None
                for jb in range(4):
                    r = e.transpose(p_st[:, 128 * jb:128 * jb + 32], dacs_g[:, 128 * jb:128 * (jb + 1)], sidf[0:32, 0:32])
                return r
            em.step("tensor", trda4)
            def evda4(e):
                r = None
                for jb in range(4):
                    r = e.tensor_copy(dacstm_g[:, 32 * jb:32 * (jb + 1)], p_st[:, 128 * jb:128 * jb + 32])
                return r
            em.step("vector", evda4)
            def stdacstm(e, g=g):
                rs = []
                for jb in range(4):
                    c = 2 * g + jb // 2
                    rs.append(e.dma_start(out=o_dacstm[c, 128 * (jb % 2):128 * (jb % 2 + 1), :],
                                          in_=dacstm_g[:, 32 * jb:32 * (jb + 1)]))
                return rs
            em.step("sync", stdacstm, dma=True, n_dma=4)

            for cc in range(2):
                c = 2 * g + cc
                em.step("tensor", lambda e, c=c: e.transpose(
                    p_trf[0:1, 0:32], alast_sb[:, c:c + 1], sidf[0:32, 0:32]))
                em.step("vector", lambda e: e.tensor_copy(tmp_sb[0:1, 0:32], p_trf[0:1, 0:32]))
                em.step("tensor", lambda e: e.matmul(p_trf[:, 0:32], onesr[0:1, :], tmp_sb[0:1, 0:32], start=True, stop=True))
                em.step("scalar", lambda e: e.activation(out=cdbc[:, :], in_=p_trf[:, 0:32], func=AF.Exp))
                def decw(e, cc=cc):
                    r = None
                    for jb2 in range(2):
                        jb = 2 * cc + jb2
                        e.tensor_sub(decay_g[:, 32 * jb:32 * (jb + 1)], p_trf[:, 0:32], dacstm_g[:, 32 * jb:32 * (jb + 1)])
                    return e.tensor_copy(tmp_sb[0:1, 0:1], tmp_sb[0:1, 0:1])
                em.step("vector", decw)
                def decexp(e, cc=cc):
                    r = None
                    for jb2 in range(2):
                        jb = 2 * cc + jb2
                        r = e.activation(out=decay_g[:, 32 * jb:32 * (jb + 1)],
                                         in_=decay_g[:, 32 * jb:32 * (jb + 1)], func=AF.Exp)
                    return r
                em.step("scalar", decexp)
                def wxt(e, cc=cc):
                    r = None
                    for jb2 in range(2):
                        jb = 2 * cc + jb2
                        e.tensor_mul(w_g[:, 32 * jb:32 * (jb + 1)], decay_g[:, 32 * jb:32 * (jb + 1)],
                                     dttm_g[:, 32 * jb:32 * (jb + 1)])
                    for jb2 in range(2):
                        jb = 2 * cc + jb2
                        for h in range(NH):
                            e.tensor_scalar_mul(xtd_g[:, jb * DI + 64 * h:jb * DI + 64 * (h + 1)],
                                                xtm_g[:, jb * DI + 64 * h:jb * DI + 64 * (h + 1)],
                                                w_g[:, 32 * jb + h:32 * jb + h + 1])
                            r = e.tensor_scalar_mul(xtm_g[:, jb * DI + 64 * h:jb * DI + 64 * (h + 1)],
                                                    xtm_g[:, jb * DI + 64 * h:jb * DI + 64 * (h + 1)],
                                                    dttm_g[:, 32 * jb + h:32 * jb + h + 1])
                    return r
                em.step("vector", wxt)
                def stxt(e, cc=cc, c=c):
                    rs = []
                    for jb2 in range(2):
                        jb = 2 * cc + jb2
                        rs.append(e.dma_start(out=o_xt[c, 128 * jb2:128 * (jb2 + 1), :],
                                              in_=xtm_g[:, jb * DI:(jb + 1) * DI]))
                    return rs
                em.step("sync", stxt, dma=True, n_dma=2)
                for hh in range(2):
                    def mmstates(e, cc=cc, hh=hh):
                        r = None
                        for h2 in range(16):
                            h = 16 * hh + h2
                            for jb2 in range(2):
                                jb = 2 * cc + jb2
                                r = e.matmul(p_st[:, 64 * h2:64 * (h2 + 1)],
                                             btm_g[:, 128 * jb:128 * (jb + 1)],
                                             xtd_g[:, jb * DI + 64 * h:jb * DI + 64 * (h + 1)],
                                             start=(jb2 == 0), stop=(jb2 == 1))
                        return r
                    em.step("tensor", mmstates)
                    em.step("vector", lambda e, hh=hh: e.tensor_copy(
                        states_sb[:, 1024 * hh:1024 * (hh + 1)], p_st[:, :]))
                em.step("sync", lambda e, c=c: e.dma_start(out=o_states[c, :, :], in_=states_sb[:, :]), dma=True)
                def scanup(e):
                    r = None
                    for h in range(NH):
                        r = e.scalar_tensor_tensor(prev[:, 64 * h:64 * (h + 1)], prev[:, 64 * h:64 * (h + 1)],
                                                   cdbc[:, h:h + 1], states_sb[:, 64 * h:64 * (h + 1)],
                                                   ALU.mult, ALU.add)
                    return r
                em.step("vector", scanup)
                def mmbase(e, cc=cc):
                    r = None
                    for jb2 in range(2):
                        r = e.matmul(p_mm[:, 512 * jb2:512 * jb2 + 256],
                                     bfm_g[:, 256 * cc + 128 * jb2:256 * cc + 128 * (jb2 + 1)],
                                     cfm_g[:, 256 * cc:256 * (cc + 1)], start=True, stop=True)
                    return r
                em.step("tensor", mmbase)
                def evbase(e):
                    e.tensor_copy(base_sb[:, 0:256], p_mm[:, 0:256])
                    return e.tensor_copy(base_sb[:, 256:512], p_mm[:, 512:768])
                em.step("vector", evbase)
                def stbase(e, c=c):
                    return [e.dma_start(out=o_baseT[c, 0:128, :], in_=base_sb[:, 0:256]),
                            e.dma_start(out=o_baseT[c, 128:256, :], in_=base_sb[:, 256:512])]
                em.step("sync", stbase, dma=True, n_dma=2)

        em.step("sync", lambda e: e.dma_start(out=o_alast, in_=alast_sb[:, :]), dma=True)
        em.step("sync", lambda e: e.dma_start(out=o_sout, in_=prev[:, :]), dma=True)


def emit_s4(em, io):
    nc = em.nc
    GH = 8
    NG_H = NH // GH
    with em.arena():
        i_xt = io["xt_tm"]
        i_base = io["baseT"]
        i_cfm = io["C_fm"]
        i_dacs = io["dAcs"]
        i_dacstm = io["dAcs_tm"]
        i_alast = io["alast"]
        i_states = io["states"]
        i_sin = io["S_in"]
        i_zs = io["zs"]
        i_xfm = io["xfm"]
        i_wout = io["W_out"]
        i_nw = io["normw"]
        i_dbc = io["d_bc"]
        i_tril = io["trilT"]
        o_s0 = io["s0"]

        w2_sb = em.sb("wout", [128, 16 * 1024], BF16)
        xt_sb = em.sb("xt", [128, 2 * DI], BF16)
        base_sb = em.sb("base", [128, 2 * 256], BF16)
        st_sb = em.sb("st", [128, DI])
        prev = em.sb("prev4", [128, DI])
        prev_bf = em.sb("prevbf", [128, DI], BF16)
        cfm_sb = em.sb("cfm", [128, T], BF16)
        zs_sb = em.sb("zs", [128, 16 * 256], BF16)
        xf_sb = em.sb("xf", [128, 16 * 256], BF16)
        dacsr = em.sb("dacsr", [1, NH * 256])
        dtm_sb = em.sb("dtm", [128, 2 * 32])
        alr = em.sb("alr", [1, 32])
        cd_sb = em.sb("cd", [128, 32])
        tril_sb = em.sb("tril", [128, 2 * 256])
        nw_sb = em.sb("nw", [128, 16])
        dbc_sb = em.sb("dbc", [128, 32])
        onesr = em.sb("onesr4", [1, 128])
        onesc = em.sb("onesc4", [128, 1])
        e_sb = em.sb("esb", [128, GH * 256])
        csc_sb = em.sb("csc", [128, GH * 256], BF16)
        dif_sb = em.sb("dif", [128, 2 * GH * 256])
        expd_sb = em.sb("expd", [128, 2 * GH * 256], BF16)
        rhs_sb = em.sb("rhs", [128, 2 * GH * 256], BF16)
        y2_sb = em.sb("y2", [128, 16 * 256])
        sq_sb = em.sb("sq4", [128, 16 * 256])
        y3_sb = em.sb("y3", [128, 16 * 256], BF16)
        rows = em.sb("rows4", [1, 4 * 256])
        outst = em.sb("outst", [128, 2 * 256], BF16)
        t64 = em.sb("t64", [128, 2 * 256])

        p_sh = em.ps("psh", [128, GH * 256])
        p_out = em.ps("pout4", [128, 2 * 512])
        p_misc = em.ps("pmisc4", [128, 512])
        p_bc = em.ps("pbc4", [128, 32])

        b_eps = em.cf1(1e-5)
        b_half = em.cf1(-0.5)

        em.step("vector", lambda e: e.memset(onesr[:, :], 1.0))
        em.step("vector", lambda e: e.memset(onesc[:, :], 1.0))
        em.step("sync", lambda e: e.dma_start(out=nw_sb[:, :], in_=i_nw), dma=True)
        em.step("sync", lambda e: e.dma_start(out=dbc_sb[:, :], in_=i_dbc), dma=True)
        em.step("sync", lambda e: e.dma_start(out=prev[:, :], in_=i_sin), dma=True)
        if "singate" in io:
            sg_sb = em.sb("sgate", [128, 1])
            em.step("sync", lambda e: e.dma_start(out=sg_sb[:, :], in_=io["singate"]), dma=True)
            em.step("vector", lambda e: e.tensor_scalar_mul(prev[:, :], prev[:, :], sg_sb[:, 0:1]))
        em.step("sync", lambda e: e.dma_start(out=cfm_sb[:, :], in_=i_cfm), dma=True)
        def ldtril(e):
            return [e.dma_start(out=tril_sb[:, 256 * jb:256 * (jb + 1)], in_=i_tril[jb, :, :])
                    for jb in range(2)]
        em.step("sync", ldtril, dma=True, n_dma=2)
        def ldw(e):
            return [e.dma_start(out=w2_sb[:, 1024 * k:1024 * (k + 1)],
                                in_=i_wout[128 * k:128 * (k + 1), :]) for k in range(16)]
        em.step("gpsimd", ldw, dma=True, n_dma=16)

        for c in range(NCH):
            t0 = c * 256
            def ldchunk(e, c=c, t0=t0):
                rs = []
                for jb in range(2):
                    rs.append(e.dma_start(out=xt_sb[:, DI * jb:DI * (jb + 1)], in_=i_xt[c, 128 * jb:128 * (jb + 1), :]))
                    rs.append(e.dma_start(out=base_sb[:, 256 * jb:256 * (jb + 1)], in_=i_base[c, 128 * jb:128 * (jb + 1), :]))
                    rs.append(e.dma_start(out=dtm_sb[:, 32 * jb:32 * (jb + 1)], in_=i_dacstm[c, 128 * jb:128 * (jb + 1), :]))
                rs.append(e.dma_start(out=st_sb[:, :], in_=i_states[c, :, :]))
                rs.append(e.dma_start(out=dacsr[0:1, :].rearrange("p (a b) -> p a b", a=32), in_=i_dacs[:, t0:t0 + 256]))
                rs.append(e.dma_start(out=alr[0:1, :].rearrange("p (a b) -> p a b", a=32), in_=i_alast[:, c:c + 1], allow_slow_non_contiguous=True))
                for k in range(16):
                    rs.append(e.dma_start(out=zs_sb[:, 256 * k:256 * (k + 1)], in_=i_zs[k, :, t0:t0 + 256]))
                    rs.append(e.dma_start(out=xf_sb[:, 256 * k:256 * (k + 1)], in_=i_xfm[k, :, t0:t0 + 256]))
                return rs
            em.step("sync", ldchunk, dma=True, n_dma=41)

            em.step("tensor", lambda e: e.matmul(p_bc[:, :], onesr[0:1, :], alr[0:1, :], start=True, stop=True))
            em.step("scalar", lambda e: e.activation(out=cd_sb[:, :], in_=p_bc[:, :], func=AF.Exp))
            em.step("vector", lambda e: e.tensor_copy(prev_bf[:, :], prev[:, :]))

            for g in range(NG_H):
                h0 = g * GH
                def mmab(e, h0=h0):
                    r = None
                    for j in range(GH):
                        r = e.matmul(p_sh[:, 256 * j:256 * (j + 1)], onesr[0:1, :],
                                     dacsr[0:1, 256 * (h0 + j):256 * (h0 + j + 1)], start=True, stop=True)
                    return r
                em.step("tensor", mmab)
                def acte(e):
                    r = None
                    for j in range(GH):
                        r = e.activation(out=e_sb[:, 256 * j:256 * (j + 1)],
                                         in_=p_sh[:, 256 * j:256 * (j + 1)], func=AF.Exp)
                    return r
                em.step("scalar", acte)
                def dv1(e, h0=h0, t0=t0):
                    r = None
                    for j in range(GH):
                        h = h0 + j
                        for jb in range(2):
                            r = e.scalar_tensor_tensor(
                                dif_sb[:, 256 * (2 * j + jb):256 * (2 * j + jb + 1)],
                                p_sh[:, 256 * j:256 * (j + 1)],
                                dtm_sb[:, 32 * jb + h:32 * jb + h + 1],
                                tril_sb[:, 256 * jb:256 * (jb + 1)], ALU.subtract, ALU.add)
                        r = e.tensor_mul(csc_sb[:, 256 * j:256 * (j + 1)],
                                         cfm_sb[:, t0:t0 + 256], e_sb[:, 256 * j:256 * (j + 1)])
                    return r
                em.step("vector", dv1)
                def act2(e):
                    r = None
                    for j in range(2 * GH):
                        r = e.activation(out=expd_sb[:, 256 * j:256 * (j + 1)],
                                         in_=dif_sb[:, 256 * j:256 * (j + 1)], func=AF.Exp)
                    return r
                em.step("scalar", act2)
                def dv2(e):
                    r = None
                    for j in range(GH):
                        for jb in range(2):
                            r = e.tensor_mul(rhs_sb[:, 256 * (2 * j + jb):256 * (2 * j + jb + 1)],
                                             expd_sb[:, 256 * (2 * j + jb):256 * (2 * j + jb + 1)],
                                             base_sb[:, 256 * jb:256 * (jb + 1)])
                    return r
                em.step("vector", dv2)
                def mmy(e, h0=h0):
                    r = None
                    for j in range(GH):
                        h = h0 + j
                        pb = (h % 2) * 64
                        for jb in range(2):
                            e.matmul(p_sh[pb:pb + 64, 256 * j:256 * (j + 1)],
                                     xt_sb[:, DI * jb + 64 * h:DI * jb + 64 * (h + 1)],
                                     rhs_sb[:, 256 * (2 * j + jb):256 * (2 * j + jb + 1)],
                                     start=(jb == 0), stop=False)
                        r = e.matmul(p_sh[pb:pb + 64, 256 * j:256 * (j + 1)],
                                     prev_bf[:, 64 * h:64 * (h + 1)],
                                     csc_sb[:, 256 * j:256 * (j + 1)], start=False, stop=True)
                    return r
                em.step("tensor", mmy)
                def dv3(e, h0=h0):
                    r = None
                    for j in range(GH):
                        h = h0 + j
                        pb = (h % 2) * 64
                        kb = h // 2
                        e.scalar_tensor_tensor(t64[pb:pb + 64, 0:256],
                                               xf_sb[pb:pb + 64, 256 * kb:256 * (kb + 1)],
                                               dbc_sb[pb:pb + 64, h:h + 1],
                                               p_sh[pb:pb + 64, 256 * j:256 * (j + 1)], ALU.mult, ALU.add)
                        r = e.tensor_mul(y2_sb[pb:pb + 64, 256 * kb:256 * (kb + 1)],
                                         t64[pb:pb + 64, 0:256],
                                         zs_sb[pb:pb + 64, 256 * kb:256 * (kb + 1)])
                    return r
                em.step("vector", dv3)

            def scanup(e):
                r = None
                for h in range(NH):
                    r = e.scalar_tensor_tensor(prev[:, 64 * h:64 * (h + 1)], prev[:, 64 * h:64 * (h + 1)],
                                               cd_sb[:, h:h + 1], st_sb[:, 64 * h:64 * (h + 1)],
                                               ALU.mult, ALU.add)
                return r
            em.step("vector", scanup)

            em.step("scalar", lambda e: e.activation(out=sq_sb[:, :], in_=y2_sb[:, :], func=AF.Square))
            def mmsum(e):
                r = None
                for k in range(16):
                    r = e.matmul(p_misc[0:1, 0:256], onesc[:, 0:1], sq_sb[:, 256 * k:256 * (k + 1)],
                                 start=(k == 0), stop=(k == 15))
                return r
            em.step("tensor", mmsum)
            ms = rows[0:1, 0:256]
            rstd = rows[0:1, 256:512]
            em.step("vector", lambda e: e.tensor_scalar_mul(ms, p_misc[0:1, 0:256], 1.0 / DI))
            em.step("scalar", lambda e: e.activation(out=rstd, in_=ms, func=AF.Ln, bias=b_eps))
            em.step("scalar", lambda e: e.activation(out=rstd, in_=rstd, func=AF.Exp, scale=b_half))
            em.step("tensor", lambda e: e.matmul(p_misc[:, 256:512], onesr[0:1, :], rstd, start=True, stop=True))
            def y3ops(e):
                r = None
                for k in range(16):
                    r = e.scalar_tensor_tensor(y3_sb[:, 256 * k:256 * (k + 1)], y2_sb[:, 256 * k:256 * (k + 1)],
                                               nw_sb[:, k:k + 1], p_misc[:, 256:512], ALU.mult, ALU.mult)
                return r
            em.step("vector", y3ops)
            for mp2 in range(4):
                def mmo2(e, mp2=mp2):
                    r = None
                    for mi in range(2):
                        m = 2 * mp2 + mi
                        for k in range(16):
                            r = e.matmul(p_out[:, 512 * mi:512 * mi + 256],
                                         w2_sb[:, 1024 * k + 128 * m:1024 * k + 128 * (m + 1)],
                                         y3_sb[:, 256 * k:256 * (k + 1)], start=(k == 0), stop=(k == 15))
                    return r
                em.step("tensor", mmo2)
                def ev2(e):
                    e.tensor_copy(outst[:, 0:256], p_out[:, 0:256])
                    return e.tensor_copy(outst[:, 256:512], p_out[:, 512:768])
                em.step("vector", ev2)
                def st2(e, mp2=mp2, t0=t0):
                    return [e.dma_start(out=o_s0[2 * mp2, :, t0:t0 + 256], in_=outst[:, 0:256]),
                            e.dma_start(out=o_s0[2 * mp2 + 1, :, t0:t0 + 256], in_=outst[:, 256:512])]
                em.step("sync", st2, dma=True, n_dma=2)


def emit_moe(em, io, HID, has_bias, post, HSPLIT=1, part="all", mode="dense"):
    nc = em.nc
    if part in ("all", "experts"):
        i_xn = io["xn_bf"]
        i_masks = io["masks"]
        i_scores = io["scores"]
        i_w1 = io["w1"]
        i_w2 = io["w2"]
        HH = HID // HSPLIT
        NKH = HH // 128
        NKO = HH // 128
        NSPL = 8 * HSPLIT

        acc_sb = em.sb("accm", [128, 8 * T], BF16)
        s_bc = em.sb("sbc", [128, 256])
        tstage = em.sb("tstage", [128, 512])
        gstage = em.sb("gstage", [128, 256], BF16)
        m_bc = em.sb("mbc", [128, 512])
        onesr = em.sb("onesrm", [1, 128])
        p_bc = em.ps("pbcm", [128, 512])

        em.step("vector", lambda e: e.memset(onesr[:, :], 1.0))
        em.step("vector", lambda e: e.memset(acc_sb[:, :], 0.0))
        ch0 = (em.ch, em.n)
        gch0 = (em.gch, em.gn)

        em._moe_acc = acc_sb
        if mode == "top1":
            _moe_top1_experts(em, io, HID, has_bias, HSPLIT)
        else:
            _moe_dense_experts(em, io, HID, has_bias, HSPLIT, acc_sb, m_bc, tstage, onesr, p_bc, ch0, gch0)
        em._moe_st = dict(acc_sb=acc_sb, s_bc=s_bc, tstage=tstage, gstage=gstage,
                          m_bc=m_bc, onesr=onesr, p_bc=p_bc)

    if part in ("all", "post"):
        _moe_post(em, io, has_bias and mode != "top1", post)


def _moe_top1_experts(em, io, HID, has_bias, HSPLIT):
    nc = em.nc
    i_idxf = io["idxf"]
    i_xtm = io["xn_tm"]
    go = io["gout_tm"]
    i_w1 = io["w1"]
    i_w2 = io["w2"]
    acc_sb = em._moe_acc
    HH = HID // HSPLIT
    NKH = HH // 128
    NKO = HH // 128
    NT2 = CAP // 512
    NSLOT = 8 * CAP
    NTT = T // 128

    with em.arena():
        sidb = em.sb("sidb5", [128, 128], BF16)
        sidf = em.sb("sidf5", [128, 128])
        idxst = em.sb("idxst", [16, 128])
        idxc = em.sb("idxc", [128, NTT], U32)
        idxcf = em.sb("idxcf", [128, NTT])
        idt = em.sb("idt", [128, 512])
        iot = em.sb("iot", [128, 512])
        xtm = em.sb("xtm5", [128, 16 * 1024], BF16)
        qb = em.sb("qb5", [128, 16 * 512], BF16)
        xin = em.sb("xin", [128, 2 * 1024], BF16)
        xout = em.sb("xout", [128, 4 * 1024], BF16)
        gxf = em.sb("gxf", [128, 8 * 512], BF16)
        h_sb = em.sb("hm5", [128, NKH * 512], BF16)
        obuf = em.sb("obuf", [128, 8 * 512], BF16)
        if HSPLIT > 1:
            oacc = em.sb("oacc", [128, 8 * 512])
        w1_sb = em.sb("w1t", [128, 2 * 8 * HH], BF16)
        w2_sb = em.sb("w2t", [128, NKO * 1024], BF16)
        if has_bias:
            b1c = em.sb("b1c5", [128, HID // 128])
            b2c = em.sb("b2c5", [128, 8])
        p_h = em.ps("ph5", [128, 2 * 512])
        p_out = em.ps("po5", [128, 2 * 512])
        p_gx = em.ps("pgx5", [128, 2 * 512])
        p_trb = em.ps("ptr5", [128, 8 * 128], BF16)

        em.step("sync", lambda e: e.dma_start(out=sidb[:, :], in_=io["identb"]), dma=True)
        em.step("sync", lambda e: e.dma_start(out=sidf[:, :], in_=io["identf"]), dma=True)
        em.step("sync", lambda e: e.dma_start(
            out=idxst[:, :], in_=i_idxf[0:1, :].rearrange("p (a b) -> (p a) b", a=16)), dma=True)
        em.step("tensor", lambda e: e.transpose(p_h[:, 0:16], idxst[0:16, :], sidf[0:16, 0:16]))
        em.step("vector", lambda e: e.tensor_copy(idxc[:, :], p_h[:, 0:16]))
        em.step("vector", lambda e: e.tensor_copy(idxcf[:, :], p_h[:, 0:16]))
        onesr5 = em.sb("onesr5", [1, 128])
        em.step("vector", lambda e: e.memset(onesr5[:, :], 1.0))
        em.step("sync", lambda e: e.dma_start(out=iot[0:1, :], in_=io["iota512"]), dma=True)
        em.step("tensor", lambda e: e.matmul(p_gx[:, 0:512], onesr5[0:1, :], iot[0:1, :],
                                             start=True, stop=True))
        em.step("vector", lambda e: e.tensor_copy(iot[:, :], p_gx[:, 0:512]))
        def ldxtm(e):
            return [e.dma_start(out=xtm[:, 1024 * c:1024 * (c + 1)],
                                in_=i_xtm[128 * c:128 * (c + 1), :]) for c in range(16)]
        em.step("sync", ldxtm, dma=True, n_dma=16)
        ch1 = (em.ch, em.n)
        gch1 = (em.gch, em.gn)

        SW = em.mksem("SW5")
        SB = em.mksem("SB5")
        SQB = em.mksem("SQB")
        SGE = em.mksem("SGE")
        STR = em.mksem("STR")
        SEV = em.mksem("SEV")
        SMMG = em.mksem("SMG5")
        SMMH = em.mksem("SMH5")
        SH = em.mksem("SH5")
        SMMO = em.mksem("SMO5")
        SOA = em.mksem("SOA")
        SST = em.mksem("SST")

        w1slot_free = [0, 0]
        w2_free = 0
        b1_free = 0
        b2_free = 0
        b_ready = 0
        gxf_free = 0
        qb_free = 0
        ptrb_free = 0
        xout_free = 0
        hsb_free = 0
        obuf_free = 0
        gxf_ready = 0
        gslot = 0

        for vi in range(8 * NT2):
            e_i, t2 = divmod(vi, NT2)
            base = float(e_i * CAP + 512 * t2)
            g0 = e_i * CAP + 512 * t2
            if has_bias and t2 == 0:
                em.stepd("sync", lambda e, e_i=e_i: e.dma_start(
                    out=b1c[:, :], in_=io["b1c"][e_i, :, :]),
                    deps=[(SH, b1_free), ch1], inc=SB, dma=True)
                em.stepd("sync", lambda e, e_i=e_i: e.dma_start(
                    out=b2c[:, :], in_=io["b2cc"][e_i, :, :]),
                    deps=[(SOA, b2_free)], inc=SB, dma=True)
                b_ready = SB.n
            em.stepd("vector", lambda e, base=base: e.tensor_scalar_add(idt[:, :], iot[:, :], base),
                     deps=[(SQB, max(SQB.n - 2, 0)), ch1], inc=SQB)
            for ch8 in range(2):
                def mkq(e, ch8=ch8):
                    r = None
                    for cc in range(8):
                        c = 8 * ch8 + cc
                        r = e.tensor_scalar(qb[:, 512 * c:512 * (c + 1)], idt[:, :],
                                            idxcf[:, c:c + 1], None, ALU.is_equal)
                    return r
                em.stepd("vector", mkq, deps=[(SMMG, qb_free), (SQB, SQB.n)], inc=SQB)
            q_ready = SQB.n
            for m in range(8):
                ps0 = 512 * (m % 2)
                def mmg(e, m=m, ps0=ps0):
                    r = None
                    for c in range(16):
                        r = e.matmul(p_gx[:, ps0:ps0 + 512],
                                     xtm[:, 1024 * c + 128 * m:1024 * c + 128 * (m + 1)],
                                     qb[:, 512 * c:512 * (c + 1)],
                                     start=(c == 0), stop=(c == 15))
                    return r
                em.stepd("tensor", mmg,
                         deps=[(SQB, q_ready), (SGE, max(SGE.n - 1, 0)), (SMMH, gxf_free), ch1],
                         inc=SMMG)
                def evgx(e, m=m, ps0=ps0):
                    return e.activation(out=gxf[:, 512 * m:512 * (m + 1)],
                                        in_=p_gx[:, ps0:ps0 + 512], func=AF.Copy)
                em.stepd("scalar", evgx, deps=[(SMMG, SMMG.n), (SMMH, gxf_free)], inc=SGE)
            qb_free = SMMG.n
            gxf_ready = SGE.n

            for hs in range(HSPLIT):
                slot = gslot % 2
                gslot += 1
                h0 = hs * HH

                def ldw1(e, e_i=e_i, h0=h0, slot=slot):
                    return [e.dma_start(out=w1_sb[:, slot * 8 * HH + HH * k:slot * 8 * HH + HH * (k + 1)],
                                        in_=i_w1[e_i, 128 * k:128 * (k + 1), h0:h0 + HH]) for k in range(8)]
                em.stepd("gpsimd", ldw1, deps=[(SMMH, w1slot_free[slot]), ch1, gch1],
                         inc=SW, dma=True, n_dma=8)
                w1_ready = SW.n

                def ldw2(e, e_i=e_i, h0=h0):
                    return [e.dma_start(out=w2_sb[:, 1024 * k:1024 * (k + 1)],
                                        in_=i_w2[e_i, h0 + 128 * k:h0 + 128 * (k + 1), :]) for k in range(NKO)]
                em.stepd("gpsimd", ldw2, deps=[(SMMO, w2_free)], inc=SW, dma=True, n_dma=NKO)
                w2_ready = SW.n

                for mh in range(NKH):
                    ps0 = 512 * (mh % 2)
                    def mmh(e, mh=mh, slot=slot, ps0=ps0):
                        r = None
                        for k in range(8):
                            c0 = slot * 8 * HH + HH * k + 128 * mh
                            r = e.matmul(p_h[:, ps0:ps0 + 512],
                                         w1_sb[:, c0:c0 + 128],
                                         gxf[:, 512 * k:512 * (k + 1)],
                                         start=(k == 0), stop=(k == 7))
                        return r
                    em.stepd("tensor", mmh,
                             deps=[(SW, w1_ready), (SGE, gxf_ready), (SH, max(SH.n - 1, 0))],
                             inc=SMMH)

                    def evh(e, mh=mh, hs=hs, ps0=ps0):
                        if has_bias:
                            return e.activation(out=h_sb[:, 512 * mh:512 * (mh + 1)],
                                                in_=p_h[:, ps0:ps0 + 512], func=AF.Gelu,
                                                bias=b1c[:, hs * NKH + mh:hs * NKH + mh + 1])
                        return e.activation(out=h_sb[:, 512 * mh:512 * (mh + 1)],
                                            in_=p_h[:, ps0:ps0 + 512], func=AF.Gelu)
                    em.stepd("scalar", evh,
                             deps=[(SMMH, SMMH.n), (SB, b_ready), (SMMO, hsb_free)], inc=SH)
                h_ready = SH.n

                for mo in range(8):
                    ps0 = 512 * (mo % 2)
                    def mmo(e, mo=mo, ps0=ps0):
                        r = None
                        for k in range(NKO):
                            r = e.matmul(p_out[:, ps0:ps0 + 512],
                                         w2_sb[:, 1024 * k + 128 * mo:1024 * k + 128 * (mo + 1)],
                                         h_sb[:, 512 * k:512 * (k + 1)],
                                         start=(k == 0), stop=(k == NKO - 1))
                        return r
                    em.stepd("tensor", mmo,
                             deps=[(SW, w2_ready), (SH, h_ready), (SOA, max(SOA.n - 1, 0))],
                             inc=SMMO)

                    def evo(e, mo=mo, e_i=e_i, hs=hs, ps0=ps0):
                        ob = obuf[:, 512 * mo:512 * (mo + 1)]
                        po = p_out[:, ps0:ps0 + 512]
                        if HSPLIT > 1 and hs == 0:
                            return e.tensor_copy(oacc[:, 512 * mo:512 * (mo + 1)], po)
                        elif HSPLIT > 1 and hs < HSPLIT - 1:
                            oa = oacc[:, 512 * mo:512 * (mo + 1)]
                            return e.tensor_add(oa, oa, po)
                        elif HSPLIT > 1:
                            oa = oacc[:, 512 * mo:512 * (mo + 1)]
                            e.tensor_add(oa, oa, po)
                            if has_bias:
                                return e.tensor_scalar_add(ob, oa, b2c[:, e_i:e_i + 1])
                            return e.tensor_copy(ob, oa)
                        else:
                            if has_bias:
                                return e.tensor_scalar_add(ob, po, b2c[:, e_i:e_i + 1])
                            return e.tensor_copy(ob, po)
                    deps_evo = [(SMMO, SMMO.n)]
                    if hs == HSPLIT - 1:
                        deps_evo.append((STR, obuf_free))
                    em.stepd("vector", evo, deps=deps_evo, inc=SOA)

                if hs == HSPLIT - 1:
                    obuf_ready = SOA.n
                    for qt in range(4):
                        def trout(e, qt=qt):
                            r = None
                            for mo in range(8):
                                r = e.transpose(
                                    p_trb[:, 128 * mo:128 * (mo + 1)],
                                    obuf[:, 512 * mo + 128 * qt:512 * mo + 128 * (qt + 1)],
                                    sidb[:, :])
                            return r
                        em.stepd("tensor", trout, deps=[(SOA, obuf_ready), (SEV, ptrb_free)], inc=STR)

                        def evout(e, qt=qt):
                            r = None
                            for mo in range(8):
                                r = e.tensor_copy(
                                    xout[:, 1024 * qt + 128 * mo:1024 * qt + 128 * (mo + 1)],
                                    p_trb[:, 128 * mo:128 * (mo + 1)])
                            return r
                        em.stepd("vector", evout, deps=[(STR, STR.n), (SST, xout_free)], inc=SEV)
                        ptrb_free = SEV.n
                    obuf_free = STR.n

                    def sto(e, g0=g0):
                        return [e.dma_start(out=go[g0 + 128 * j:g0 + 128 * (j + 1), :],
                                            in_=xout[:, 1024 * j:1024 * (j + 1)]) for j in range(4)]
                    em.stepd("sync", sto, deps=[(SEV, SEV.n)], inc=SST, dma=True, n_dma=4)
                    xout_free = SST.n
                hsb_free = SMMO.n
                w1slot_free[slot] = SMMH.n
                w2_free = SMMO.n
                if _T1DBG >= 2:
                    em.join([SW, SB, SQB, SGE, STR, SEV, SMMG, SMMH, SH, SMMO, SOA, SST])
            if _T1DBG == 1:
                em.join([SW, SB, SQB, SGE, STR, SEV, SMMG, SMMH, SH, SMMO, SOA, SST])
            gxf_free = SMMH.n
            if t2 == NT2 - 1:
                b1_free = SH.n
                b2_free = SOA.n

        em.join([SW, SB, SQB, SGE, STR, SEV, SMMG, SMMH, SH, SMMO, SOA, SST])

        SGM = em.mksem("SGM")
        SGD = em.mksem("SGD")
        SGT = em.mksem("SGT")
        SGV = em.mksem("SGV")
        trg_prev = 0
        ptrb2_free = 0
        for g2 in range(NTT // 2):
            em.stepd("vector", lambda e: e.memset(xin[:, 0:2048], 0.0),
                     deps=[(SGT, trg_prev)], inc=SGM)

            def gat(e, g2=g2):
                return [e.indirect_dma_start(
                    xin[0:128, 1024 * j:1024 * (j + 1)], None, go[0:NSLOT, :],
                    IndirectOffsetOnAxis(idxc[:, 2 * g2 + j:2 * g2 + j + 1], 0),
                    bounds_check=NSLOT - 1, oob_is_err=False) for j in range(2)]
            em.stepd("gpsimd", gat, deps=[(SGM, SGM.n), ch1], inc=SGD, dma=True, n_dma=2)

            for j2 in range(2):
                def trg(e, j2=j2):
                    r = None
                    for kf in range(8):
                        r = e.transpose(p_trb[:, 128 * kf:128 * (kf + 1)],
                                        xin[:, 1024 * j2 + 128 * kf:1024 * j2 + 128 * (kf + 1)],
                                        sidb[:, :])
                    return r
                em.stepd("tensor", trg, deps=[(SGD, SGD.n), (SGV, ptrb2_free)], inc=SGT)

                def evg(e, g2=g2, j2=j2):
                    r = None
                    t = 2 * g2 + j2
                    for kf in range(8):
                        r = e.tensor_copy(acc_sb[:, T * kf + 128 * t:T * kf + 128 * (t + 1)],
                                          p_trb[:, 128 * kf:128 * (kf + 1)])
                    return r
                em.stepd("vector", evg, deps=[(SGT, SGT.n)], inc=SGV)
                ptrb2_free = SGV.n
            trg_prev = SGT.n
        em.join([SGM, SGD, SGT, SGV])


def _moe_dense_experts(em, io, HID, has_bias, HSPLIT, acc_sb, m_bc, tstage, onesr, p_bc, ch0, gch0):
    nc = em.nc
    i_xn = io["xn_bf"]
    i_masks = io["masks"]
    i_w1 = io["w1"]
    i_w2 = io["w2"]
    HH = HID // HSPLIT
    NKH = HH // 128
    NKO = HH // 128
    NSPL = 8 * HSPLIT
    if True:
        with em.arena():
            w1_sb = em.sb("w1", [128, 2 * 8 * HH], BF16)
            w2_sb = em.sb("w2", [128, 2 * NKO * 1024], BF16)
            xn_sb = em.sb("xnm", [128, 2 * 8 * 512], BF16)
            h_sb = em.sb("hm", [128, NKH * 512], BF16)
            mrow = em.sb("mrow", [1, T])
            if has_bias:
                b1c = em.sb("b1c", [128, HID // 128])
            p_h = em.ps("ph", [128, 2 * 512])
            p_out = em.ps("poutm", [128, 2 * 512])

            SW = em.mksem("SW")
            SX = em.mksem("SX")
            SMR = em.mksem("SMR")
            SMMH = em.mksem("SMMH")
            SMMO = em.mksem("SMMO")
            SH = em.mksem("SH")
            SACC = em.mksem("SACC")
            SBC = em.mksem("SBC")
            SMBC = em.mksem("SMBC")

            wslot_free_h = [0, 0]
            wslot_free_o = [0, 0]
            xslot_free = [0, 0]
            mrow_free = 0
            b1c_free = 0
            smr_ready = 0
            gt = 0

            for si in range(NSPL):
                e_i, hs = divmod(si, HSPLIT)
                slot = si % 2
                h0 = hs * HH
                def ldw1(e, e_i=e_i, h0=h0, slot=slot):
                    return [e.dma_start(out=w1_sb[:, slot * 8 * HH + HH * k:slot * 8 * HH + HH * (k + 1)],
                                        in_=i_w1[e_i, 128 * k:128 * (k + 1), h0:h0 + HH]) for k in range(8)]
                em.stepd("gpsimd", ldw1, deps=[(SMMH, wslot_free_h[slot]), gch0, ch0], inc=SW, dma=True, n_dma=8)
                w1_ready = SW.n
                def ldw2(e, e_i=e_i, h0=h0, slot=slot):
                    return [e.dma_start(out=w2_sb[:, slot * NKO * 1024 + 1024 * k:slot * NKO * 1024 + 1024 * (k + 1)],
                                        in_=i_w2[e_i, h0 + 128 * k:h0 + 128 * (k + 1), :]) for k in range(NKO)]
                em.stepd("gpsimd", ldw2, deps=[(SMMO, wslot_free_o[slot])], inc=SW, dma=True, n_dma=NKO)
                w2_ready = SW.n
                if hs == 0:
                    em.stepd("sync", lambda e, e_i=e_i: e.dma_start(
                        out=mrow[0:1, :].rearrange("p (a b) -> p a b", a=1),
                        in_=i_masks[e_i:e_i + 1, :]), deps=[(SBC, mrow_free), ch0], inc=SMR, dma=True)
                    smr_ready = SMR.n
                    if has_bias:
                        em.stepd("sync", lambda e, e_i=e_i: e.dma_start(
                            out=b1c[:, :], in_=io["b1c"][e_i, :, :]),
                            deps=[(SH, b1c_free)], inc=SMR, dma=True)
                        smr_ready = SMR.n

                for t4 in range(4):
                    t0 = 512 * t4
                    xslot = gt % 2
                    def ldxn(e, t0=t0, xslot=xslot):
                        return [e.dma_start(out=xn_sb[:, xslot * 4096 + 512 * k:xslot * 4096 + 512 * (k + 1)],
                                            in_=i_xn[k, :, t0:t0 + 512]) for k in range(8)]
                    em.stepd("sync", ldxn, deps=[(SMMH, xslot_free[xslot]), ch0], inc=SX, dma=True, n_dma=8)
                    x_ready = SX.n
                    out_prev = SMMO.n
                    for mh in range(NKH):
                        pslot = (mh % 2) * 512
                        def mmh(e, mh=mh, pslot=pslot, slot=slot, xslot=xslot):
                            r = None
                            for k in range(8):
                                r = e.matmul(p_h[:, pslot:pslot + 512],
                                             w1_sb[:, slot * 8 * HH + HH * k + 128 * mh:slot * 8 * HH + HH * k + 128 * (mh + 1)],
                                             xn_sb[:, xslot * 4096 + 512 * k:xslot * 4096 + 512 * (k + 1)],
                                             start=(k == 0), stop=(k == 7))
                            return r
                        em.stepd("tensor", mmh,
                                 deps=[(SW, w1_ready), (SX, x_ready), (SH, max(SH.n - 1, 0))],
                                 inc=SMMH)
                        bia = hs * NKH + mh
                        if has_bias:
                            em.stepd("scalar", lambda e, mh=mh, pslot=pslot, bia=bia: e.activation(
                                out=h_sb[:, 512 * mh:512 * (mh + 1)], in_=p_h[:, pslot:pslot + 512],
                                func=AF.Gelu, bias=b1c[:, bia:bia + 1]),
                                deps=[(SMMH, SMMH.n), (SMMO, out_prev), (SMR, smr_ready)], inc=SH)
                        else:
                            em.stepd("scalar", lambda e, mh=mh, pslot=pslot: e.activation(
                                out=h_sb[:, 512 * mh:512 * (mh + 1)], in_=p_h[:, pslot:pslot + 512],
                                func=AF.Gelu),
                                deps=[(SMMH, SMMH.n), (SMMO, out_prev)], inc=SH)
                    xslot_free[xslot] = SMMH.n
                    h_ready = SH.n
                    em.stepd("tensor", lambda e, t0=t0: e.matmul(
                        p_bc[:, :], onesr[0:1, :], mrow[0:1, t0:t0 + 512], start=True, stop=True),
                        deps=[(SMR, smr_ready), (SMBC, SMBC.n), ch0], inc=SBC)
                    em.stepd("vector", lambda e: e.tensor_copy(m_bc[:, :], p_bc[:, :]),
                             deps=[(SBC, SBC.n)], inc=SMBC)
                    for mo in range(8):
                        pslot = (mo % 2) * 512
                        def mmo(e, mo=mo, pslot=pslot, slot=slot):
                            r = None
                            for k in range(NKO):
                                r = e.matmul(p_out[:, pslot:pslot + 512],
                                             w2_sb[:, slot * NKO * 1024 + 1024 * k + 128 * mo:slot * NKO * 1024 + 1024 * k + 128 * (mo + 1)],
                                             h_sb[:, 512 * k:512 * (k + 1)], start=(k == 0), stop=(k == NKO - 1))
                            return r
                        em.stepd("tensor", mmo,
                                 deps=[(SW, w2_ready), (SH, h_ready), (SACC, max(SACC.n - 1, 0))],
                                 inc=SMMO)
                        def accop(e, mo=mo, t0=t0, pslot=pslot):
                            e.tensor_mul(tstage[:, :], p_out[:, pslot:pslot + 512], m_bc[:, :])
                            return e.tensor_add(acc_sb[:, T * mo + t0:T * mo + t0 + 512],
                                                acc_sb[:, T * mo + t0:T * mo + t0 + 512], tstage[:, :])
                        em.stepd("vector", accop, deps=[(SMMO, SMMO.n), (SMBC, SMBC.n)], inc=SACC)
                    gt += 1
                wslot_free_h[slot] = SMMH.n
                wslot_free_o[slot] = SMMO.n
                mrow_free = SBC.n
                b1c_free = SH.n
            em.join([SW, SX, SMR, SMMH, SMMO, SH, SACC, SBC, SMBC])


def _moe_post(em, io, has_bias, post):
    nc = em.nc
    _st = em._moe_st
    acc_sb = _st["acc_sb"]; s_bc = _st["s_bc"]; tstage = _st["tstage"]
    gstage = _st["gstage"]; m_bc = _st["m_bc"]; onesr = _st["onesr"]
    p_bc = _st["p_bc"]
    i_masks = io["masks"]
    i_scores = io["scores"]
    if True:
        with em.arena():
            masks8 = em.sb("masks8", [8, T])
            srow = em.sb("srow", [1, T])
            p_out2 = em.ps("pout2", [128, 256])
            em.step("sync", lambda e: e.dma_start(out=masks8[0:8, :], in_=i_masks), dma=True)
            em.step("sync", lambda e: e.dma_start(out=srow[0:1, :].rearrange("p (a b) -> p a b", a=T // 128),
                                                  in_=i_scores), dma=True)
            if has_bias:
                b2r = em.sb("b2r", [8, 1024])
                em.step("sync", lambda e: e.dma_start(out=b2r[:, :], in_=io["b2"]), dma=True)
            if post == "l0":
                o_g = io["g"]
                if not has_bias:
                    tst8 = em.sb("tst8", [128, 8 * 256])
                    gst8 = em.sb("gst8", [128, 8 * 256], BF16)
                for t8 in range(8):
                    t0 = 256 * t8
                    em.step("tensor", lambda e, t0=t0: e.matmul(
                        p_bc[:, 0:256], onesr[0:1, :], srow[0:1, t0:t0 + 256], start=True, stop=True))
                    em.step("vector", lambda e: e.tensor_copy(s_bc[:, :], p_bc[:, 0:256]))
                    if has_bias:
                        for mo in range(8):
                            def mmb2(e, mo=mo, t0=t0):
                                return e.matmul(p_out2[:, :], b2r[0:8, 128 * mo:128 * (mo + 1)],
                                                masks8[0:8, t0:t0 + 256], start=True, stop=True)
                            em.step("tensor", mmb2)
                            em.step("vector", lambda e, mo=mo, t0=t0: e.tensor_add(
                                tstage[:, 0:256], acc_sb[:, T * mo + t0:T * mo + t0 + 256], p_out2[:, :]))
                            em.step("vector", lambda e: e.tensor_mul(tstage[:, 0:256], tstage[:, 0:256], s_bc[:, :]))
                            em.step("scalar", lambda e: e.activation(out=gstage[:, :], in_=tstage[:, 0:256], func=AF.Silu))
                            em.step("sync", lambda e, mo=mo, t0=t0: e.dma_start(
                                out=o_g[mo, :, t0:t0 + 256], in_=gstage[:, :]), dma=True)
                    else:
                        def mul8(e, t0=t0):
                            r = None
                            for mo in range(8):
                                r = e.tensor_mul(tst8[:, 256 * mo:256 * (mo + 1)],
                                                 acc_sb[:, T * mo + t0:T * mo + t0 + 256], s_bc[:, :])
                            return r
                        em.step("vector", mul8)
                        em.step("scalar", lambda e: e.activation(out=gst8[:, :], in_=tst8[:, :], func=AF.Silu))
                        def st8(e, t0=t0):
                            return [e.dma_start(out=o_g[mo, :, t0:t0 + 256],
                                                in_=gst8[:, 256 * mo:256 * (mo + 1)]) for mo in range(8)]
                        em.step("sync", st8, dma=True, n_dma=8)
            else:
                o_x2 = io["x2_fm"]
                i_x1 = io["x1_fm"]
                i_s1 = io["s1"]
                gam = em.sb("gam", [128, 8])
                bet = em.sb("bet", [128, 8])
                onesc = em.sb("onescm", [128, 1])
                x1_sb = em.sb("x1sb", [128, 256])
                s1_sb = em.sb("s1sb", [128, 256], BF16)
                as_sb = em.sb("assb", [128, 8 * 256])
                sq_sb = em.sb("sqm", [128, 8 * 256])
                rows = em.sb("rowsm", [1, 6 * 256])
                p_st1 = em.ps("pst1", [1, 256])
                p_st2 = em.ps("pst2", [1, 256])
                em.step("vector", lambda e: e.memset(onesc[:, :], 1.0))
                em.step("sync", lambda e: e.dma_start(out=gam[:, :], in_=io["gamma"]), dma=True)
                em.step("sync", lambda e: e.dma_start(out=bet[:, :], in_=io["beta"]), dma=True)
                for t8 in range(8):
                    t0 = 256 * t8
                    em.step("tensor", lambda e, t0=t0: e.matmul(
                        p_bc[:, 0:256], onesr[0:1, :], srow[0:1, t0:t0 + 256], start=True, stop=True))
                    em.step("vector", lambda e: e.tensor_copy(s_bc[:, :], p_bc[:, 0:256]))
                    for mo in range(8):
                        em.step("vector", lambda e, mo=mo, t0=t0: e.tensor_mul(
                            as_sb[:, 256 * mo:256 * (mo + 1)],
                            acc_sb[:, T * mo + t0:T * mo + t0 + 256], s_bc[:, :]))
                    em.step("scalar", lambda e: e.activation(out=sq_sb[:, :], in_=as_sb[:, :], func=AF.Square))
                    mm_ones_sums(em, p_st1, onesc, [as_sb[:, 256 * k:256 * (k + 1)] for k in range(8)])
                    mm_ones_sums(em, p_st2, onesc, [sq_sb[:, 256 * k:256 * (k + 1)] for k in range(8)])
                    mu = rows[0:1, 0:256]
                    var = rows[0:1, 256:512]
                    rstd = rows[0:1, 512:768]
                    scr = rows[0:1, 768:1024]
                    em.step("vector", lambda e: e.tensor_scalar_mul(mu, p_st1[:, :], 1.0 / D))
                    em.step("vector", lambda e: e.tensor_scalar_mul(var, p_st2[:, :], 1.0 / D))
                    em.step("scalar", lambda e: e.activation(out=scr, in_=mu, func=AF.Square))
                    em.step("vector", lambda e: e.tensor_sub(var, var, scr))
                    rstd_from_var(em, rstd, var, 1e-5)
                    bcast_row(em, p_bc[:, 0:256], onesr, mu)
                    em.step("vector", lambda e: e.tensor_copy(m_bc[:, 0:256], p_bc[:, 0:256]))
                    bcast_row(em, p_bc[:, 0:256], onesr, rstd)
                    for mo in range(8):
                        def post1(e, mo=mo, t0=t0):
                            e.tensor_sub(tstage[:, 0:256], as_sb[:, 256 * mo:256 * (mo + 1)], m_bc[:, 0:256])
                            e.tensor_mul(tstage[:, 0:256], tstage[:, 0:256], p_bc[:, 0:256])
                            return e.tensor_scalar(tstage[:, 0:256], tstage[:, 0:256], gam[:, mo:mo + 1],
                                                   bet[:, mo:mo + 1], ALU.mult, ALU.add)
                        em.step("vector", post1)
                        em.step("scalar", lambda e: e.activation(out=gstage[:, :], in_=tstage[:, 0:256], func=AF.Silu))
                        def fin(e, mo=mo, t0=t0):
                            rs = [e.dma_start(out=x1_sb[:, :], in_=i_x1[128 * mo:128 * (mo + 1), t0:t0 + 256]),
                                  e.dma_start(out=s1_sb[:, :], in_=i_s1[mo, :, t0:t0 + 256])]
                            return rs
                        em.step("sync", fin, dma=True, n_dma=2)
                        def resid(e, mo=mo):
                            e.tensor_mul(tstage[:, 0:256], s1_sb[:, :], gstage[:, :])
                            return e.tensor_add(tstage[:, 0:256], tstage[:, 0:256], x1_sb[:, :])
                        em.step("vector", resid)
                        em.step("sync", lambda e, mo=mo, t0=t0: e.dma_start(
                            out=o_x2[128 * mo:128 * (mo + 1), t0:t0 + 256], in_=tstage[:, 0:256]), dma=True)



def emit_s7(em, io):
    nc = em.nc
    with em.arena():
        i_xn = io["xn_bf"]
        i_wq, i_wk, i_wv = io["WqT"], io["WkT"], io["WvT"]
        i_cw = io["cwc"]
        i_cb = io["cbc"]
        identb = io["identb"]
        o_q = io["q_fm"]
        o_kv = io["kv_part"]
        o_kst = io["kst_part"]
        o_loc = io["local"]

        wq_sb = em.sb("wq", [128, 8 * 1024], BF16)
        wk_sb = em.sb("wk", [128, 8 * 1024], BF16)
        wv_sb = em.sb("wv", [128, 8 * 1024], BF16)
        xn_sb = em.sb("xn7", [128, 8 * 514], BF16)
        q_sb = em.sb("q7", [128, 8 * 512], BF16)
        k_sb = em.sb("k7", [128, 8 * 512], BF16)
        v_sb = em.sb("v7", [128, 8 * 512], BF16)
        ktm_sb = em.sb("ktm", [128, 4 * 1024], BF16)
        vtm_sb = em.sb("vtm", [128, 4 * 1024], BF16)
        ks_acc = em.sb("ksacc", [128, 8])
        kred = em.sb("kred", [128, 8])
        kv_sb = em.sb("kvsb", [128, 512])
        cw_sb = em.sb("cw7", [128, 24])
        cb_sb = em.sb("cb7", [128, 8])
        sid = em.sb("sid7", [128, 128], BF16)
        ta = em.sb("t7a", [128, 4 * 512])
        tb = em.sb("t7b", [128, 4 * 512])
        lstage = em.sb("lst", [128, 8 * 512], BF16)

        p_mm = em.ps("pmm7", [128, 4 * 512])
        p_kv = em.ps("pkv", [128, 512])
        p_trb = em.ps("ptr7", [128, 8 * 128], BF16)

        em.step("vector", lambda e: e.memset(ks_acc[:, :], 0.0))
        em.step("vector", lambda e: e.memset(xn_sb[:, :], 0.0))
        em.step("sync", lambda e: e.dma_start(out=sid[:, :], in_=identb), dma=True)
        em.step("sync", lambda e: e.dma_start(out=cw_sb[:, 0:24], in_=i_cw), dma=True)
        em.step("sync", lambda e: e.dma_start(out=cb_sb[:, :], in_=i_cb), dma=True)
        def ldw(e):
            rs = []
            for k in range(8):
                rs.append(e.dma_start(out=wq_sb[:, 1024 * k:1024 * (k + 1)], in_=i_wq[128 * k:128 * (k + 1), :]))
                rs.append(e.dma_start(out=wk_sb[:, 1024 * k:1024 * (k + 1)], in_=i_wk[128 * k:128 * (k + 1), :]))
                rs.append(e.dma_start(out=wv_sb[:, 1024 * k:1024 * (k + 1)], in_=i_wv[128 * k:128 * (k + 1), :]))
            return rs
        em.step("gpsimd", ldw, dma=True, n_dma=24)

        for it in range(4):
            t0 = it * 512
            lo = max(t0 - 1, 0)
            hi = min(t0 + 513, T)
            off = lo - (t0 - 1)
            def ldxn(e, t0=t0, lo=lo, hi=hi, off=off):
                return [e.dma_start(out=xn_sb[:, 514 * k + off:514 * k + off + (hi - lo)],
                                    in_=i_xn[k, :, lo:hi]) for k in range(8)]
            em.step("sync", ldxn, dma=True, n_dma=8)
            def conv(e):
                r = None
                for m in range(8):
                    base = 514 * m
                    e.tensor_scalar_mul(ta[:, 512 * (m % 4):512 * (m % 4 + 1)] if False else tb[:, 0:512],
                                        xn_sb[:, base:base + 512], cw_sb[:, 3 * m:3 * m + 1]) if False else None
                for m in range(8):
                    base = 514 * m
                    e.tensor_scalar_mul(ta[:, 0:512], xn_sb[:, base:base + 512], cw_sb[:, 3 * m:3 * m + 1])
                    e.scalar_tensor_tensor(ta[:, 0:512], xn_sb[:, base + 1:base + 513],
                                           cw_sb[:, 3 * m + 1:3 * m + 2], ta[:, 0:512], ALU.mult, ALU.add)
                    e.scalar_tensor_tensor(ta[:, 0:512], xn_sb[:, base + 2:base + 514],
                                           cw_sb[:, 3 * m + 2:3 * m + 3], ta[:, 0:512], ALU.mult, ALU.add)
                    r = e.tensor_scalar_add(lstage[:, 512 * m:512 * (m + 1)], ta[:, 0:512], cb_sb[:, m:m + 1])
                return r
            em.step("vector", conv)
            def stloc(e, t0=t0):
                return [e.dma_start(out=o_loc[m, :, t0:t0 + 512], in_=lstage[:, 512 * m:512 * (m + 1)])
                        for m in range(8)]
            em.step("sync", stloc, dma=True, n_dma=8)
            for which, wsb, dst in (("q", wq_sb, q_sb), ("k", wk_sb, k_sb), ("v", wv_sb, v_sb)):
                for mb in range(0, 8, 4):
                    def mmp(e, wsb=wsb, mb=mb):
                        r = None
                        for mi in range(4):
                            m = mb + mi
                            for k in range(8):
                                r = e.matmul(p_mm[:, 512 * mi:512 * (mi + 1)],
                                             wsb[:, 1024 * k + 128 * m:1024 * k + 128 * (m + 1)],
                                             xn_sb[:, 514 * k + 1:514 * k + 513], start=(k == 0), stop=(k == 7))
                        return r
                    em.step("tensor", mmp)
                    if which == "v":
                        def evv(e, mb=mb, dst=dst):
                            r = None
                            for mi in range(4):
                                r = e.tensor_copy(dst[:, 512 * (mb + mi):512 * (mb + mi + 1)],
                                                  p_mm[:, 512 * mi:512 * (mi + 1)])
                            return r
                        em.step("vector", evv)
                    else:
                        def ev1(e):
                            r = None
                            for mi in range(4):
                                e.tensor_scalar_min(ta[:, 512 * mi:512 * (mi + 1)], p_mm[:, 512 * mi:512 * (mi + 1)], 0.0)
                                r = e.tensor_relu(tb[:, 512 * mi:512 * (mi + 1)], p_mm[:, 512 * mi:512 * (mi + 1)])
                            return r
                        em.step("vector", ev1)
                        def ev2(e):
                            r = None
                            for mi in range(4):
                                r = e.activation(out=ta[:, 512 * mi:512 * (mi + 1)],
                                                 in_=ta[:, 512 * mi:512 * (mi + 1)], func=AF.Exp)
                            return r
                        em.step("scalar", ev2)
                        def ev3(e, mb=mb, dst=dst):
                            r = None
                            for mi in range(4):
                                r = e.tensor_add(dst[:, 512 * (mb + mi):512 * (mb + mi + 1)],
                                                 ta[:, 512 * mi:512 * (mi + 1)], tb[:, 512 * mi:512 * (mi + 1)])
                            return r
                        em.step("vector", ev3)
            def stq(e, t0=t0):
                return [e.dma_start(out=o_q[m, :, t0:t0 + 512], in_=q_sb[:, 512 * m:512 * (m + 1)])
                        for m in range(8)]
            em.step("sync", stq, dma=True, n_dma=8)
            def kst(e, it=it):
                r = None
                for m in range(8):
                    e.reduce_sum(kred[:, m:m + 1], k_sb[:, 512 * m:512 * (m + 1)], axis=AX.X)
                for m in range(8):
                    r = e.tensor_add(ks_acc[:, m:m + 1], ks_acc[:, m:m + 1], kred[:, m:m + 1])
                return r
            em.step("vector", kst)
            for srcb, dstb in ((k_sb, ktm_sb), (v_sb, vtm_sb)):
                for half in range(4):
                    idx = [(m, jb) for m in range(8) for jb in range(4)][8 * half:8 * (half + 1)]
                    def trx(e, idx=idx, srcb=srcb):
                        r = None
                        for oi, (m, jb) in enumerate(idx):
                            r = e.transpose(p_trb[:, 128 * oi:128 * (oi + 1)],
                                            srcb[:, 512 * m + 128 * jb:512 * m + 128 * (jb + 1)], sid[:, :])
                        return r
                    em.step("tensor", trx)
                    def evt(e, idx=idx, dstb=dstb):
                        r = None
                        for oi, (m, jb) in enumerate(idx):
                            r = e.tensor_copy(dstb[:, 1024 * jb + 128 * m:1024 * jb + 128 * (m + 1)],
                                              p_trb[:, 128 * oi:128 * (oi + 1)])
                        return r
                    em.step("vector", evt)
            def mmkv(e, it=it):
                r = None
                for h in range(16):
                    pb = (h % 2) * 64
                    cb_ = 64 * (h // 2)
                    for jb in range(4):
                        r = e.matmul(p_kv[pb:pb + 64, cb_:cb_ + 64],
                                     ktm_sb[:, 1024 * jb + 64 * h:1024 * jb + 64 * (h + 1)],
                                     vtm_sb[:, 1024 * jb + 64 * h:1024 * jb + 64 * (h + 1)],
                                     start=(jb == 0), stop=(jb == 3))
                return r
            em.step("tensor", mmkv)
            if it == 0:
                em.step("vector", lambda e: e.tensor_copy(kv_sb[:, :], p_kv[:, :]))
            else:
                em.step("vector", lambda e: e.tensor_add(kv_sb[:, :], kv_sb[:, :], p_kv[:, :]))

        em.step("sync", lambda e: e.dma_start(out=o_kv, in_=kv_sb[:, :]), dma=True)
        em.step("sync", lambda e: e.dma_start(out=o_kst, in_=ks_acc[:, :]), dma=True)


def emit_s9(em, io):
    nc = em.nc
    with em.arena():
        i_q = io["q_fm"]
        i_kvm = io["kv_mine"]
        i_kvp = io["kv_peer"]
        i_ksm = io["kst_mine"]
        i_ksp = io["kst_peer"]
        i_loc = io["local"]
        i_halo = io.get("halo_cols")
        i_cw = io["cwc"]
        i_wp = io["WpT"]
        i_bp = io["bpc"]
        o_s1 = io["s1"]

        wp_sb = em.sb("wp", [128, 8 * 1024], BF16)
        q_sb = em.sb("q9", [128, 8 * 512], BF16)
        kv_sb = em.sb("kv9", [128, 512])
        kv_bf = em.sb("kvbf", [128, 512], BF16)
        ks_sb = em.sb("ks9", [128, 8])
        ksp_sb = em.sb("ksp9", [128, 8])
        ksd = em.sb("ksd", [128, 8 * 16], BF16)
        halo_sb = em.sb("halo", [128, 8 * 2], BF16)
        cw_sb = em.sb("cw9", [128, 24])
        loc_sb = em.sb("loc9", [128, 8 * 512], BF16)
        r_sb = em.sb("r9", [16, 512])
        r_flat = em.sb("rflat", [1, 8192])
        u_sb = em.sb("u9", [128, 512])
        att_sb = em.sb("att9", [128, 8 * 512], BF16)
        bp_sb = em.sb("bp9", [128, 8])
        onesr = em.sb("onesr9", [1, 128])
        sstage = em.sb("sst9", [128, 512], BF16)

        p_den = em.ps("pden", [16, 512])
        p_at = em.ps("pat", [128, 512])
        p_rb = em.ps("prb9", [128, 512])
        p_s1 = em.ps("ps19", [128, 512])

        em.step("vector", lambda e: e.memset(onesr[:, :], 1.0))
        em.step("vector", lambda e: e.memset(ksd[:, :], 0.0))
        em.step("sync", lambda e: e.dma_start(out=kv_sb[:, :], in_=i_kvm), dma=True)
        em.step("sync", lambda e: e.dma_start(out=u_sb[:, :], in_=i_kvp), dma=True)
        em.step("vector", lambda e: e.tensor_add(kv_sb[:, :], kv_sb[:, :], u_sb[:, :]))
        em.step("vector", lambda e: e.tensor_copy(kv_bf[:, :], kv_sb[:, :]))
        em.step("sync", lambda e: e.dma_start(out=ks_sb[:, :], in_=i_ksm), dma=True)
        em.step("sync", lambda e: e.dma_start(out=ksp_sb[:, :], in_=i_ksp), dma=True)
        em.step("vector", lambda e: e.tensor_add(ks_sb[:, :], ks_sb[:, :], ksp_sb[:, :]))
        if "halo_raw" in io:
            raw_sb = em.sb("haloraw", [128, 16], BF16)
            fo_sb = em.sb("fodd", [128, 1])
            fe_sb = em.sb("feven", [128, 1])
            em.step("sync", lambda e: e.dma_start(out=raw_sb[:, :], in_=io["halo_raw"]), dma=True)
            em.step("sync", lambda e: e.dma_start(out=fo_sb[:, :], in_=io["fodd"]), dma=True)
            em.step("sync", lambda e: e.dma_start(out=fe_sb[:, :], in_=io["feven"]), dma=True)
            def mkhalo(e):
                r = None
                for m in range(8):
                    e.tensor_scalar_mul(halo_sb[:, 2 * m:2 * m + 1], raw_sb[:, 2 * m + 1:2 * m + 2], fo_sb[:, 0:1])
                    r = e.tensor_scalar_mul(halo_sb[:, 2 * m + 1:2 * m + 2], raw_sb[:, 2 * m:2 * m + 1], fe_sb[:, 0:1])
                return r
            em.step("vector", mkhalo)
        else:
            em.step("sync", lambda e: e.dma_start(out=halo_sb[:, 0:16].rearrange("b (a c) -> b a c", a=8), in_=i_halo.rearrange("a b c -> b a c")), dma=True)
        em.step("sync", lambda e: e.dma_start(out=cw_sb[:, 0:24], in_=i_cw), dma=True)
        em.step("sync", lambda e: e.dma_start(out=bp_sb[:, :], in_=i_bp), dma=True)
        for k in range(8):
            em.step("gpsimd", lambda e, k=k: e.dma_start(
                out=wp_sb[:, 1024 * k:1024 * (k + 1)], in_=i_wp[128 * k:128 * (k + 1), :]), dma=True)
        def mkksd(e):
            r = None
            for kb in range(8):
                e.tensor_copy(ksd[0:64, kb * 16 + 2 * kb:kb * 16 + 2 * kb + 1], ks_sb[0:64, kb:kb + 1])
                r = e.tensor_copy(ksd[64:128, kb * 16 + 2 * kb + 1:kb * 16 + 2 * kb + 2], ks_sb[64:128, kb:kb + 1])
            return r
        em.step("vector", mkksd)

        for it in range(4):
            t0 = it * 512
            def ldq(e, t0=t0):
                return [e.dma_start(out=q_sb[:, 512 * m:512 * (m + 1)], in_=i_q[m, :, t0:t0 + 512])
                        for m in range(8)]
            em.step("sync", ldq, dma=True, n_dma=8)
            def ldloc(e, t0=t0):
                return [e.dma_start(out=loc_sb[:, 512 * m:512 * (m + 1)], in_=i_loc[m, :, t0:t0 + 512])
                        for m in range(8)]
            em.step("sync", ldloc, dma=True, n_dma=8)
            if it == 0:
                def fixl(e):
                    r = None
                    for m in range(8):
                        r = e.scalar_tensor_tensor(loc_sb[:, 512 * m:512 * m + 1],
                                                   halo_sb[:, 2 * m:2 * m + 1], cw_sb[:, 3 * m:3 * m + 1],
                                                   loc_sb[:, 512 * m:512 * m + 1], ALU.mult, ALU.add)
                    return r
                em.step("vector", fixl)
            if it == 3:
                def fixr(e):
                    r = None
                    for m in range(8):
                        r = e.scalar_tensor_tensor(loc_sb[:, 512 * m + 511:512 * m + 512],
                                                   halo_sb[:, 2 * m + 1:2 * m + 2], cw_sb[:, 3 * m + 2:3 * m + 3],
                                                   loc_sb[:, 512 * m + 511:512 * m + 512], ALU.mult, ALU.add)
                    return r
                em.step("vector", fixr)
            def mmden(e):
                r = None
                for kb in range(8):
                    r = e.matmul(p_den[:, :], ksd[:, 16 * kb:16 * (kb + 1)],
                                 q_sb[:, 512 * kb:512 * (kb + 1)], start=(kb == 0), stop=(kb == 7))
                return r
            em.step("tensor", mmden)
            em.step("vector", lambda e: e.tensor_scalar_max(r_sb[:, :], p_den[:, :], 1e-6))
            em.step("vector", lambda e: e.reciprocal(r_sb[:, :], r_sb[:, :]))
            em.step("sync", lambda e: e.dma_start(
                out=r_flat[0:1, :].rearrange("p (a b) -> p a b", a=16), in_=r_sb[:, :]), dma=True)
            for kb in range(8):
                def mmat(e, kb=kb):
                    h1, h2 = 2 * kb, 2 * kb + 1
                    e.matmul(p_at[0:64, :], kv_bf[0:64, 64 * kb:64 * (kb + 1)],
                             q_sb[0:64, 512 * kb:512 * (kb + 1)], start=True, stop=True)
                    return e.matmul(p_at[64:128, :], kv_bf[64:128, 64 * kb:64 * (kb + 1)],
                                    q_sb[64:128, 512 * kb:512 * (kb + 1)], start=True, stop=True)
                em.step("tensor", mmat)
                def mmrb(e, kb=kb):
                    e.matmul(p_rb[0:64, :], onesr[0:1, 0:64], r_flat[0:1, 512 * (2 * kb):512 * (2 * kb) + 512],
                             start=True, stop=True)
                    return e.matmul(p_rb[64:128, :], onesr[0:1, 0:64], r_flat[0:1, 512 * (2 * kb + 1):512 * (2 * kb + 1) + 512],
                                    start=True, stop=True)
                em.step("tensor", mmrb)
                em.step("vector", lambda e: e.tensor_copy(u_sb[:, :], p_at[:, :]))
                em.step("vector", lambda e: e.tensor_mul(u_sb[:, :], u_sb[:, :], p_rb[:, :]))
                em.step("vector", lambda e, kb=kb: e.tensor_add(
                    att_sb[:, 512 * kb:512 * (kb + 1)], u_sb[:, :], loc_sb[:, 512 * kb:512 * (kb + 1)]))
            for m in range(8):
                def mms1(e, m=m):
                    r = None
                    for k in range(8):
                        r = e.matmul(p_s1[:, :], wp_sb[:, 1024 * k + 128 * m:1024 * k + 128 * (m + 1)],
                                     att_sb[:, 512 * k:512 * (k + 1)], start=(k == 0), stop=(k == 7))
                    return r
                em.step("tensor", mms1)
                em.step("vector", lambda e, m=m: e.tensor_scalar_add(sstage[:, :], p_s1[:, :], bp_sb[:, m:m + 1]))
                em.step("sync", lambda e, m=m, t0=t0: e.dma_start(
                    out=o_s1[m, :, t0:t0 + 512], in_=sstage[:, :]), dma=True)




MAGIC1, MAGIC2 = 0xA5A50001, 0xA5A50002


_RVC = [0]


def _rv(e, tile, col, mx):
    from concourse.expressions import RuntimeValue
    _RVC[0] += 1
    r = e.alloc_register("xrv%d" % _RVC[0])
    e.reg_load(r, tile[0:1, col:col + 1])
    return RuntimeValue(val=r, min_val=0, max_val=mx)


def emit_exchange(em, io, phase, part="all"):
    nc = em.nc
    if not hasattr(em, "_xch"):
        em._xch = {}
        em._xb = {"fo": em.sb("xfo", [128, 1]), "fe": em.sb("xfe", [128, 1])}
        em.step("sync", lambda e: e.dma_start(out=em._xb["fo"][:, :], in_=io["fodd"]), dma=True)
        em.step("sync", lambda e: e.dma_start(out=em._xb["fe"][:, :], in_=io["feven"]), dma=True)
    st = em._xch.get(phase)
    if st is None:
        st = {}
        st["psem"] = em.sem("psem")
        st["par_sb"] = em.sb("par", [1, 8], U32)
        st["mg"] = em.sb("mg", [1, 1], U32)
        st["flg"] = em.sb("flg", [1, 4], U32)
        if phase == 2:
            st["hb"] = em.sb("xhalo", [128, 16], BF16)
            st["hf"] = em.sb("xhf", [128, 16])
        em._xch[phase] = st
        xpar = io["xpar"]
        em.step("sync", lambda e: e.dma_start(out=st["par_sb"][:, :], in_=xpar), dma=True)
        em.step("vector", lambda e: e.memset(st["mg"][:, :], MAGIC1 if phase == 1 else MAGIC2))
    shf = io["shflags"]
    psem = st["psem"]
    par_sb = st["par_sb"]
    mg = st["mg"]
    flg = st["flg"]
    fo, fe = em._xb["fo"], em._xb["fe"]
    magic = MAGIC1 if phase == 1 else MAGIC2
    fcols = (0, 1) if phase == 1 else (2, 3)
    fcol_par = 2 if phase == 1 else 4

    def xpoll(e, ch, need, gch, gneed):
        e.wait_ge(ch, need)
        e.wait_ge(gch, gneed)
        tag = f"{phase}p"
        with e.register("nd" + tag) as nd, e.register("pc" + tag) as pc, \
                e.register("fl" + tag) as fl:
            e.reg_mov(pc, 0)
            for fc in fcols:
                e.reg_mov(nd, 1)
                with e.While(nd):
                    e.reg_add(pc, pc, 16)
                    e.dma_start(out=flg[:, :], in_=shf[0:1, 0:4]).then_inc(psem, 16)
                    e.wait_ge(psem, pc)
                    e.reg_load(fl, flg[0:1, fc:fc + 1])
                    e.reg_alu(nd, fl, magic, ALU.not_equal)
        return e.sem_inc(ch, 1)

    def selchunk(e, cw):
        e.tensor_scalar(xa[:, 0:cw], xa[:, 0:cw], fo[:, 0:1], None, ALU.mult)
        e.tensor_scalar(xb[:, 0:cw], xb[:, 0:cw], fe[:, 0:1], None, ALU.mult)
        return e.tensor_add(xa[:, 0:cw], xa[:, 0:cw], xb[:, 0:cw])

    if phase == 1:
        if part in ("all", "send"):
            def xsend(e, ch, need, gch, gneed):
                e.wait_ge(ch, need)
                e.wait_ge(gch, gneed)
                myrow = _rv(e, par_sb, 0, 128)
                myf = _rv(e, par_sb, fcol_par, 3)
                e.dma_start(out=io["sh_sout"][DS(myrow, 128), :], in_=io["S_out"]).then_inc(psem, 16)
                e.wait_ge(psem, 16)
                e.dma_start(out=shf[0:1, DS(myf, 1)], in_=mg[:, :]).then_inc(psem, 16)
                e.wait_ge(psem, 32)
                return e.sem_inc(ch, 1)
            em.step_manual("gpsimd", xsend, n_inc=1)
        if part in ("all", "recv"):
            em.step_manual("gpsimd", xpoll, n_inc=1)
            shs = io["sh_sout"]
            with em.arena():
                xa = em.sb("xba", [128, 512])
                xb = em.sb("xbb", [128, 512])
                _emit_sel1(em, io, shs, xa, xb, fo, fe)
    else:
        if part in ("all", "send"):
            hb = st["hb"]
            hf = st["hf"]
            i_xn1 = io["xn1_bf"]
            def ldedge(e):
                rs = []
                for m in range(8):
                    rs.append(e.dma_start(out=hb[:, 2 * m:2 * m + 1], in_=i_xn1[m, :, 0:1],
                                          allow_slow_non_contiguous=True))
                    rs.append(e.dma_start(out=hb[:, 2 * m + 1:2 * m + 2], in_=i_xn1[m, :, T - 1:T],
                                          allow_slow_non_contiguous=True))
                return rs
            em.step("sync", ldedge, dma=True, n_dma=16)
            em.step("vector", lambda e: e.tensor_copy(hf[:, :], hb[:, :]))
            em.step("sync", lambda e: e.dma_start(out=io["pay2"][0:128, 520:536], in_=hf[:, :]), dma=True)

            def xsend(e, ch, need, gch, gneed):
                e.wait_ge(ch, need)
                e.wait_ge(gch, gneed)
                myrow = _rv(e, par_sb, 0, 128)
                myf = _rv(e, par_sb, fcol_par, 3)
                e.dma_start(out=io["sh_x2"][DS(myrow, 128), :], in_=io["pay2"]).then_inc(psem, 16)
                e.wait_ge(psem, 16)
                e.dma_start(out=shf[0:1, DS(myf, 1)], in_=mg[:, :]).then_inc(psem, 16)
                e.wait_ge(psem, 32)
                return e.sem_inc(ch, 1)
            em.step_manual("gpsimd", xsend, n_inc=1)
        if part in ("all", "recv"):
            em.step_manual("gpsimd", xpoll, n_inc=1)
            shx = io["sh_x2"]
            with em.arena():
                xa = em.sb("xba2", [128, 512])
                xb = em.sb("xbb2", [128, 512])
                h16 = em.sb("xh16", [128, 16], BF16)
                _emit_sel2(em, io, shx, xa, xb, h16, fo, fe)


def _emit_sel1(em, io, shs, xa, xb, fo, fe):
    def selchunk(e, cw):
        e.tensor_scalar(xa[:, 0:cw], xa[:, 0:cw], fo[:, 0:1], None, ALU.mult)
        e.tensor_scalar(xb[:, 0:cw], xb[:, 0:cw], fe[:, 0:1], None, ALU.mult)
        return e.tensor_add(xa[:, 0:cw], xa[:, 0:cw], xb[:, 0:cw])
    if True:
        if True:
            for cc in range(4):
                c0 = 512 * cc
                def ld(e, c0=c0, shs=shs):
                    return [e.dma_start(out=xa[:, 0:512], in_=shs[0:128, c0:c0 + 512]),
                            e.dma_start(out=xb[:, 0:512], in_=shs[128:256, c0:c0 + 512])]
                em.step("sync", ld, dma=True, n_dma=2)
                em.step("vector", lambda e: selchunk(e, 512))
                em.step("sync", lambda e, c0=c0: e.dma_start(
                    out=io["S_in"][0:128, c0:c0 + 512], in_=xa[:, 0:512]), dma=True)


def _emit_sel2(em, io, shx, xa, xb, h16, fo, fe):
    def selchunk(e, cw):
        e.tensor_scalar(xa[:, 0:cw], xa[:, 0:cw], fo[:, 0:1], None, ALU.mult)
        e.tensor_scalar(xb[:, 0:cw], xb[:, 0:cw], fe[:, 0:1], None, ALU.mult)
        return e.tensor_add(xa[:, 0:cw], xa[:, 0:cw], xb[:, 0:cw])
    if True:
        if True:
            def ld0(e, shx=shx):
                return [e.dma_start(out=xa[:, 0:512], in_=shx[0:128, 0:512]),
                        e.dma_start(out=xb[:, 0:512], in_=shx[128:256, 0:512])]
            em.step("sync", ld0, dma=True, n_dma=2)
            em.step("vector", lambda e: selchunk(e, 512))
            em.step("sync", lambda e: e.dma_start(
                out=io["pay2_peer"][0:128, 0:512], in_=xa[:, 0:512]), dma=True)
            def ld1(e, shx=shx):
                return [e.dma_start(out=xa[:, 0:24], in_=shx[0:128, 512:536]),
                        e.dma_start(out=xb[:, 0:24], in_=shx[128:256, 512:536])]
            em.step("sync", ld1, dma=True, n_dma=2)
            em.step("vector", lambda e: selchunk(e, 24))
            em.step("sync", lambda e: e.dma_start(
                out=io["pay2_peer"][0:128, 512:520], in_=xa[:, 0:8]), dma=True)
            em.step("vector", lambda e: e.tensor_copy(h16[:, :], xa[:, 8:24]))
            em.step("sync", lambda e: e.dma_start(
                out=io["halo_raw"], in_=h16[:, :]), dma=True)


def build_fused(nc, A):
    em = Emit(nc)
    with em.ctx:
        em.init_chain()
        io1 = {"x_fm": A["x0_fm"], "identf": A["identf"], "xn_bf": A["xn0_bf"],
               "masks": A["masks0"], "scores": A["scores0"], "wg_redT": A["wg_redT"],
               "wg": A["wg"], "xh_fm": A["x0h_fm"], "xnh_bf": A["xnh_bf"],
               "identb": A["identb"], "capoff": A["capoff"],
               "xn_tm": A["xn_tm"], "idxf": A["idxf"]}
        with em.arena():
            emit_s1(em, io1, tag_l1=False)
        em.new_chain()
        io2 = {"xn_bf": A["xn0_bf"], "xnh_bf": A["xnh_bf"], "W_in": A["W_in"],
               "convw": A["convw"], "convb": A["convb"], "dtbias": A["dtbias"],
               "alog": A["alog"], "identb": A["identb"], "identf": A["identf"],
               "zs": A["zs"], "xfm": A["xfm"], "xt_tm": A["xt_tm"], "baseT": A["baseT"],
               "C_fm": A["C_fm"], "dAcs": A["dAcs"], "dAcs_tm": A["dAcs_tm"],
               "alast": A["alast"], "states": A["states"], "S_out": A["S_out"]}
        with em.arena():
            emit_s2(em, io2)
        em.new_chain()
        xio1 = {"xpar": A["xpar"], "shflags": A["shflags"],
                "S_out": A["S_out"], "S_in": A["S_in"], "sh_sout": A["sh_sout"],
                "fodd": A["fodd"], "feven": A["feven"]}
        emit_exchange(em, xio1, 1, part="send")
        em.new_chain()
        io5 = {"xn_bf": A["xn0_bf"], "masks": A["masks0"], "scores": A["scores0"],
               "w1": A["l0w1"], "w2": A["l0w2"], "b1c": A["b1c"], "b2cc": A["b2cc"], "b2": A["b2"],
               "xn_tm": A["xn_tm"], "idxf": A["idxf"], "iota512": A["iota512"],
               "gout_tm": A["gout"], "identb": A["identb"], "identf": A["identf"],
               "g": A["g0"]}
        with em.arena():
            emit_moe(em, io5, HID=4096, has_bias=True, post="l0", HSPLIT=4, mode="top1")
        em.new_chain()
        emit_exchange(em, xio1, 1, part="recv")
        em.new_chain()
        io4 = {"xt_tm": A["xt_tm"], "baseT": A["baseT"], "C_fm": A["C_fm"],
               "dAcs": A["dAcs"], "dAcs_tm": A["dAcs_tm"], "alast": A["alast"],
               "states": A["states"], "S_in": A["S_in"], "zs": A["zs"], "xfm": A["xfm"],
               "W_out": A["W_out"], "normw": A["normw"], "d_bc": A["d_bc"],
               "trilT": A["trilT"], "s0": A["s0"], "singate": A["singate"]}
        with em.arena():
            emit_s4(em, io4)
        em.new_chain()
        io6 = {"x_fm": A["x0_fm"], "identf": A["identf"], "s_bf": A["s0"],
               "g_bf": A["g0"], "x1_fm": A["x1_fm"], "gateT": A["gateT"],
               "xn_bf": A["xn1_bf"], "masks": A["masks1"], "scores": A["scores1"],
               "identb": A["identb"], "capoff": A["capoff"],
               "xn_tm": A["xn_tm"], "idxf": A["idxf"]}
        with em.arena():
            emit_s1(em, io6, tag_l1=True)
        em.new_chain()
        io7 = {"xn_bf": A["xn1_bf"], "WqT": A["WqT"], "WkT": A["WkT"], "WvT": A["WvT"],
               "cwc": A["cwc"], "cbc": A["cbc"], "identb": A["identb"],
               "q_fm": A["q_fm"], "kv_part": A["pay2"][:, 0:512],
               "kst_part": A["pay2"][:, 512:520], "local": A["local"]}
        with em.arena():
            emit_s7(em, io7)
        em.new_chain()
        xio2 = {"xpar": A["xpar"], "shflags": A["shflags"],
                "xn1_bf": A["xn1_bf"], "pay2": A["pay2"],
                "pay2_peer": A["pay2_peer"], "sh_x2": A["sh_x2"],
                "halo_raw": A["halo_raw"],
                "fodd": A["fodd"], "feven": A["feven"]}
        emit_exchange(em, xio2, 2, part="send")
        em.new_chain()
        io10 = {"xn_bf": A["xn1_bf"], "masks": A["masks1"], "scores": A["scores1"],
                "w1": A["l1w1"], "w2": A["l1w2"], "x1_fm": A["x1_fm"], "s1": A["s1"],
                "gamma": A["gamma"], "beta": A["beta"], "x2_fm": A["x2_fm"],
                "xn_tm": A["xn_tm"], "idxf": A["idxf"], "iota512": A["iota512"],
                "gout_tm": A["gout"], "identb": A["identb"], "identf": A["identf"]}
        io9 = {"q_fm": A["q_fm"], "kv_mine": A["pay2"][:, 0:512],
               "kv_peer": A["pay2_peer"][:, 0:512],
               "kst_mine": A["pay2"][:, 512:520],
               "kst_peer": A["pay2_peer"][:, 512:520], "local": A["local"],
               "halo_raw": A["halo_raw"], "fodd": A["fodd"], "feven": A["feven"],
               "cwc": A["cwc"], "WpT": A["WpT"], "bpc": A["bpc"], "s1": A["s1"]}
        with em.arena():
            emit_moe(em, io10, HID=2048, has_bias=False, post="l1", HSPLIT=2, part="experts", mode="top1")
            em.new_chain()
            emit_exchange(em, xio2, 2, part="recv")
            em.new_chain()
            emit_s9(em, io9)
            em.new_chain()
            emit_moe(em, io10, HID=2048, has_bias=False, post="l1", HSPLIT=2, part="post", mode="top1")
        em.finale()
        with nc.Block() as block:
            em.emit(block)
    return nc


def _standalone(emitfn):
    def build(nc, io, **kw):
        em = Emit(nc)
        with em.ctx:
            em.init_chain()
            with em.arena():
                emitfn(em, io, **kw)
            em.finale()
            with nc.Block() as block:
                em.emit(block)
        return nc
    return build


build_s1 = _standalone(emit_s1)
build_s2 = _standalone(emit_s2)
build_s4 = _standalone(emit_s4)
build_moe = _standalone(emit_moe)
build_s7 = _standalone(emit_s7)
build_s9 = _standalone(emit_s9)


def _cols(v, nb):
    return np.ascontiguousarray(np.asarray(v, np.float32).reshape(nb, 128).T)


def _fm(x):
    return np.ascontiguousarray(np.asarray(x, np.float32).T)


def run_stage(build_fn, hbm_specs, per_core_inputs, out_names, n_cores=8):
    nc = bass.Bass("TRN2", target_bir_lowering=False)
    io = {}
    for (name, shape, dt, kind) in hbm_specs:
        if kind == "Internal":
            io[name] = nc.dram_tensor(name, list(shape), dt).ap()
        else:
            io[name] = nc.dram_tensor(name, list(shape), dt, kind=kind).ap()
    build_fn(nc, io)
    res = run_bass_kernel_spmd(nc, per_core_inputs, core_ids=list(range(n_cores)))
    return [{k: r[k] for k in out_names} for r in res.results]


IDENT_F32 = np.eye(128, dtype=np.float32)
IDENT_BF16 = np.eye(128).astype(ml_dtypes.bfloat16)


def stage1_run(x0, tag_l1=False, gate_w=None, wg_red=None, wg=None, halo=None):
    specs = [
        ("x_fm", [D, T], F32, "ExternalInput"),
        ("identf", [128, 128], F32, "ExternalInput"),
        ("xn_bf", [8, 128, T], BF16, "ExternalOutput"),
        ("masks", [8, T], F32, "ExternalOutput"),
        ("scores", [T // 128, 128], F32, "ExternalOutput"),
    ]
    if tag_l1:
        specs.append(("gateT", [D, 8], F32, "ExternalInput"))
    else:
        specs.append(("wg_redT", [D, 16], F32, "ExternalInput"))
        specs.append(("wg", [8, 16], F32, "ExternalInput"))
    if halo is not None:
        specs.append(("xh_fm", [D, 3], F32, "ExternalInput"))
        specs.append(("xnh_bf", [8, 128, 3], BF16, "ExternalOutput"))
    specs += [("identb", [128, 128], BF16, "ExternalInput"),
              ("capoff", [8, 1], F32, "ExternalInput"),
              ("xn_tm", [T, 1024], BF16, "ExternalOutput"),
              ("idxf", [1, T], F32, "ExternalOutput")]

    ins = []
    capoff = (np.arange(8, dtype=np.float32) * CAP - 1.0)[:, None]
    for c in range(8):
        d = {"x_fm": x0[c], "identf": IDENT_F32, "identb": IDENT_BF16, "capoff": capoff}
        if tag_l1:
            d["gateT"] = np.ascontiguousarray(gate_w.T)
        else:
            d["wg_redT"] = np.ascontiguousarray(wg_red.T)
            d["wg"] = np.ascontiguousarray(wg)
        if halo is not None:
            d["xh_fm"] = halo[c]
        ins.append(d)
    outs = ["xn_bf", "masks", "scores", "xn_tm", "idxf"] + (["xnh_bf"] if halo is not None else [])
    return run_stage(lambda nc, io: build_s1(nc, io, tag_l1=tag_l1), specs, ins, outs)




def stage4_run(s2_outs, s_in, d):
    specs = [
        ("xt_tm", [NCH, 256, DI], BF16, "ExternalInput"),
        ("baseT", [NCH, 256, 256], BF16, "ExternalInput"),
        ("C_fm", [128, T], BF16, "ExternalInput"),
        ("dAcs", [32, T], F32, "ExternalInput"),
        ("dAcs_tm", [NCH, 256, 32], F32, "ExternalInput"),
        ("alast", [32, NCH], F32, "ExternalInput"),
        ("states", [NCH, 128, DI], F32, "ExternalInput"),
        ("S_in", [128, DI], F32, "ExternalInput"),
        ("zs", [16, 128, T], BF16, "ExternalInput"),
        ("xfm", [16, 128, T], BF16, "ExternalInput"),
        ("W_out", [DI, D], F32, "ExternalInput"),
        ("normw", [128, 16], F32, "ExternalInput"),
        ("d_bc", [128, 32], F32, "ExternalInput"),
        ("trilT", [2, 128, 256], F32, "ExternalInput"),
        ("s0", [8, 128, T], BF16, "ExternalOutput"),
    ]
    tril = np.zeros((2, 128, 256), np.float32)
    for jb in range(2):
        for jj in range(128):
            tril[jb, jj, :jj + 128 * jb] = -1e30
    ins = []
    for c in range(8):
        o = s2_outs[c]
        ins.append({
            "xt_tm": o["xt_tm"], "baseT": o["baseT"], "C_fm": o["C_fm"],
            "dAcs": o["dAcs"], "dAcs_tm": o["dAcs_tm"], "alast": o["alast"],
            "states": o["states"], "S_in": s_in[c], "zs": o["zs"], "xfm": o["xfm"],
            "W_out": np.asarray(d["L0_attn_W_out"], np.float32),
            "normw": _cols(d["L0_attn_norm_w"], 16),
            "d_bc": np.ascontiguousarray(np.broadcast_to(
                np.asarray(d["L0_attn_D"], np.float32)[None, :], (128, 32))),
            "trilT": tril,
        })
    return run_stage(build_s4, specs, ins, ["s0"])


def stage2_run(xn_bf, xnh_bf, d):
    specs = [
        ("xn_bf", [8, 128, T], BF16, "ExternalInput"),
        ("xnh_bf", [8, 128, 3], BF16, "ExternalInput"),
        ("W_in", [D, DIP], F32, "ExternalInput"),
        ("convw", [128, 18 * 4], F32, "ExternalInput"),
        ("convb", [128, 18], F32, "ExternalInput"),
        ("dtbias", [32, 1], F32, "ExternalInput"),
        ("alog", [32, 1], F32, "ExternalInput"),
        ("identb", [128, 128], BF16, "ExternalInput"),
        ("identf", [128, 128], F32, "ExternalInput"),
        ("zs", [16, 128, T], BF16, "ExternalOutput"),
        ("xfm", [16, 128, T], BF16, "ExternalOutput"),
        ("xt_tm", [NCH, 256, DI], BF16, "ExternalOutput"),
        ("baseT", [NCH, 256, 256], BF16, "ExternalOutput"),
        ("C_fm", [128, T], BF16, "ExternalOutput"),
        ("dAcs", [32, T], F32, "ExternalOutput"),
        ("dAcs_tm", [NCH, 256, 32], F32, "ExternalOutput"),
        ("alast", [32, NCH], F32, "ExternalOutput"),
        ("states", [NCH, 128, DI], F32, "ExternalOutput"),
        ("S_out", [128, DI], F32, "ExternalOutput"),
    ]
    cw = np.ascontiguousarray(np.asarray(d["L0_attn_conv_w"], np.float32)
                              .reshape(18, 128, 4).transpose(1, 0, 2).reshape(128, 72))
    cb = _cols(d["L0_attn_conv_b"], 18)
    ins = []
    for c in range(8):
        ins.append({
            "xn_bf": xn_bf[c], "xnh_bf": xnh_bf[c],
            "W_in": np.asarray(d["L0_attn_W_in"], np.float32),
            "convw": cw, "convb": cb,
            "dtbias": np.asarray(d["L0_attn_dt_bias"], np.float32)[:, None],
            "alog": np.asarray(d["L0_attn_A_log"], np.float32)[:, None],
            "identb": IDENT_BF16, "identf": IDENT_F32,
        })
    outs = ["zs", "xfm", "xt_tm", "baseT", "C_fm", "dAcs", "dAcs_tm", "alast", "states", "S_out"]
    return run_stage(build_s2, specs, ins, outs)




def stage6_run(x0_fm, s0, g0, d):
    specs = [
        ("x_fm", [D, T], F32, "ExternalInput"),
        ("identf", [128, 128], F32, "ExternalInput"),
        ("s_bf", [8, 128, T], BF16, "ExternalInput"),
        ("g_bf", [8, 128, T], BF16, "ExternalInput"),
        ("gateT", [D, 8], F32, "ExternalInput"),
        ("xn_bf", [8, 128, T], BF16, "ExternalOutput"),
        ("masks", [8, T], F32, "ExternalOutput"),
        ("scores", [T // 128, 128], F32, "ExternalOutput"),
        ("x1_fm", [D, T], F32, "ExternalOutput"),
    ]
    ins = []
    for c in range(8):
        ins.append({"x_fm": x0_fm[c], "identf": IDENT_F32, "s_bf": s0[c], "g_bf": g0[c],
                    "gateT": np.ascontiguousarray(np.asarray(d["L1_ffn_gate_w"], np.float32).T)})
    return run_stage(lambda nc, io: build_s1(nc, io, tag_l1=True), specs, ins,
                     ["xn_bf", "masks", "scores", "x1_fm"])


def stage7_run(xn_bf, d):
    specs = [
        ("xn_bf", [8, 128, T], BF16, "ExternalInput"),
        ("WqT", [D, D], F32, "ExternalInput"),
        ("WkT", [D, D], F32, "ExternalInput"),
        ("WvT", [D, D], F32, "ExternalInput"),
        ("cwc", [128, 24], F32, "ExternalInput"),
        ("cbc", [128, 8], F32, "ExternalInput"),
        ("identb", [128, 128], BF16, "ExternalInput"),
        ("q_fm", [8, 128, T], BF16, "ExternalOutput"),
        ("kv_part", [128, 512], F32, "ExternalOutput"),
        ("kst_part", [128, 8], F32, "ExternalOutput"),
        ("local", [8, 128, T], BF16, "ExternalOutput"),
    ]
    cw = np.ascontiguousarray(np.asarray(d["L1_attn_conv_w"], np.float32)
                              .reshape(8, 128, 3).transpose(1, 0, 2).reshape(128, 24))
    ins = []
    for c in range(8):
        ins.append({"xn_bf": xn_bf[c],
                    "WqT": np.ascontiguousarray(np.asarray(d["L1_attn_Wq"], np.float32).T),
                    "WkT": np.ascontiguousarray(np.asarray(d["L1_attn_Wk"], np.float32).T),
                    "WvT": np.ascontiguousarray(np.asarray(d["L1_attn_Wv"], np.float32).T),
                    "cwc": cw, "cbc": _cols(d["L1_attn_conv_b"], 8),
                    "identb": IDENT_BF16})
    return run_stage(build_s7, specs, ins, ["q_fm", "kv_part", "kst_part", "local"])


def stage9_run(q_fm, kv, kst, local, halo, d):
    specs = [
        ("q_fm", [8, 128, T], BF16, "ExternalInput"),
        ("kv_mine", [128, 512], F32, "ExternalInput"),
        ("kv_peer", [128, 512], F32, "ExternalInput"),
        ("kst_mine", [128, 8], F32, "ExternalInput"),
        ("kst_peer", [128, 8], F32, "ExternalInput"),
        ("local", [8, 128, T], BF16, "ExternalInput"),
        ("halo_cols", [8, 128, 2], BF16, "ExternalInput"),
        ("cwc", [128, 24], F32, "ExternalInput"),
        ("WpT", [D, D], F32, "ExternalInput"),
        ("bpc", [128, 8], F32, "ExternalInput"),
        ("s1", [8, 128, T], BF16, "ExternalOutput"),
    ]
    cw = np.ascontiguousarray(np.asarray(d["L1_attn_conv_w"], np.float32)
                              .reshape(8, 128, 3).transpose(1, 0, 2).reshape(128, 24))
    ins = []
    for c in range(8):
        p = c ^ 1
        ins.append({"q_fm": q_fm[c], "kv_mine": kv[c], "kv_peer": kv[p],
                    "kst_mine": kst[c], "kst_peer": kst[p],
                    "local": local[c], "halo_cols": halo[c], "cwc": cw,
                    "WpT": np.ascontiguousarray(np.asarray(d["L1_attn_Wp"], np.float32).T),
                    "bpc": _cols(d["L1_attn_bp"], 8)})
    return run_stage(build_s9, specs, ins, ["s1"])


def stage10_run(xn_bf, masks, scores, x1_fm, s1, d):
    specs = [
        ("xn_bf", [8, 128, T], BF16, "ExternalInput"),
        ("masks", [8, T], F32, "ExternalInput"),
        ("scores", [T // 128, 128], F32, "ExternalInput"),
        ("w1", [8, D, 2048], F32, "ExternalInput"),
        ("w2", [8, 2048, D], F32, "ExternalInput"),
        ("x1_fm", [D, T], F32, "ExternalInput"),
        ("s1", [8, 128, T], BF16, "ExternalInput"),
        ("gamma", [128, 8], F32, "ExternalInput"),
        ("beta", [128, 8], F32, "ExternalInput"),
        ("x2_fm", [D, T], F32, "ExternalOutput"),
    ]
    ins = []
    for c in range(8):
        ins.append({"xn_bf": xn_bf[c], "masks": masks[c], "scores": scores[c],
                    "w1": np.asarray(d["L1_ffn_expert_w1"], np.float32),
                    "w2": np.asarray(d["L1_ffn_expert_w2"], np.float32),
                    "x1_fm": x1_fm[c], "s1": s1[c],
                    "gamma": _cols(d["L1_ffn_ln_gamma"], 8),
                    "beta": _cols(d["L1_ffn_ln_beta"], 8)})
    return run_stage(lambda nc, io: build_moe(nc, io, HID=2048, has_bias=False, post="l1", HSPLIT=1),
                     specs, ins, ["x2_fm"])


FUSED = True


def kernel(**inputs):
    if FUSED:
        return kernel_fused(**inputs)
    return kernel_unfused(**inputs)


def kernel_fused(**inputs):
    d = inputs
    x0 = np.asarray(d["L0_attn_hidden_states"], np.float32)

    nc = bass.Bass("TRN2", target_bir_lowering=False)
    A = {}

    def inp(name, shape, dt=F32):
        A[name] = nc.dram_tensor(name, list(shape), dt, kind="ExternalInput").ap()

    def internal(name, shape, dt=F32):
        A[name] = nc.dram_tensor(name, list(shape), dt).ap()

    def shared(name, shape, dt=F32):
        A[name] = nc.dram_tensor(name, list(shape), dt, addr_space="Shared").ap()

    inp("x0_fm", [D, T]); inp("x0h_fm", [D, 3])
    inp("identf", [128, 128]); inp("identb", [128, 128], BF16)
    inp("wg_redT", [D, 16]); inp("wg", [8, 16])
    inp("W_in", [D, DIP]); inp("convw", [128, 72]); inp("convb", [128, 18])
    inp("dtbias", [32, 1]); inp("alog", [32, 1])
    inp("W_out", [DI, D]); inp("normw", [128, 16]); inp("d_bc", [128, 32])
    inp("trilT", [2, 128, 256])
    inp("l0w1", [8, D, 4096]); inp("l0w2", [8, 4096, D])
    inp("b1c", [8, 128, 32]); inp("b2", [8, D])
    inp("gateT", [D, 8])
    inp("WqT", [D, D]); inp("WkT", [D, D]); inp("WvT", [D, D]); inp("WpT", [D, D])
    inp("cwc", [128, 24]); inp("cbc", [128, 8]); inp("bpc", [128, 8])
    inp("l1w1", [8, D, 2048]); inp("l1w2", [8, 2048, D])
    inp("gamma", [128, 8]); inp("beta", [128, 8])
    inp("xpar", [1, 8], U32); inp("singate", [128, 1])
    inp("fodd", [128, 1]); inp("feven", [128, 1])
    inp("capoff", [8, 1]); inp("b2cc", [8, 128, 8]); inp("iota512", [1, 512])

    internal("xn0_bf", [8, 128, T], BF16); internal("xnh_bf", [8, 128, 3], BF16)
    internal("masks0", [8, T]); internal("scores0", [T // 128, 128])
    internal("zs", [16, 128, T], BF16); internal("xfm", [16, 128, T], BF16)
    internal("xt_tm", [NCH, 256, DI], BF16); internal("baseT", [NCH, 256, 256], BF16)
    internal("C_fm", [128, T], BF16); internal("dAcs", [32, T])
    internal("dAcs_tm", [NCH, 256, 32]); internal("alast", [32, NCH])
    internal("states", [NCH, 128, DI]); internal("S_out", [128, DI]); internal("S_in", [128, DI])
    internal("s0", [8, 128, T], BF16); internal("g0", [8, 128, T], BF16)
    internal("x1_fm", [D, T]); internal("xn1_bf", [8, 128, T], BF16)
    internal("masks1", [8, T]); internal("scores1", [T // 128, 128])
    internal("q_fm", [8, 128, T], BF16); internal("local", [8, 128, T], BF16)
    internal("pay2", [128, 536]); internal("pay2_peer", [128, 536])
    internal("halo_raw", [128, 16], BF16); internal("s1", [8, 128, T], BF16)
    internal("xn_tm", [T, D], BF16); internal("idxf", [1, T])
    internal("gout", [8 * CAP, D], BF16)

    shared("sh_sout", [256, DI]); shared("sh_x2", [256, 536])
    shared("shflags", [1, 4], U32)

    A["x2_fm"] = nc.dram_tensor("x2_fm", [D, T], F32, kind="ExternalOutput").ap()

    build_fused(nc, A)

    tril = np.zeros((2, 128, 256), np.float32)
    for jb in range(2):
        for jj in range(128):
            tril[jb, jj, :jj + 128 * jb] = -1e30
    cw0 = np.ascontiguousarray(np.asarray(d["L0_attn_conv_w"], np.float32)
                               .reshape(18, 128, 4).transpose(1, 0, 2).reshape(128, 72))
    cw1 = np.ascontiguousarray(np.asarray(d["L1_attn_conv_w"], np.float32)
                               .reshape(8, 128, 3).transpose(1, 0, 2).reshape(128, 24))
    common = {
        "identf": IDENT_F32, "identb": IDENT_BF16,
        "wg_redT": np.ascontiguousarray(np.asarray(d["L0_ffn_wg_red"], np.float32).T),
        "wg": np.ascontiguousarray(np.asarray(d["L0_ffn_wg"], np.float32)),
        "W_in": np.asarray(d["L0_attn_W_in"], np.float32),
        "convw": cw0, "convb": _cols(d["L0_attn_conv_b"], 18),
        "dtbias": np.asarray(d["L0_attn_dt_bias"], np.float32)[:, None],
        "alog": np.asarray(d["L0_attn_A_log"], np.float32)[:, None],
        "W_out": np.asarray(d["L0_attn_W_out"], np.float32),
        "normw": _cols(d["L0_attn_norm_w"], 16),
        "d_bc": np.ascontiguousarray(np.broadcast_to(
            np.asarray(d["L0_attn_D"], np.float32)[None, :], (128, 32))),
        "trilT": tril,
        "l0w1": np.asarray(d["L0_ffn_w1"], np.float32),
        "l0w2": np.asarray(d["L0_ffn_w2"], np.float32),
        "b1c": np.ascontiguousarray(np.asarray(d["L0_ffn_b1"], np.float32)
                                    .reshape(8, 32, 128).transpose(0, 2, 1)),
        "b2": np.asarray(d["L0_ffn_b2"], np.float32),
        "gateT": np.ascontiguousarray(np.asarray(d["L1_ffn_gate_w"], np.float32).T),
        "WqT": np.ascontiguousarray(np.asarray(d["L1_attn_Wq"], np.float32).T),
        "WkT": np.ascontiguousarray(np.asarray(d["L1_attn_Wk"], np.float32).T),
        "WvT": np.ascontiguousarray(np.asarray(d["L1_attn_Wv"], np.float32).T),
        "WpT": np.ascontiguousarray(np.asarray(d["L1_attn_Wp"], np.float32).T),
        "cwc": cw1, "cbc": _cols(d["L1_attn_conv_b"], 8),
        "bpc": _cols(d["L1_attn_bp"], 8),
        "l1w1": np.asarray(d["L1_ffn_expert_w1"], np.float32),
        "l1w2": np.asarray(d["L1_ffn_expert_w2"], np.float32),
        "gamma": _cols(d["L1_ffn_ln_gamma"], 8),
        "beta": _cols(d["L1_ffn_ln_beta"], 8),
        "capoff": (np.arange(8, dtype=np.float32) * CAP - 1.0)[:, None],
        "iota512": np.arange(512, dtype=np.float32)[None, :],
        "b2cc": np.ascontiguousarray(np.asarray(d["L0_ffn_b2"], np.float32)
                                     .reshape(8, 8, 128).transpose(0, 2, 1)),
    }
    in_maps = []
    ones_col = np.ones((128, 1), np.float32)
    zeros_col = np.zeros((128, 1), np.float32)
    for c in range(8):
        b, h = c // 2, c % 2
        m = dict(common)
        m["x0_fm"] = np.ascontiguousarray(x0[b, h * 2048:(h + 1) * 2048].T)
        m["x0h_fm"] = (np.ascontiguousarray(x0[b, 2045:2048].T) if h == 1
                       else np.zeros((D, 3), np.float32))
        s = h
        m["xpar"] = np.array([[s * 128, (1 - s) * 128, s, 1 - s, 2 + s, 3 - s,
                               s * 8, (1 - s) * 8]], dtype=np.uint32)
        m["singate"] = ones_col if h == 1 else zeros_col
        m["fodd"] = ones_col if h == 1 else zeros_col
        m["feven"] = ones_col if h == 0 else zeros_col
        in_maps.append(m)

    res = run_bass_kernel_spmd(nc, in_maps, core_ids=list(range(8)))
    out = np.empty((4, 4096, D), np.float32)
    for c in range(8):
        b, h = c // 2, c % 2
        out[b, h * 2048:(h + 1) * 2048, :] = np.asarray(res.results[c]["x2_fm"], np.float32).T
    return out


def kernel_unfused(**inputs):
    d = inputs
    x0 = np.asarray(d["L0_attn_hidden_states"], np.float32)

    x0_fm, halo = [], []
    for c in range(8):
        b, h = c // 2, c % 2
        x0_fm.append(np.ascontiguousarray(x0[b, h * 2048:(h + 1) * 2048].T))
        if h == 1:
            halo.append(np.ascontiguousarray(x0[b, 2045:2048].T))
        else:
            halo.append(np.zeros((D, 3), np.float32))

    s1o = stage1_run(x0_fm, tag_l1=False, wg_red=d["L0_ffn_wg_red"], wg=d["L0_ffn_wg"], halo=halo)
    xn0_bf = [o["xn_bf"] for o in s1o]
    masks0 = [o["masks"] for o in s1o]
    scores0 = [o["scores"] for o in s1o]
    xnh_bf = [o["xnh_bf"] for o in s1o]

    s2o = stage2_run(xn0_bf, xnh_bf, d)

    s_in = []
    for c in range(8):
        if c % 2 == 1:
            s_in.append(np.ascontiguousarray(np.asarray(s2o[c - 1]["S_out"], np.float32)))
        else:
            s_in.append(np.zeros((128, DI), np.float32))

    s4o = stage4_run(s2o, s_in, d)
    s0 = [o["s0"] for o in s4o]

    s5o = stage5_run(xn0_bf, masks0, scores0, d)
    g0 = [o["g"] for o in s5o]

    s6o = stage6_run(x0_fm, s0, g0, d)
    xn1_bf = [o["xn_bf"] for o in s6o]
    masks1 = [o["masks"] for o in s6o]
    scores1 = [o["scores"] for o in s6o]
    x1_fm = [o["x1_fm"] for o in s6o]

    s7o = stage7_run(xn1_bf, d)
    q_fm = [o["q_fm"] for o in s7o]
    kv = [o["kv_part"] for o in s7o]
    kst = [o["kst_part"] for o in s7o]
    local = [o["local"] for o in s7o]

    halo_cols = []
    for c in range(8):
        p = c ^ 1
        hc = np.zeros((8, 128, 2), dtype=xn1_bf[0].dtype)
        if c % 2 == 1:
            hc[:, :, 0] = xn1_bf[p][:, :, 2047]
        else:
            hc[:, :, 1] = xn1_bf[p][:, :, 0]
        halo_cols.append(np.ascontiguousarray(hc))

    s9o = stage9_run(q_fm, kv, kst, local, halo_cols, d)
    s1 = [o["s1"] for o in s9o]

    s10o = stage10_run(xn1_bf, masks1, scores1, x1_fm, s1, d)

    out = np.empty((4, 4096, D), np.float32)
    for c in range(8):
        b, h = c // 2, c % 2
        out[b, h * 2048:(h + 1) * 2048, :] = np.asarray(s10o[c]["x2_fm"], np.float32).T
    return out
